# version 18
# baseline (speedup 1.0000x reference)
import math
import numpy as np
import concourse.bass as bass
import concourse.mybir as mybir
from concourse.bass_utils import run_bass_kernel_spmd

F32 = mybir.dt.float32
BF16 = mybir.dt.bfloat16
I32 = mybir.dt.int32
AF = mybir.ActivationFunctionType
ALU = mybir.AluOpType
AX = mybir.AxisListType

NC = 8
D = 1024
HEADS = 16
HD = 64
KT = D // 128
SCALE = float(HD ** (-0.25))
PI = float(np.float32(math.pi))
EPS = 1e-5


def _trunc12(x):
    import struct
    b = struct.unpack('>I', struct.pack('>f', np.float32(x)))[0]
    b &= ~((1 << 11) - 1)
    return float(struct.unpack('>f', struct.pack('>I', b))[0])


TWO_PI = 2.0 * math.pi
CW1 = _trunc12(TWO_PI)
CW2 = _trunc12(TWO_PI - CW1)
CW3 = float(np.float32(TWO_PI - CW1 - CW2))
INV_2PI = float(np.float32(1.0 / TWO_PI))
S_HI = float(np.float32(-math.log(10000.0) / 32.0))
S_LO = float(np.float32(-math.log(10000.0) / 32.0 - S_HI))

NLANES = 4
DMA_LANES = tuple(f"Y{i}" for i in range(NLANES)) + tuple(f"G{i}" for i in range(NLANES))
ENGS = ("T", "V", "S", "G", "Y")
SEM_NAMES = ENGS + DMA_LANES
_AUTO_EDGE = ("V", "S", "G")


class Prog:

    def __init__(self, counts):
        self.q = {e: [] for e in ENGS}
        self.count = counts
        self.entry_waits = [(l, counts[l]) for l in DMA_LANES if counts[l] > 0]

    def op(self, eng, fn, after=(), dma=False):
        waits = {}
        if eng in _AUTO_EDGE and not dma and self.count[eng] > 3:
            waits[eng] = self.count[eng] - 3
        for t in after:
            if t is None:
                continue
            e2, v = t
            if e2 == eng == "T":
                continue
            if v > 0:
                waits[e2] = max(waits.get(e2, 0), v)
        if dma:
            lane_i = self.count[eng + "_rr"] if (eng + "_rr") in self.count else 0
            self.count[eng + "_rr"] = (lane_i + 1) % NLANES
            lane = f"{eng}{lane_i}"
            if self.count[lane] > 0:
                waits[lane] = max(waits.get(lane, 0), self.count[lane])
            self.count[lane] += 16
            tok = (lane, self.count[lane])
            self.q[eng].append((fn, sorted(waits.items()), 16, lane))
            return tok
        self.count[eng] += 1
        tok = (eng, self.count[eng])
        self.q[eng].append((fn, sorted(waits.items()), 1, eng))
        return tok

    def replay(self, eng_name, eng_obj, sems):
        if self.q[eng_name]:
            for e2, v in self.entry_waits:
                eng_obj.wait_ge(sems[e2], v)
        for fn, waits, inc, sem in self.q[eng_name]:
            for e2, v in waits:
                eng_obj.wait_ge(sems[e2], v)
            inst = fn(eng_obj)
            inst.then_inc(sems[sem], inc)


def r3d(ap, p=128):
    return ap.rearrange("(a p) b -> p a b", p=p)


def r3s(ap, a):
    return ap.rearrange("p (a b) -> p a b", a=a)


def build(n_own, n_pre, f_ffn, dbg=False):
    NT = n_own // 128
    NTB = n_own // 512
    NCH = n_own // 128
    NPTB = n_pre // 512
    NPT = n_pre // 128
    NFB = f_ffn // 512
    E = 8

    def stpos(h):
        return 64 * (h % 2), (h // 8) * 260 + ((h % 8) // 2) * 65

    nc = bass.Bass("TRN2", target_bir_lowering=False)

    x_d = nc.dram_tensor("x", [n_own, D], F32, kind="ExternalInput")
    xpre_d = nc.dram_tensor("xpre", [n_pre, D], F32, kind="ExternalInput")
    wqkt_d = nc.dram_tensor("wqkt", [4 * KT, 128, KT * 128], F32, kind="ExternalInput")
    wkvp_d = nc.dram_tensor("wkvp", [4, 128, KT * 512], F32, kind="ExternalInput")
    wout_d = nc.dram_tensor("wout", [128, KT * D], F32, kind="ExternalInput")
    rwt_d = nc.dram_tensor("rwt", [128, KT * 8], F32, kind="ExternalInput")
    bbc_d = nc.dram_tensor("bbc", [128, 8], F32, kind="ExternalInput")
    gw_d = nc.dram_tensor("gw", [E * NFB, 128, KT * 512], F32, kind="ExternalInput")
    uw_d = nc.dram_tensor("uw", [E * NFB, 128, KT * 512], F32, kind="ExternalInput")
    dw_d = nc.dram_tensor("dw", [E * NFB, 128, 4 * D], F32, kind="ExternalInput")
    posb_d = nc.dram_tensor("posb", [128, n_own], F32, kind="ExternalInput")
    posp_d = nc.dram_tensor("posp", [128, NPT], F32, kind="ExternalInput")
    jcol_d = nc.dram_tensor("jcol", [128, 1], F32, kind="ExternalInput")
    j64_d = nc.dram_tensor("j64", [128, 32], F32, kind="ExternalInput")
    sgn_d = nc.dram_tensor("sgn", [128, 1], F32, kind="ExternalInput")
    prem_d = nc.dram_tensor("prem", [128, NPT], F32, kind="ExternalInput")
    ident_d = nc.dram_tensor("ident", [128, 128], F32, kind="ExternalInput")
    cmask_d = nc.dram_tensor("cmask", [128, 128], F32, kind="ExternalInput")
    ccol_d = nc.dram_tensor("ccol", [128, 2], F32, kind="ExternalInput")

    out_d = nc.dram_tensor("out", [n_own, D], F32, kind="ExternalOutput")
    dbg_d = {}
    if dbg:
        for nm, sh in [("d_xn1", [n_own, D]), ("d_qf", [128, KT * n_own]),
                       ("d_kf", [128, KT * n_own]), ("d_vt", [n_own, D]),
                       ("d_st", [128, 8 * 65]), ("d_x1", [n_own, D]),
                       ("d_xn2", [n_own, D]), ("d_w8", [128, NT * 8])]:
            dbg_d[nm] = nc.dram_tensor(nm, sh, F32, kind="ExternalOutput")

    counts = {e: 0 for e in SEM_NAMES}
    counts["Y_rr"] = 0
    counts["G_rr"] = 0
    from contextlib import ExitStack

    with ExitStack() as top:
        sems = {e: top.enter_context(nc.semaphore("sem_" + e)) for e in SEM_NAMES}
        sb = lambda n, s, d=F32: top.enter_context(nc.sbuf_tensor(n, s, d))

        x_res = sb("x_res", [128, NT * D])
        st_f = sb("st_f", [128, 8 * 65])
        st_b = sb("st_b", [128, 8 * 65], BF16)
        invf_c = sb("invf_c", [128, 1])
        invf64 = sb("invf64", [128, 32])
        jcol = sb("jcol_s", [128, 1])
        j64 = sb("j64_s", [128, 32])
        sgn = sb("sgn_s", [128, 1])
        ccol = sb("ccol_s", [128, 2])
        ident_b = sb("ident_b", [128, 128], BF16)
        cmask_b = sb("cmask_b", [128, 128], BF16)
        ones_b = sb("ones_b", [128, 1], BF16)
        posp_s = sb("posp_s", [128, NPT])
        prem_s = sb("prem_s", [128, NPT])
        bbc_s = sb("bbc_s", [128, 8])
        w8_s = sb("w8_s", [128, NT * 8])

        def run_block(p):
            with nc.Block() as blk:
                @blk.tensor
                def _(e):
                    p.replay("T", e, sems)

                @blk.vector
                def _(e):
                    p.replay("V", e, sems)

                @blk.scalar
                def _(e):
                    p.replay("S", e, sems)

                @blk.gpsimd
                def _(e):
                    p.replay("G", e, sems)

                @blk.sync
                def _(e):
                    p.replay("Y", e, sems)

        def emit_ln(p, src_ap_fn, dst_ap_fn, lnc, lcol, src_after=(), dst_after=(), prev=(None, None, None)):
            a0 = p.op("V", lambda e: e.reduce_sum(lcol[:, 0:1], src_ap_fn(), axis=AX.X),
                      after=list(src_after) + [prev[0]])
            a2 = p.op("V", lambda e: e.tensor_tensor(lnc[:, :], src_ap_fn(), src_ap_fn(), ALU.mult),
                      after=list(src_after) + [prev[1]])
            a3 = p.op("V", lambda e: e.reduce_sum(lcol[:, 2:3], lnc[:, :], axis=AX.X),
                      after=[a2, prev[2]])
            a1 = p.op("V", lambda e: e.tensor_scalar(lcol[:, 1:2], lcol[:, 0:1], 1.0 / D, None, ALU.mult), after=[a0])
            a4 = p.op("V", lambda e: e.tensor_tensor(lcol[:, 3:4], lcol[:, 1:2], lcol[:, 1:2], ALU.mult), after=[a1])
            a5 = p.op("V", lambda e: e.tensor_scalar(lcol[:, 2:3], lcol[:, 2:3], 1.0 / D, None, ALU.mult), after=[a3])
            a6 = p.op("V", lambda e: e.tensor_tensor(lcol[:, 2:3], lcol[:, 2:3], lcol[:, 3:4], ALU.subtract), after=[a5, a4])
            a7 = p.op("S", lambda e: e.activation(lcol[:, 2:3], lcol[:, 2:3], AF.Ln, bias=ccol[:, 0:1]), after=[a6])
            a8 = p.op("S", lambda e: e.activation(lcol[:, 3:4], lcol[:, 2:3], AF.Exp, scale=-0.5), after=[a7])
            a9 = p.op("V", lambda e: e.tensor_scalar(dst_ap_fn(), src_ap_fn(), lcol[:, 1:2], lcol[:, 3:4], ALU.subtract, ALU.mult),
                      after=[a8] + list(dst_after))
            return a9, (a1, a3, a7)

        def emit_reduce(p, th, thp, kfb, kib, w, tok_in):
            a = p.op("V", lambda e: e.tensor_scalar(thp[:, :w], th[:, :w], PI, None, ALU.add), after=[tok_in])
            b = p.op("V", lambda e: e.tensor_scalar(kfb[:, :w], thp[:, :w], INV_2PI, None, ALU.mult), after=[a])
            c = p.op("V", lambda e: e.tensor_copy(kib[:, :w], kfb[:, :w]), after=[b])
            d = p.op("V", lambda e: e.tensor_copy(kfb[:, :w], kib[:, :w]), after=[c])
            ng = p.op("V", lambda e: e.tensor_scalar(kfb[:, :w], kfb[:, :w], -1.0, None, ALU.mult), after=[d])
            r = ng
            for cwc in (CW1, CW2, CW3):
                r = p.op("V", lambda e, cc=cwc: e.scalar_tensor_tensor(thp[:, :w], kfb[:, :w], cc, thp[:, :w], ALU.mult, ALU.add), after=[r])
            r = p.op("V", lambda e: e.tensor_scalar(thp[:, :w], thp[:, :w], -PI, None, ALU.add), after=[r])
            g = p.op("V", lambda e: e.tensor_scalar(kfb[:, :w], thp[:, :w], PI, None, ALU.is_gt), after=[r])
            r = p.op("V", lambda e: e.scalar_tensor_tensor(thp[:, :w], kfb[:, :w], -2.0 * PI, thp[:, :w], ALU.mult, ALU.add), after=[g])
            l = p.op("V", lambda e: e.tensor_scalar(kfb[:, :w], thp[:, :w], -PI, None, ALU.is_lt), after=[r])
            r = p.op("V", lambda e: e.scalar_tensor_tensor(thp[:, :w], kfb[:, :w], 2.0 * PI, thp[:, :w], ALU.mult, ALU.add), after=[l])
            r = p.op("V", lambda e: e.tensor_scalar(thp[:, :w], thp[:, :w], PI, -PI, ALU.min, ALU.max), after=[r])
            return r

        p = Prog(counts)
        p.op("Y", lambda e: e.dma_start(
            out=r3s(x_res[:, :], NT), in_=r3d(x_d[:, :])), dma=True)
        for dst, src in [(posp_s, posp_d), (prem_s, prem_d),
                         (jcol, jcol_d), (j64, j64_d), (sgn, sgn_d),
                         (ccol, ccol_d), (bbc_s, bbc_d)]:
            p.op("Y", lambda e, d=dst, s=src: e.dma_start(out=d[:, :], in_=s[:, :]), dma=True)
        p.op("G", lambda e: e.dma_start(out=ident_b[:, :], in_=ident_d[:, :]), dma=True)
        p.op("G", lambda e: e.dma_start(out=cmask_b[:, :], in_=cmask_d[:, :]), dma=True)
        p.op("V", lambda e: e.memset(ones_b[:, :], 1.0))
        p.op("V", lambda e: e.memset(st_f[:, :], 0.0))
        run_block(p)

        p = Prog(counts)
        t0 = p.op("V", lambda e: e.tensor_scalar(invf_c[:, :], jcol[:, :], S_HI, None, ALU.mult))
        t1 = p.op("V", lambda e: e.scalar_tensor_tensor(invf_c[:, :], jcol[:, :], S_LO, invf_c[:, :], ALU.mult, ALU.add), after=[t0])
        p.op("S", lambda e: e.activation(invf_c[:, :], invf_c[:, :], AF.Exp), after=[t1])
        t0 = p.op("V", lambda e: e.tensor_scalar(invf64[:, :], j64[:, :], S_HI, None, ALU.mult))
        t1 = p.op("V", lambda e: e.scalar_tensor_tensor(invf64[:, :], j64[:, :], S_LO, invf64[:, :], ALU.mult, ALU.add), after=[t0])
        p.op("S", lambda e: e.activation(invf64[:, :], invf64[:, :], AF.Exp), after=[t1])
        run_block(p)

        cosmM = sb("cosmM", [128, n_own], BF16)
        sinmM = sb("sinmM", [128, n_own], BF16)
        tabstk = ExitStack()
        coswA = tabstk.enter_context(nc.sbuf_tensor("coswA", [128, NPT * 64], BF16))
        sinwA = tabstk.enter_context(nc.sbuf_tensor("sinwA", [128, NPT * 64], BF16))
        with ExitStack() as ph:
            psb = lambda n, s, d=F32: ph.enter_context(nc.sbuf_tensor(n, s, d))
            WT = NPT * 32 + n_own
            tha = psb("tha", [128, WT])
            thb = psb("thb", [128, WT])
            thk = psb("thk", [128, WT])
            tki = ph.enter_context(nc.sbuf_tensor("tkiT", [128, WT], I32))
            posbf = psb("posbf", [128, n_own])
            p = Prog(counts)
            pb = p.op("Y", lambda e: e.dma_start(out=posbf[:, :], in_=posb_d[:, :]), dma=True)
            ppre = bass.AP(posp_s, 0, [[NPT, 128], [1, NPT], [0, 32]])
            ifre = bass.AP(invf64, 0, [[32, 128], [0, NPT], [1, 32]])
            tA = p.op("V", lambda e: e.tensor_tensor(
                bass.AP(tha, 0, [[WT, 128], [32, NPT], [1, 32]]), ppre, ifre, ALU.mult))
            tB = p.op("V", lambda e: e.tensor_scalar(
                tha[:, NPT * 32: WT], posbf[:, :], invf_c[:, 0:1], None, ALU.mult), after=[pb])
            red = emit_reduce(p, tha, thb, thk, tki, WT, tB)
            sn1 = p.op("S", lambda e: e.activation(tha[:, :], thb[:, :], AF.Sin), after=[red])
            c0 = p.op("V", lambda e: e.tensor_scalar(thk[:, :], thb[:, :], PI / 2.0, None, ALU.add), after=[red])
            c1 = p.op("V", lambda e: e.tensor_scalar(thb[:, :], thk[:, :], PI, None, ALU.is_gt), after=[c0, sn1])
            c2 = p.op("V", lambda e: e.scalar_tensor_tensor(thk[:, :], thb[:, :], -2.0 * PI, thk[:, :], ALU.mult, ALU.add), after=[c1])
            c3 = p.op("V", lambda e: e.tensor_scalar(thk[:, :], thk[:, :], PI, -PI, ALU.min, ALU.max), after=[c2])
            sn2 = p.op("S", lambda e: e.activation(thk[:, :], thk[:, :], AF.Sin), after=[c3])
            f0t = p.op("V", lambda e: e.tensor_scalar(
                bass.AP(sinwA, 0, [[NPT * 64, 128], [64, NPT], [1, 32]]),
                bass.AP(tha, 0, [[WT, 128], [32, NPT], [1, 32]]), -SCALE, None, ALU.mult), after=[sn1])
            f1t = p.op("V", lambda e: e.tensor_scalar(
                bass.AP(sinwA, 32, [[NPT * 64, 128], [64, NPT], [1, 32]]),
                bass.AP(tha, 0, [[WT, 128], [32, NPT], [1, 32]]), SCALE, None, ALU.mult), after=[sn1])
            f2t = p.op("V", lambda e: e.tensor_scalar(
                bass.AP(coswA, 0, [[NPT * 64, 128], [64, NPT], [1, 32]]),
                bass.AP(thk, 0, [[WT, 128], [32, NPT], [1, 32]]), SCALE, None, ALU.mult), after=[sn2])
            f3t = p.op("V", lambda e: e.tensor_scalar(
                bass.AP(coswA, 32, [[NPT * 64, 128], [64, NPT], [1, 32]]),
                bass.AP(thk, 0, [[WT, 128], [32, NPT], [1, 32]]), SCALE, None, ALU.mult), after=[sn2])
            fm0 = p.op("V", lambda e: e.tensor_scalar(sinmM[:, :], tha[:, NPT * 32: WT], sgn[:, 0:1], None, ALU.mult), after=[sn1])
            fm1 = p.op("V", lambda e: e.tensor_scalar(cosmM[:, :], thk[:, NPT * 32: WT], SCALE, None, ALU.mult), after=[sn2])
            run_block(p)

        with ExitStack() as ph:
            psb = lambda n, s, d=F32: ph.enter_context(nc.sbuf_tensor(n, s, d))
            xps = [psb("xps0", [128, 4 * D], BF16)] * 2
            xnp_t = [psb(f"xnp_t{i}", [128, 4 * D], BF16) for i in range(2)]
            xnp_f = [psb(f"xnp_f{j}", [128, KT * 512], BF16) for j in range(2)]
            kp_t = psb("kp_t", [128, 4 * D], BF16)
            vp_t = psb("vp_t", [128, 4 * D], BF16)
            wk_b = [psb(f"wk_b{i}", [128, KT * 512], BF16) for i in range(2)]
            wv_b = [psb(f"wv_b{i}", [128, KT * 512], BF16) for i in range(2)]
            lnc = [psb("lnc", [128, D])] * 2
            lcol = [psb(f"lcol{j}", [128, 4]) for j in range(2)]
            rsc1 = [psb(f"rsc1{j}", [128, D], BF16) for j in range(2)]
            rsc2 = [psb(f"rsc2{j}", [128, D], BF16) for j in range(2)]
            psK = [ph.enter_context(nc.psum_tensor(f"psK{i}", [128, 512], F32)) for i in range(2)]
            psV = [ph.enter_context(nc.psum_tensor(f"psV{i}", [128, 512], F32)) for i in range(2)]
            psT = [ph.enter_context(nc.psum_tensor(f"psT{i}", [128, 1024], BF16)) for i in range(2)]
            psS = [ph.enter_context(nc.psum_tensor(f"psS{i}", [128, 512], F32)) for i in range(2)]

            p = Prog(counts)
            wk_tok, wv_tok = [], []
            for hf in range(2):
                wk_tok.append(p.op("G", lambda e, d=wk_b[hf], hf=hf: e.dma_start(
                    out=d[:, :], in_=wkvp_d[hf, :, :]), dma=True))
                wv_tok.append(p.op("G", lambda e, d=wv_b[hf], hf=hf: e.dma_start(
                    out=d[:, :], in_=wkvp_d[2 + hf, :, :]), dma=True))

            ld_tok = [None, None]
            ln_prev = [(None, None, None), (None, None, None)]
            stage_free = [None, None]
            xnpt_free = [None, None]
            xnpf_free = [None, None]
            psK_free = [None, None]
            psV_free = [None, None]
            psT_free = [None, None]
            kp_free = None
            vp_free = None
            rprev = [[None, None, None], [None, None, None]]
            s_chain = None
            first_s = [True, True]

            for ptb in range(NPTB):
                par = ptb % 2
                ld_tok[par] = p.op("G", lambda e, b=xps[par], i0=ptb * 512:
                                   e.dma_start(out=r3s(b[:, :], 4), in_=r3d(xpre_d[i0:i0 + 512, :])),
                                   after=[stage_free[0], stage_free[1]], dma=True)
                ln_done = []
                for i in range(4):
                    tok, ln_prev[i % 2] = emit_ln(
                        p,
                        lambda b=xps[par], i=i: b[:, i * D:(i + 1) * D],
                        lambda i=i, par=par: xnp_t[par][:, i * D:(i + 1) * D],
                        lnc[i % 2], lcol[i % 2],
                        src_after=[ld_tok[par]], dst_after=[xnpt_free[par]], prev=ln_prev[i % 2])
                    ln_done.append(tok)
                stage_free[par] = ln_done[-1]

                tr_last = None
                for i in range(4):
                    for k in range(KT):
                        tp_ = (i * KT + k) % 2
                        tm = p.op("T", lambda e, i=i, k=k, tp_=tp_, par=par: e.transpose(
                            psT[tp_][:, 0:128], xnp_t[par][:, i * D + k * 128: i * D + (k + 1) * 128], ident_b[:, :]),
                            after=[ln_done[i], psT_free[tp_], xnpf_free[par]])
                        psT_free[tp_] = p.op("S" if tp_ else "V",
                            lambda e, i=i, k=k, tp_=tp_, par=par: (e.copy if tp_ else e.tensor_copy)(
                            xnp_f[par][:, k * 512 + i * 128: k * 512 + (i + 1) * 128], psT[tp_][:, 0:128]), after=[tm])
                        tr_last = psT_free[tp_]
                xnpt_free[par] = tr_last
                xnpf_ready_l = [psT_free[0], psT_free[1]]
                xnpf_ready = tr_last

                kp_ready = {}
                vp_ready = {}
                for i in range(4):
                    for kind in range(2):
                        for hf in range(2):
                            bank = (psK if kind == 0 else psV)[hf]
                            wb = (wk_b if kind == 0 else wv_b)[hf]
                            wt = (wk_tok if kind == 0 else wv_tok)[hf]
                            free = (psK_free if kind == 0 else psV_free)[hf]

                            def mmkv(e, i=i, b=bank, w=wb, par=par):
                                last = None
                                for k in range(KT):
                                    last = e.matmul(b[:, :],
                                                    xnp_f[par][:, k * 512 + i * 128: k * 512 + (i + 1) * 128],
                                                    w[:, k * 512:(k + 1) * 512],
                                                    start=(k == 0), stop=(k == KT - 1))
                                return last
                            mt = p.op("T", mmkv, after=xnpf_ready_l + [wt, free])
                            dst = kp_t if kind == 0 else vp_t
                            eng_ev = "V" if kind == 0 else "S"
                            ev = p.op(eng_ev, lambda e, d=dst, i=i, hf=hf, b=bank, k0=(kind == 0):
                                      (e.tensor_copy if k0 else e.copy)(
                                      d[:, i * D + hf * 512: i * D + hf * 512 + 512], b[:, :]),
                                      after=[mt, kp_free if kind == 0 else vp_free])
                            if kind == 0:
                                psK_free[hf] = ev
                            else:
                                psV_free[hf] = ev
                            if hf == 1:
                                (kp_ready if kind == 0 else vp_ready)[i] = ev
                xnpf_free[par] = mt

                for i in range(4):
                    ti = ptb * 4 + i
                    pj = i % 2
                    s1b, s2b = rsc1[pj], rsc2[pj]
                    cosb = bass.AP(coswA, ti * 64, [[NPT * 64, 128], [0, 16], [1, 64]])
                    sinb = bass.AP(sinwA, ti * 64, [[NPT * 64, 128], [0, 16], [1, 64]])
                    r0 = p.op("V", lambda e, i=i, cb=cosb, s1b=s1b: e.tensor_tensor(
                        s1b[:, :], kp_t[:, i * D:(i + 1) * D], cb, ALU.mult),
                        after=[kp_ready[i], rprev[pj][0]])
                    r1 = p.op("V", lambda e, i=i, sbp=sinb, s2b=s2b: e.tensor_tensor(
                        s2b[:, :], bass.AP(kp_t, i * D + 32, [[4 * D, 128], [64, 16], [-32, 2], [1, 32]]),
                        sbp, ALU.mult), after=[kp_ready[i], rprev[pj][1], rprev[pj][2]])
                    r2 = p.op("V", lambda e, s1b=s1b, s2b=s2b: e.tensor_tensor(s1b[:, :], s1b[:, :], s2b[:, :], ALU.add), after=[r0, r1])
                    r3 = p.op("V", lambda e, s1b=s1b, s2b=s2b: e.tensor_scalar(s2b[:, :], s1b[:, :], 0.0, None, ALU.min), after=[r2])
                    r4 = p.op("S", lambda e, s2b=s2b: e.activation(s2b[:, :], s2b[:, :], AF.Exp), after=[r3])
                    r5 = p.op("V", lambda e, s1b=s1b: e.tensor_scalar(s1b[:, :], s1b[:, :], 0.0, None, ALU.max), after=[r2, r3])
                    r6 = p.op("V", lambda e, s1b=s1b, s2b=s2b: e.tensor_tensor(s1b[:, :], s1b[:, :], s2b[:, :], ALU.add), after=[r5, r4])
                    r7 = p.op("V", lambda e, i=i, ptb=ptb, s1b=s1b: e.tensor_scalar(
                        kp_t[:, i * D:(i + 1) * D], s1b[:, :],
                        prem_s[:, ptb * 4 + i: ptb * 4 + i + 1], None, ALU.mult), after=[r6, kp_free])
                    kp_ready[i] = r7
                    rprev[pj] = [r7, r6, r4]

                for i in range(4):
                    def smm(e, i=i, first=first_s[0], last_acc=(ptb == NPTB - 1 and i == 3)):
                        last = None
                        for h in range(HEADS):
                            bank = psS[h // 8]
                            rb = 64 * (h % 2)
                            cb = ((h % 8) // 2) * 64
                            st = first and (h % 8 in (0, 1))
                            sp = last_acc and (h % 8 in (6, 7))
                            last = e.matmul(bank[rb:rb + 64, cb:cb + 64],
                                            kp_t[:, i * D + h * HD: i * D + (h + 1) * HD],
                                            vp_t[:, i * D + h * HD: i * D + (h + 1) * HD],
                                            start=st, stop=sp, tile_position=(0, rb),
                                            skip_group_check=True)
                            last = e.matmul(psS[h // 8][rb:rb + 64, 256 + (h % 8): 257 + (h % 8)],
                                            kp_t[:, i * D + h * HD: i * D + (h + 1) * HD],
                                            ones_b[:, :],
                                            start=False, stop=False,
                                            tile_position=(0, rb), skip_group_check=True)
                        return last
                    s_chain = p.op("T", smm, after=[kp_ready[i], vp_ready[i], s_chain])
                    first_s = [False, False]
                kp_free = s_chain
                vp_free = s_chain

            evs = []
            for h in range(HEADS):
                rb, scol = stpos(h)
                cb = ((h % 8) // 2) * 64
                evs.append(p.op("V", lambda e, h=h, rb=rb, scol=scol, cb=cb: e.tensor_copy(
                    st_f[rb:rb + 64, scol:scol + 64], psS[h // 8][rb:rb + 64, cb:cb + 64]), after=[s_chain]))
                evs.append(p.op("V", lambda e, h=h, rb=rb, scol=scol: e.tensor_copy(
                    st_f[rb:rb + 64, scol + 64:scol + 65],
                    psS[h // 8][rb:rb + 64, 256 + (h % 8): 257 + (h % 8)]), after=[s_chain]))
            ev2 = p.op("V", lambda e: e.tensor_copy(st_b[:, :], st_f[:, :]), after=evs)
            if dbg:
                p.op("Y", lambda e: e.dma_start(out=dbg_d["d_st"][:, :], in_=st_f[:, :]), after=[ev2], dma=True)
            run_block(p)
        tabstk.close()

        mid = ExitStack()
        qf = mid.enter_context(nc.sbuf_tensor("qf", [128, KT * n_own], BF16))
        kf = mid.enter_context(nc.sbuf_tensor("kf", [128, KT * n_own], BF16))
        vt = mid.enter_context(nc.sbuf_tensor("vt", [128, NT * D], BF16))
        with ExitStack() as ph:
            psb = lambda n, s, d=F32: ph.enter_context(nc.sbuf_tensor(n, s, d))
            xn_t = psb("xn_t", [128, D], BF16)
            xn_f = psb("xn_f", [128, KT * 512], BF16)
            lnc = [psb("lnc2", [128, D])] * 2
            lcol = [psb(f"lcol2{j}", [128, 4]) for j in range(2)]
            ra = [psb(f"ra{j}", [128, 512]) for j in range(2)]
            rb_ = [psb(f"rb{j}", [128, 512]) for j in range(2)]
            wqk = [psb(f"wqk{i}", [128, KT * 128], BF16) for i in range(2)]
            wvb = psb("wvb", [128, KT * 512], BF16)
            psQ = [ph.enter_context(nc.psum_tensor(f"psQ{i}", [128, 512], F32)) for i in range(4)]
            psV2 = [ph.enter_context(nc.psum_tensor(f"psV2{i}", [128, 512], F32)) for i in range(2)]
            psT2 = [ph.enter_context(nc.psum_tensor(f"psT2{i}", [128, 1024], BF16)) for i in range(2)]

            p = Prog(counts)
            xnt_free = []
            dbg_free = []
            ln_prev = [(None, None, None), (None, None, None)]
            xnf_free = None
            psT2_free = [None, None]
            qprev = [[None, None], [None, None]]
            wqk_free = [None] * 2
            wvb_free = None
            psQ_free = [None] * 4
            psV2_free = [None] * 2
            last_write = []

            for tb in range(NTB):
                tr_last = None
                for i in range(4):
                    ti = tb * 4 + i
                    tok, ln_prev[i % 2] = emit_ln(
                        p,
                        lambda ti=ti: x_res[:, ti * D:(ti + 1) * D],
                        lambda: xn_t[:, :],
                        lnc[i % 2], lcol[i % 2], dst_after=xnt_free, prev=ln_prev[i % 2])
                    if dbg:
                        dbg_free.append(p.op("G", lambda e, ti=ti: e.dma_start(
                            out=dbg_d["d_xn1"][ti * 128:(ti + 1) * 128, :],
                            in_=xn_t[:, :]), after=[tok], dma=True))
                    for k in range(KT):
                        tp_ = k % 2
                        tm = p.op("T", lambda e, i=i, k=k, tp_=tp_: e.transpose(
                            psT2[tp_][:, 0:128], xn_t[:, k * 128:(k + 1) * 128], ident_b[:, :]),
                            after=[tok, psT2_free[tp_], xnf_free])
                        psT2_free[tp_] = p.op("V", lambda e, i=i, k=k, tp_=tp_: e.tensor_copy(
                            xn_f[:, k * 512 + i * 128: k * 512 + (i + 1) * 128], psT2[tp_][:, 0:128]), after=[tm])
                        tr_last = psT2_free[tp_]
                    xnt_free = [tr_last, psT2_free[0], psT2_free[1]] + dbg_free
                    dbg_free = []
                xnf_ready_l = [psT2_free[0], psT2_free[1]]
                xnf_ready = tr_last

                for pair, dst in ((0, qf), (1, kf)):
                    for ft in range(KT):
                        bi = 0
                        bq, bp = psQ[(ft % 2) * 2], psQ[(ft % 2) * 2 + 1]
                        i_main = pair * 2 * KT + ft
                        i_perm = i_main + KT
                        wbi = 0
                        t_w0 = p.op("G", lambda e, w=wqk[wbi], ix=i_main: e.dma_start(
                            out=w[:, :], in_=wqkt_d[ix, :, :]),
                            after=[wqk_free[wbi]], dma=True)
                        t_w1 = p.op("G", lambda e, w=wqk[wbi + 1], ix=i_perm: e.dma_start(
                            out=w[:, :], in_=wqkt_d[ix, :, :]),
                            after=[wqk_free[wbi + 1]], dma=True)

                        bqi = (ft % 2) * 2

                        def mmq(e, w=wqk[wbi], b=bq):
                            last = None
                            for k in range(KT):
                                last = e.matmul(b[:, :], w[:, k * 128:(k + 1) * 128],
                                                xn_f[:, k * 512:(k + 1) * 512],
                                                start=(k == 0), stop=(k == KT - 1))
                            return last
                        m0 = p.op("T", mmq, after=xnf_ready_l + [t_w0, psQ_free[bqi]])

                        def mmp(e, w=wqk[wbi + 1], b=bp):
                            last = None
                            for k in range(KT):
                                last = e.matmul(b[:, :], w[:, k * 128:(k + 1) * 128],
                                                xn_f[:, k * 512:(k + 1) * 512],
                                                start=(k == 0), stop=(k == KT - 1))
                            return last
                        m1 = p.op("T", mmp, after=[t_w1, psQ_free[bqi + 1], m0])
                        wqk_free[wbi] = m0
                        wqk_free[wbi + 1] = m1
                        pj = ft % 2
                        rab, rbb = ra[pj], rb_[pj]
                        r0 = p.op("V", lambda e, b=bq, tb=tb, rab=rab: e.tensor_tensor(
                            rab[:, :], b[:, :], cosmM[:, tb * 512:(tb + 1) * 512], ALU.mult),
                            after=[m0, qprev[pj][0]])
                        r1 = p.op("V", lambda e, b=bp, tb=tb, rbb=rbb: e.tensor_tensor(
                            rbb[:, :], b[:, :], sinmM[:, tb * 512:(tb + 1) * 512], ALU.mult),
                            after=[m1, qprev[pj][1]])
                        r2 = p.op("V", lambda e, rab=rab, rbb=rbb: e.tensor_tensor(rab[:, :], rab[:, :], rbb[:, :], ALU.add), after=[r0, r1])
                        r3 = p.op("V", lambda e, rab=rab, rbb=rbb: e.tensor_scalar(rbb[:, :], rab[:, :], 0.0, None, ALU.min), after=[r2])
                        r4 = p.op("S", lambda e, rbb=rbb: e.activation(rbb[:, :], rbb[:, :], AF.Exp), after=[r3])
                        r5 = p.op("V", lambda e, rab=rab: e.tensor_scalar(rab[:, :], rab[:, :], 0.0, None, ALU.max), after=[r2, r3])
                        r6 = p.op("V", lambda e, d=dst, ft=ft, tb=tb, rab=rab, rbb=rbb: e.tensor_tensor(
                            d[:, ft * n_own + tb * 512: ft * n_own + (tb + 1) * 512],
                            rab[:, :], rbb[:, :], ALU.add), after=[r5, r4])
                        qprev[pj] = [r6, r4]
                        psQ_free[bqi] = r6
                        psQ_free[bqi + 1] = r6
                        last_write.append(r6)

                for hf in range(2):
                    t_wv = p.op("G", lambda e, hf=hf: e.dma_start(
                        out=wvb[:, :], in_=wkvp_d[2 + hf, :, :]), after=[wvb_free], dma=True)
                    for i in range(4):
                        ti = tb * 4 + i
                        bank = psV2[i % 2]

                        def mmv(e, i=i, b=bank):
                            last = None
                            for k in range(KT):
                                last = e.matmul(b[:, :], xn_f[:, k * 512 + i * 128: k * 512 + (i + 1) * 128],
                                                wvb[:, k * 512:(k + 1) * 512],
                                                start=(k == 0), stop=(k == KT - 1))
                            return last
                        mv = p.op("T", mmv, after=xnf_ready_l + [t_wv, psV2_free[i % 2]])
                        wvb_free = mv
                        ev = p.op("S", lambda e, ti=ti, hf=hf, b=bank: e.copy(
                            vt[:, ti * D + hf * 512: ti * D + hf * 512 + 512], b[:, :]), after=[mv])
                        psV2_free[i % 2] = ev
                        last_write.append(ev)
                xnf_free = wvb_free

            if dbg:
                lw = last_write[-20:]
                p.op("G", lambda e: e.dma_start(out=dbg_d["d_qf"][:, :], in_=qf[:, :]), after=lw, dma=True)
                p.op("G", lambda e: e.dma_start(out=dbg_d["d_kf"][:, :], in_=kf[:, :]), after=lw, dma=True)
                p.op("G", lambda e: e.dma_start(
                    out=r3d(dbg_d["d_vt"][:, :]), in_=r3s(vt[:, :], NT)), after=lw, dma=True)
            run_block(p)

        with ExitStack() as ph:
            psb = lambda n, s, d=F32: ph.enter_context(nc.sbuf_tensor(n, s, d))
            am = [psb(f"am{i}", [128, 128], BF16) for i in range(2)]
            kcs = [psb(f"kcs{i}", [128, 64], BF16) for i in range(2)]
            dcol = [psb(f"dcol{j}", [128, 2]) for j in range(2)]
            otc = psb("otc", [128, D], BF16)
            otf = psb("otf", [128, KT * 128], BF16)
            woutb = psb("woutb", [128, KT * D], BF16)
            pA = [ph.enter_context(nc.psum_tensor(f"pA{i}", [128, 512], F32)) for i in range(2)]
            pO = [ph.enter_context(nc.psum_tensor(f"pO{i}", [128, 512], F32)) for i in range(2)]
            pS2 = [ph.enter_context(nc.psum_tensor("pS2", [128, 512], F32))] * 2
            pB16 = ph.enter_context(nc.psum_tensor("pB16", [128, 1024], BF16))
            pP = [ph.enter_context(nc.psum_tensor(f"pP{i}", [128, 512], F32)) for i in range(2)]

            p = Prog(counts)
            t_wo = p.op("G", lambda e: e.dma_start(
                out=woutb[:, :], in_=wout_d[:, :]), dma=True)

            am_free = [None, None]
            dcol_free = [None, None]
            kcs_free = [None, None]
            pA_free = [None, None]
            pO_free = [None, None]
            pS2_free = [None, None]
            pB16_free = None
            pP_free = [None, None]
            st_rd = {h: None for h in range(HEADS)}
            st_wr = {h: None for h in range(HEADS)}
            otc_wr = [None] * HEADS
            otc_free = None
            otf_free = [None] * KT
            x1_toks = []

            def qsl(buf, h, c):
                return buf[64 * (h % 2):64 * (h % 2) + 64,
                           (h // 2) * n_own + c * 128:(h // 2) * n_own + (c + 1) * 128]

            for c in range(NCH):
                for h in range(HEADS):
                    par = h % 2
                    base, scol = stpos(h)
                    a0 = p.op("T", lambda e, h=h, c=c, par=par: e.matmul(
                        pA[par][:, 0:128], qsl(kf, h, c), qsl(qf, h, c), start=True, stop=True),
                        after=[pA_free[par]])
                    a1 = p.op("V", lambda e, par=par: e.tensor_tensor(
                        am[par][:, :], pA[par][:, 0:128], cmask_b[:, :], ALU.mult), after=[a0, am_free[par]])
                    pA_free[par] = a1
                    k0 = p.op("T", lambda e, h=h, c=c, base=base: e.transpose(
                        pB16[:, 0:64], qsl(kf, h, c), ident_b[base:base + 64, base:base + 64]),
                        after=[pB16_free])
                    k1 = p.op("S", lambda e, par=par: e.copy(kcs[par][:, :], pB16[:, 0:64]),
                              after=[k0, kcs_free[par]])
                    pB16_free = k1

                    def mmo(e, h=h, c=c, par=par, base=base, scol=scol):
                        ob = pO[par]
                        e.matmul(ob[:, 0:64], am[par][:, :], vt[:, c * D + h * HD: c * D + (h + 1) * HD],
                                 start=True, stop=False)
                        e.matmul(ob[:, 0:64], qsl(qf, h, c), st_b[base:base + 64, scol:scol + 64],
                                 start=False, stop=False)
                        e.matmul(ob[:, 64:65], am[par][:, :], ones_b[:, :], start=False, stop=False)
                        return e.matmul(ob[:, 64:65], qsl(qf, h, c),
                                        st_b[base:base + 64, scol + 64:scol + 65], start=False, stop=True)
                    o0 = p.op("T", mmo, after=[a1, pO_free[par], st_wr[h]])
                    st_rd[h] = o0
                    am_free[par] = o0
                    d0 = p.op("V", lambda e, par=par: e.tensor_scalar(
                        dcol[par][:, 0:1], pO[par][:, 64:65], 1e-6, None, ALU.max), after=[o0])
                    d1 = p.op("V", lambda e, par=par: e.reciprocal(dcol[par][:, 1:2], dcol[par][:, 0:1]),
                              after=[d0, dcol_free[par]])
                    d2 = p.op("S", lambda e, h=h, par=par: e.activation(
                        otc[:, h * HD:(h + 1) * HD], pO[par][:, 0:64], AF.Copy, scale=dcol[par][:, 1:2]),
                        after=[d1, otc_free])
                    dcol_free[par] = d2
                    pO_free[par] = d2
                    otc_wr[h] = d2

                    def mms(e, h=h, c=c, base=base, par=par):
                        e.matmul(pS2[par][base:base + 64, 0:64], kcs[par][:, :],
                                 vt[:, c * D + h * HD: c * D + (h + 1) * HD],
                                 start=True, stop=False, tile_position=(0, base))
                        return e.matmul(pS2[par][base:base + 64, 64:65], kcs[par][:, :], ones_b[:, :],
                                        start=False, stop=True, tile_position=(0, base))
                    s0 = p.op("T", mms, after=[k1, pS2_free[0], pS2_free[1]])
                    kcs_free[par] = s0
                    s1 = p.op("V", lambda e, base=base, scol=scol, par=par: e.tensor_tensor(
                        st_f[base:base + 64, scol:scol + 65], st_f[base:base + 64, scol:scol + 65],
                        pS2[par][base:base + 64, 0:65], ALU.add), after=[s0, st_rd[h]])
                    s2 = p.op("S", lambda e, base=base, scol=scol: e.copy(
                        st_b[base:base + 64, scol:scol + 65], st_f[base:base + 64, scol:scol + 65]), after=[s1])
                    pS2_free[par] = s2
                    st_wr[h] = s2

                trs = []
                for k in range(KT):
                    tm = p.op("T", lambda e, k=k: e.transpose(
                        pB16[:, 0:128], otc[:, k * 128:(k + 1) * 128], ident_b[:, :]),
                        after=[pB16_free, otc_wr[2 * k], otc_wr[2 * k + 1], otf_free[k]])
                    pB16_free = p.op("V", lambda e, k=k: e.tensor_copy(
                        otf[:, k * 128:(k + 1) * 128], pB16[:, 0:128]), after=[tm])
                    trs.append(pB16_free)
                otc_free = trs[-1]

                def mmp1(e):
                    last = None
                    for k in range(KT):
                        last = e.matmul(pP[0][:, :], otf[:, k * 128:(k + 1) * 128],
                                        woutb[:, k * D: k * D + 512],
                                        start=(k == 0), stop=(k == KT - 1))
                    return last
                pm0 = p.op("T", mmp1, after=trs + [t_wo, pP_free[0]])

                def mmp2(e):
                    last = None
                    for k in range(KT):
                        last = e.matmul(pP[1][:, :], otf[:, k * 128:(k + 1) * 128],
                                        woutb[:, k * D + 512: (k + 1) * D],
                                        start=(k == 0), stop=(k == KT - 1))
                    return last
                pm1 = p.op("T", mmp2, after=[pm0, pP_free[1]])
                for k in range(KT):
                    otf_free[k] = pm1
                e0 = p.op("V", lambda e, c=c: e.tensor_tensor(
                    x_res[:, c * D: c * D + 512], x_res[:, c * D: c * D + 512], pP[0][:, :], ALU.add), after=[pm0])
                e1 = p.op("V", lambda e, c=c: e.tensor_tensor(
                    x_res[:, c * D + 512: (c + 1) * D], x_res[:, c * D + 512: (c + 1) * D], pP[1][:, :], ALU.add), after=[pm1])
                pP_free[0] = e0
                pP_free[1] = e1
                x1_toks = [e0, e1]

            if dbg:
                p.op("Y", lambda e: e.dma_start(
                    out=r3d(dbg_d["d_x1"][:, :]), in_=r3s(x_res[:, :], NT)), after=x1_toks, dma=True)
            run_block(p)
        mid.close()

        xn2f = sb("xn2f", [128, KT * n_own], BF16)
        with ExitStack() as ph:
            psb = lambda n, s, d=F32: ph.enter_context(nc.sbuf_tensor(n, s, d))
            xn_t = [psb(f"xn_t2{j}", [128, D], BF16) for j in range(2)]
            lnc = [psb(f"lnc3{j}", [128, D]) for j in range(2)]
            lcol = [psb(f"lcol3{j}", [128, 4]) for j in range(2)]
            rwtb = psb("rwtb", [128, KT * 8], BF16)
            lgall = psb("lgall", [128, NT * 8])
            l2all = psb("l2all", [128, NT * 8])
            m1t = psb("m1t", [128, NT])
            m2t = psb("m2t", [128, NT])
            g1t = psb("g1t", [128, NT])
            g2t = psb("g2t", [128, NT])
            pR = ph.enter_context(nc.psum_tensor("pR", [128, 512], F32))
            pT4 = [ph.enter_context(nc.psum_tensor(f"pT4{i}", [128, 1024], BF16)) for i in range(2)]

            p = Prog(counts)
            t_rw = p.op("G", lambda e: e.dma_start(
                out=rwtb[:, :], in_=rwt_d[:, :]), dma=True)
            pT4_free = [None, None]
            pR_free = None
            lg_toks = []
            xnt_free2 = [[], []]
            dbg_free = []
            w8_last = None
            ln_prev = [(None, None, None), (None, None, None)]
            for ti in range(NT):
                pj = ti % 2
                a7, ln_prev[pj] = emit_ln(
                    p,
                    lambda ti=ti: x_res[:, ti * D:(ti + 1) * D],
                    lambda pj=pj: xn_t[pj][:, :],
                    lnc[pj], lcol[pj], dst_after=xnt_free2[pj], prev=ln_prev[pj])
                if dbg:
                    dbg_free.append(p.op("G", lambda e, ti=ti, pj=pj: e.dma_start(
                        out=dbg_d["d_xn2"][ti * 128:(ti + 1) * 128, :],
                        in_=xn_t[pj][:, :]), after=[a7], dma=True))
                trs = []
                for k in range(KT):
                    tp_ = k % 2
                    tm = p.op("T", lambda e, k=k, tp_=tp_, pj=pj: e.transpose(
                        pT4[tp_][:, 0:128], xn_t[pj][:, k * 128:(k + 1) * 128], ident_b[:, :]),
                        after=[a7, pT4_free[tp_]])
                    pT4_free[tp_] = p.op("S" if tp_ else "V",
                        lambda e, k=k, ti=ti, tp_=tp_: (e.copy if tp_ else e.tensor_copy)(
                        xn2f[:, k * n_own + ti * 128: k * n_own + (ti + 1) * 128], pT4[tp_][:, 0:128]), after=[tm])
                    trs.append(pT4_free[tp_])
                xnt_free2[pj] = [trs[-1], trs[-2]] + dbg_free
                dbg_free = []

                def mmr(e, ti=ti):
                    last = None
                    for k in range(KT):
                        last = e.matmul(pR[:, 0:8], xn2f[:, k * n_own + ti * 128: k * n_own + (ti + 1) * 128],
                                        rwtb[:, k * 8:(k + 1) * 8], start=(k == 0), stop=(k == KT - 1))
                    return last
                r0 = p.op("T", mmr, after=trs + [t_rw, pR_free])
                pR_free = p.op("V", lambda e, ti=ti: e.tensor_copy(
                    lgall[:, ti * 8:(ti + 1) * 8], pR[:, 0:8]), after=[r0])
                lg_toks.append(pR_free)

            bbcb = bass.AP(bbc_s, 0, [[8, 128], [0, NT], [1, 8]])
            b0 = p.op("V", lambda e: e.tensor_tensor(lgall[:, :], lgall[:, :], bbcb, ALU.add), after=lg_toks)
            b1 = p.op("V", lambda e: e.tensor_reduce(
                m1t[:, :], r3s(lgall[:, :], NT), axis=AX.X, op=ALU.max), after=[b0])
            m1b = bass.AP(m1t, 0, [[NT, 128], [1, NT], [0, 8]])
            b2 = p.op("V", lambda e: e.tensor_tensor(l2all[:, :], lgall[:, :], m1b, ALU.is_equal), after=[b1])
            b3 = p.op("V", lambda e: e.scalar_tensor_tensor(l2all[:, :], l2all[:, :], -1e30, lgall[:, :], ALU.mult, ALU.add), after=[b2])
            b4 = p.op("V", lambda e: e.tensor_reduce(
                m2t[:, :], r3s(l2all[:, :], NT), axis=AX.X, op=ALU.max), after=[b3])
            b5 = p.op("V", lambda e: e.tensor_tensor(g1t[:, :], m1t[:, :], m2t[:, :], ALU.subtract), after=[b4])
            b6 = p.op("S", lambda e: e.activation(g2t[:, :], g1t[:, :], AF.Sigmoid, scale=-1.0), after=[b5])
            b7 = p.op("S", lambda e: e.activation(g1t[:, :], g1t[:, :], AF.Sigmoid), after=[b6])
            m2b = bass.AP(m2t, 0, [[NT, 128], [1, NT], [0, 8]])
            b8 = p.op("V", lambda e: e.tensor_tensor(lgall[:, :], lgall[:, :], m1b, ALU.is_equal), after=[b3, b1])
            b9 = p.op("V", lambda e: e.tensor_tensor(l2all[:, :], l2all[:, :], m2b, ALU.is_equal), after=[b4])
            g1b = bass.AP(g1t, 0, [[NT, 128], [1, NT], [0, 8]])
            g2b = bass.AP(g2t, 0, [[NT, 128], [1, NT], [0, 8]])
            ba = p.op("V", lambda e: e.tensor_tensor(lgall[:, :], lgall[:, :], g1b, ALU.mult), after=[b8, b7])
            bb = p.op("V", lambda e: e.tensor_tensor(l2all[:, :], l2all[:, :], g2b, ALU.mult), after=[b9, b6])
            w8_last = p.op("V", lambda e: e.tensor_tensor(w8_s[:, :], lgall[:, :], l2all[:, :], ALU.add), after=[ba, bb])
            if dbg:
                p.op("Y", lambda e: e.dma_start(out=dbg_d["d_w8"][:, :], in_=w8_s[:, :]), after=[w8_last], dma=True)
            run_block(p)

        with ExitStack() as ph:
            psb = lambda n, s, d=F32: ph.enter_context(nc.sbuf_tensor(n, s, d))
            gb = [psb(f"gb{i}", [128, KT * 512], BF16) for i in range(2)]
            ub = [psb(f"ub{i}", [128, KT * 512], BF16) for i in range(2)]
            db_ = [psb(f"db{i}", [128, 4 * D], BF16) for i in range(2)]
            hs = psb("hs", [128, 4 * n_own], BF16)
            sil = [psb(f"sil{i}", [128, 512]) for i in range(3)]
            pG = [ph.enter_context(nc.psum_tensor(f"pG{i}", [128, 512], F32)) for i in range(3)]
            pU = [ph.enter_context(nc.psum_tensor(f"pU{i}", [128, 512], F32)) for i in range(3)]
            pD = [ph.enter_context(nc.psum_tensor(f"pD{i}", [128, 512], F32)) for i in range(2)]

            p = Prog(counts)
            wg_free = [None, None]
            wu_free = [None, None]
            wd_free = [None, None]
            pG_free = [None, None, None]
            pU_free = [None, None, None]
            sil3 = None
            pD_free = [None, None]
            sil_free = [None, None, None]
            hs_free = None
            out_toks = []

            for ei in range(E):
                for fb in range(NFB):
                    par = (ei * NFB + fb) % 2
                    tg = p.op("G", lambda e, ei=ei, fb=fb, par=par: e.dma_start(
                        out=gb[par][:, :], in_=gw_d[ei * NFB + fb, :, :]),
                        after=[wg_free[par]], dma=True)
                    tu = p.op("G", lambda e, ei=ei, fb=fb, par=par: e.dma_start(
                        out=ub[par][:, :], in_=uw_d[ei * NFB + fb, :, :]),
                        after=[wu_free[par]], dma=True)
                    td = p.op("G", lambda e, ei=ei, fb=fb, par=par: e.dma_start(
                        out=db_[par][:, :], in_=dw_d[ei * NFB + fb, :, :]),
                        after=[wd_free[par]], dma=True)
                    for tb in range(NTB):
                        p1_done = []
                        for f0 in range(4):
                            bpar = f0 % 3

                            def mmg(e, par=par, f0=f0, tb=tb, bpar=bpar):
                                last = None
                                for k in range(KT):
                                    last = e.matmul(pG[bpar][:, :],
                                                    gb[par][:, k * 512 + f0 * 128: k * 512 + (f0 + 1) * 128],
                                                    xn2f[:, k * n_own + tb * 512: k * n_own + (tb + 1) * 512],
                                                    start=(k == 0), stop=(k == KT - 1))
                                return last
                            mg = p.op("T", mmg, after=[tg, pG_free[bpar]])

                            def mmu(e, par=par, f0=f0, tb=tb, bpar=bpar):
                                last = None
                                for k in range(KT):
                                    last = e.matmul(pU[bpar][:, :],
                                                    ub[par][:, k * 512 + f0 * 128: k * 512 + (f0 + 1) * 128],
                                                    xn2f[:, k * n_own + tb * 512: k * n_own + (tb + 1) * 512],
                                                    start=(k == 0), stop=(k == KT - 1))
                                return last
                            mu = p.op("T", mmu, after=[tu, pU_free[bpar], mg])
                            wg_free[par] = mg
                            wu_free[par] = mu
                            sv = p.op("S", lambda e, bpar=bpar: e.activation(
                                sil[bpar][:, :], pG[bpar][:, :], AF.Silu), after=[mg, sil_free[bpar]])
                            pG_free[bpar] = sv
                            hw = p.op("V", lambda e, f0=f0, tb=tb, bpar=bpar: e.tensor_tensor(
                                hs[:, f0 * n_own + tb * 512: f0 * n_own + (tb + 1) * 512],
                                sil[bpar][:, :], pU[bpar][:, :], ALU.mult),
                                after=[sv, mu, hs_free])
                            pU_free[bpar] = hw
                            sil_free[bpar] = hw
                            p1_done.append(hw)
                    last_rd = None
                    for tb in range(NTB):
                        for i in range(4):
                            ti = tb * 4 + i
                            for hf in range(2):
                                bpar = hf

                                def mmd(e, par=par, ti=ti, hf=hf, bpar=bpar):
                                    last = None
                                    for f0 in range(4):
                                        last = e.matmul(pD[bpar][:, :],
                                                        hs[:, f0 * n_own + ti * 128: f0 * n_own + (ti + 1) * 128],
                                                        db_[par][:, f0 * D + hf * 512: f0 * D + (hf + 1) * 512],
                                                        start=(f0 == 0), stop=(f0 == 3))
                                    return last
                                md = p.op("T", mmd, after=p1_done + [td, pD_free[bpar]])
                                wd_free[par] = md
                                last_rd = md
                                ev = p.op("V", lambda e, ti=ti, hf=hf, ei=ei, bpar=bpar: e.scalar_tensor_tensor(
                                    x_res[:, ti * D + hf * 512: ti * D + (hf + 1) * 512],
                                    pD[bpar][:, :], w8_s[:, ti * 8 + ei: ti * 8 + ei + 1],
                                    x_res[:, ti * D + hf * 512: ti * D + (hf + 1) * 512],
                                    ALU.mult, ALU.add), after=[md])
                                pD_free[bpar] = ev
                                out_toks.append(ev)
                    hs_free = last_rd
            run_block(p)

        p = Prog(counts)
        st_tok = p.op("Y", lambda e: e.dma_start(
            out=r3d(out_d[:, :]), in_=r3s(x_res[:, :], NT)),
            after=out_toks[-2 * NT:], dma=True)
        p.op("Y", lambda e: e.nop(), after=[st_tok])
        run_block(p)

    return nc



def _prep_host(L0_attn_x, L0_attn_w_qkv, L0_attn_w_out, L0_ffn_router_w,
               L0_ffn_router_b, L0_ffn_gate_w, L0_ffn_up_w, L0_ffn_down_w,
               n_own, n_pre, f_ffn):
    x = np.ascontiguousarray(np.asarray(L0_attn_x, dtype=np.float32).reshape(-1, D))
    wqkv = np.asarray(L0_attn_w_qkv, dtype=np.float32)
    rw = np.asarray(L0_ffn_router_w, dtype=np.float32)
    rb = np.asarray(L0_ffn_router_b, dtype=np.float32)
    gw = np.asarray(L0_ffn_gate_w, dtype=np.float32)
    uw = np.asarray(L0_ffn_up_w, dtype=np.float32)
    dw = np.asarray(L0_ffn_down_w, dtype=np.float32)

    ntok = x.shape[0]
    assert ntok == NC * n_own

    KTn = D // 128
    NFB = f_ffn // 512

    def pshuf(mat, wtile):
        Wc = mat.shape[1]
        a = mat.reshape(KTn, 128, Wc // wtile, wtile)
        return np.ascontiguousarray(a.transpose(2, 1, 0, 3).reshape(Wc // wtile, 128, KTn * wtile))

    perm = np.arange(D)
    perm = (perm // HD) * HD + ((perm % HD) + HD // 2) % HD
    wq = wqkv[:, 0:D]
    wk = wqkv[:, D:2 * D]
    wv = wqkv[:, 2 * D:3 * D]
    wqkt = np.ascontiguousarray(np.concatenate(
        [pshuf(wq, 128), pshuf(wq[:, perm], 128),
         pshuf(wk, 128), pshuf(wk[:, perm], 128)], axis=0))
    wkvp = np.ascontiguousarray(np.concatenate([pshuf(wk, 512), pshuf(wv, 512)], axis=0))
    wouts = pshuf(np.ascontiguousarray(np.asarray(L0_attn_w_out, dtype=np.float32)), D)[0]
    rwt = pshuf(np.ascontiguousarray(rw.T), 8)[0]
    bbc = np.ascontiguousarray(np.broadcast_to(rb[None, :], (128, 8)).astype(np.float32))
    gwt = np.concatenate([pshuf(np.ascontiguousarray(gw[e].T), 512) for e in range(8)], axis=0)
    uwt = np.concatenate([pshuf(np.ascontiguousarray(uw[e].T), 512) for e in range(8)], axis=0)
    def pshuf_d(mat):
        a = mat.reshape(NFB, 4, 128, D)
        return np.ascontiguousarray(a.transpose(0, 2, 1, 3).reshape(NFB, 128, 4 * D))
    dwt = np.concatenate([pshuf_d(np.ascontiguousarray(dw[e].T)) for e in range(8)], axis=0)

    NPT = n_pre // 128
    r = np.arange(128)
    jcol = (r % 32).astype(np.float32).reshape(128, 1)
    j64 = np.ascontiguousarray(
        np.broadcast_to(np.arange(32, dtype=np.float32)[None, :], (128, 32)))
    sgn = np.where((r % 64) < 32, -np.float32(SCALE), np.float32(SCALE)).astype(np.float32).reshape(128, 1)
    posp = (np.arange(NPT, dtype=np.float32)[None, :] * 128 + r[:, None]).astype(np.float32)
    ident = np.eye(128, dtype=np.float32)
    cmask = np.triu(np.ones((128, 128), dtype=np.float32))
    ccol = np.zeros((128, 2), np.float32)
    ccol[:, 0] = EPS
    ccol[:, 1] = 1.0

    xpre = np.zeros((n_pre, D), dtype=np.float32)
    npre_eff = min(n_pre, (NC - 1) * n_own)
    xpre[:npre_eff] = x[:npre_eff]

    in_maps = []
    for c in range(NC):
        pos0 = c * n_own
        posb = np.ascontiguousarray(np.broadcast_to(
            (pos0 + np.arange(n_own, dtype=np.float32))[None, :], (128, n_own)))
        prem = np.zeros((128, NPT), np.float32)
        npre_c = c * n_own
        full_tiles = npre_c // 128
        prem[:, :full_tiles] = 1.0
        rem = npre_c - full_tiles * 128
        if rem > 0:
            prem[:rem, full_tiles] = 1.0
        in_maps.append(dict(
            x=np.ascontiguousarray(x[c * n_own:(c + 1) * n_own]),
            xpre=xpre, wqkt=wqkt, wkvp=wkvp, wout=wouts, rwt=rwt, bbc=bbc,
            gw=gwt, uw=uwt, dw=dwt,
            posb=posb, posp=posp, jcol=jcol, j64=j64, sgn=sgn,
            prem=prem, ident=ident, cmask=cmask, ccol=ccol,
        ))
    return in_maps


def kernel(L0_attn_x, L0_attn_w_qkv, L0_attn_w_out, L0_ffn_router_w,
           L0_ffn_router_b, L0_ffn_gate_w, L0_ffn_up_w, L0_ffn_down_w):
    n_own = 2048
    n_pre = 14336
    f_ffn = 4096
    in_maps = _prep_host(L0_attn_x, L0_attn_w_qkv, L0_attn_w_out,
                         L0_ffn_router_w, L0_ffn_router_b, L0_ffn_gate_w,
                         L0_ffn_up_w, L0_ffn_down_w, n_own, n_pre, f_ffn)
    nc = build(n_own, n_pre, f_ffn, dbg=False)
    res = run_bass_kernel_spmd(nc, in_maps, core_ids=list(range(NC)))
    out = np.concatenate([np.asarray(r["out"]) for r in res.results], axis=0)
    return out.reshape(1, NC * n_own, D)


# version 19
# speedup vs baseline: 1.0701x; 1.0701x over previous
import math
import numpy as np
import concourse.bass as bass
import concourse.mybir as mybir
from concourse.bass_utils import run_bass_kernel_spmd

F32 = mybir.dt.float32
BF16 = mybir.dt.bfloat16
I32 = mybir.dt.int32
AF = mybir.ActivationFunctionType
ALU = mybir.AluOpType
AX = mybir.AxisListType

NC = 8
D = 1024
HEADS = 16
HD = 64
KT = D // 128
SCALE = float(HD ** (-0.25))
PI = float(np.float32(math.pi))
EPS = 1e-5


def _trunc12(x):
    import struct
    b = struct.unpack('>I', struct.pack('>f', np.float32(x)))[0]
    b &= ~((1 << 11) - 1)
    return float(struct.unpack('>f', struct.pack('>I', b))[0])


TWO_PI = 2.0 * math.pi
CW1 = _trunc12(TWO_PI)
CW2 = _trunc12(TWO_PI - CW1)
CW3 = float(np.float32(TWO_PI - CW1 - CW2))
INV_2PI = float(np.float32(1.0 / TWO_PI))
S_HI = float(np.float32(-math.log(10000.0) / 32.0))
S_LO = float(np.float32(-math.log(10000.0) / 32.0 - S_HI))

NLANES = 4
DMA_LANES = tuple(f"Y{i}" for i in range(NLANES)) + tuple(f"G{i}" for i in range(NLANES))
ENGS = ("T", "V", "S", "G", "Y")
SEM_NAMES = ENGS + DMA_LANES
_AUTO_EDGE = ("V", "S", "G")


class Prog:

    def __init__(self, counts):
        self.q = {e: [] for e in ENGS}
        self.count = counts
        self.entry_waits = [(l, counts[l]) for l in DMA_LANES if counts[l] > 0]

    def op(self, eng, fn, after=(), dma=False):
        waits = {}
        if eng in _AUTO_EDGE and not dma and self.count[eng] > 3:
            waits[eng] = self.count[eng] - 3
        for t in after:
            if t is None:
                continue
            e2, v = t
            if e2 == eng == "T":
                continue
            if v > 0:
                waits[e2] = max(waits.get(e2, 0), v)
        if dma:
            lane_i = self.count[eng + "_rr"] if (eng + "_rr") in self.count else 0
            self.count[eng + "_rr"] = (lane_i + 1) % NLANES
            lane = f"{eng}{lane_i}"
            if self.count[lane] > 0:
                waits[lane] = max(waits.get(lane, 0), self.count[lane])
            self.count[lane] += 16
            tok = (lane, self.count[lane])
            self.q[eng].append((fn, sorted(waits.items()), 16, lane))
            return tok
        self.count[eng] += 1
        tok = (eng, self.count[eng])
        self.q[eng].append((fn, sorted(waits.items()), 1, eng))
        return tok

    def replay(self, eng_name, eng_obj, sems):
        if self.q[eng_name]:
            for e2, v in self.entry_waits:
                eng_obj.wait_ge(sems[e2], v)
        for fn, waits, inc, sem in self.q[eng_name]:
            for e2, v in waits:
                eng_obj.wait_ge(sems[e2], v)
            inst = fn(eng_obj)
            inst.then_inc(sems[sem], inc)


def r3d(ap, p=128):
    return ap.rearrange("(a p) b -> p a b", p=p)


def r3s(ap, a):
    return ap.rearrange("p (a b) -> p a b", a=a)


def build(n_own, n_pre, f_ffn, dbg=False):
    NT = n_own // 128
    NTB = n_own // 512
    NCH = n_own // 128
    NPTB = n_pre // 512
    NPT = n_pre // 128
    NFB = f_ffn // 512
    E = 8

    def stpos(h):
        return 64 * (h % 2), (h // 8) * 260 + ((h % 8) // 2) * 65

    nc = bass.Bass("TRN2", target_bir_lowering=False)

    x_d = nc.dram_tensor("x", [n_own, D], F32, kind="ExternalInput")
    xpre_d = nc.dram_tensor("xpre", [n_pre, D], F32, kind="ExternalInput")
    wqkt_d = nc.dram_tensor("wqkt", [4 * KT, 128, KT * 128], F32, kind="ExternalInput")
    wkvp_d = nc.dram_tensor("wkvp", [4, 128, KT * 512], F32, kind="ExternalInput")
    wout_d = nc.dram_tensor("wout", [128, KT * D], F32, kind="ExternalInput")
    rwt_d = nc.dram_tensor("rwt", [128, KT * 8], F32, kind="ExternalInput")
    bbc_d = nc.dram_tensor("bbc", [128, 8], F32, kind="ExternalInput")
    gw_d = nc.dram_tensor("gw", [E * NFB, 128, KT * 512], F32, kind="ExternalInput")
    uw_d = nc.dram_tensor("uw", [E * NFB, 128, KT * 512], F32, kind="ExternalInput")
    dw_d = nc.dram_tensor("dw", [E * NFB, 128, 4 * D], F32, kind="ExternalInput")
    posb_d = nc.dram_tensor("posb", [128, n_own], F32, kind="ExternalInput")
    posp_d = nc.dram_tensor("posp", [128, NPT], F32, kind="ExternalInput")
    jcol_d = nc.dram_tensor("jcol", [128, 1], F32, kind="ExternalInput")
    j64_d = nc.dram_tensor("j64", [128, 32], F32, kind="ExternalInput")
    sgn_d = nc.dram_tensor("sgn", [128, 1], F32, kind="ExternalInput")
    prem_d = nc.dram_tensor("prem", [128, NPT], F32, kind="ExternalInput")
    ident_d = nc.dram_tensor("ident", [128, 128], F32, kind="ExternalInput")
    cmask_d = nc.dram_tensor("cmask", [128, 128], F32, kind="ExternalInput")
    ccol_d = nc.dram_tensor("ccol", [128, 2], F32, kind="ExternalInput")

    out_d = nc.dram_tensor("out", [n_own, D], F32, kind="ExternalOutput")
    dbg_d = {}
    if dbg:
        for nm, sh in [("d_xn1", [n_own, D]), ("d_qf", [128, KT * n_own]),
                       ("d_kf", [128, KT * n_own]), ("d_vt", [n_own, D]),
                       ("d_st", [128, 8 * 65]), ("d_x1", [n_own, D]),
                       ("d_xn2", [n_own, D]), ("d_w8", [128, NT * 8])]:
            dbg_d[nm] = nc.dram_tensor(nm, sh, F32, kind="ExternalOutput")

    counts = {e: 0 for e in SEM_NAMES}
    counts["Y_rr"] = 0
    counts["G_rr"] = 0
    from contextlib import ExitStack

    with ExitStack() as top:
        sems = {e: top.enter_context(nc.semaphore("sem_" + e)) for e in SEM_NAMES}
        sb = lambda n, s, d=F32: top.enter_context(nc.sbuf_tensor(n, s, d))

        x_res = sb("x_res", [128, NT * D])
        st_f = sb("st_f", [128, 8 * 65])
        st_b = sb("st_b", [128, 8 * 65], BF16)
        invf_c = sb("invf_c", [128, 1])
        invf64 = sb("invf64", [128, 32])
        jcol = sb("jcol_s", [128, 1])
        j64 = sb("j64_s", [128, 32])
        sgn = sb("sgn_s", [128, 1])
        ccol = sb("ccol_s", [128, 2])
        ident_b = sb("ident_b", [128, 128], BF16)
        cmask_b = sb("cmask_b", [128, 128], BF16)
        ones_b = sb("ones_b", [128, 1], BF16)
        posp_s = sb("posp_s", [128, NPT])
        prem_s = sb("prem_s", [128, NPT])
        bbc_s = sb("bbc_s", [128, 8])
        w8_s = sb("w8_s", [128, NT * 8])

        def run_block(p):
            with nc.Block() as blk:
                @blk.tensor
                def _(e):
                    p.replay("T", e, sems)

                @blk.vector
                def _(e):
                    p.replay("V", e, sems)

                @blk.scalar
                def _(e):
                    p.replay("S", e, sems)

                @blk.gpsimd
                def _(e):
                    p.replay("G", e, sems)

                @blk.sync
                def _(e):
                    p.replay("Y", e, sems)

        def emit_ln(p, src_ap_fn, dst_ap_fn, lnc, lcol, src_after=(), dst_after=(),
                    prev=(None, None, None), lnc_prev=None):
            a0 = p.op("V", lambda e: e.reduce_sum(lcol[:, 0:1], src_ap_fn(), axis=AX.X),
                      after=list(src_after) + [prev[0]])
            a2 = p.op("V", lambda e: e.tensor_tensor(lnc[:, :], src_ap_fn(), src_ap_fn(), ALU.mult),
                      after=list(src_after) + [prev[1], lnc_prev])
            a3 = p.op("V", lambda e: e.reduce_sum(lcol[:, 2:3], lnc[:, :], axis=AX.X),
                      after=[a2, prev[2]])
            a1 = p.op("V", lambda e: e.tensor_scalar(lcol[:, 1:2], lcol[:, 0:1], 1.0 / D, None, ALU.mult), after=[a0])
            a4 = p.op("V", lambda e: e.tensor_tensor(lcol[:, 3:4], lcol[:, 1:2], lcol[:, 1:2], ALU.mult), after=[a1])
            a5 = p.op("V", lambda e: e.tensor_scalar(lcol[:, 2:3], lcol[:, 2:3], 1.0 / D, None, ALU.mult), after=[a3])
            a6 = p.op("V", lambda e: e.tensor_tensor(lcol[:, 2:3], lcol[:, 2:3], lcol[:, 3:4], ALU.subtract), after=[a5, a4])
            a7 = p.op("S", lambda e: e.activation(lcol[:, 2:3], lcol[:, 2:3], AF.Ln, bias=ccol[:, 0:1]), after=[a6])
            a8 = p.op("S", lambda e: e.activation(lcol[:, 3:4], lcol[:, 2:3], AF.Exp, scale=-0.5), after=[a7])
            a9 = p.op("V", lambda e: e.tensor_scalar(dst_ap_fn(), src_ap_fn(), lcol[:, 1:2], lcol[:, 3:4], ALU.subtract, ALU.mult),
                      after=[a8] + list(dst_after))
            return (a9, a3), (a1, a3, a7)

        def emit_reduce(p, th, thp, kfb, kib, w, tok_in):
            a = p.op("V", lambda e: e.tensor_scalar(thp[:, :w], th[:, :w], PI, None, ALU.add), after=[tok_in])
            b = p.op("V", lambda e: e.tensor_scalar(kfb[:, :w], thp[:, :w], INV_2PI, None, ALU.mult), after=[a])
            c = p.op("V", lambda e: e.tensor_copy(kib[:, :w], kfb[:, :w]), after=[b])
            d = p.op("V", lambda e: e.tensor_copy(kfb[:, :w], kib[:, :w]), after=[c])
            ng = p.op("V", lambda e: e.tensor_scalar(kfb[:, :w], kfb[:, :w], -1.0, None, ALU.mult), after=[d])
            r = ng
            for cwc in (CW1, CW2, CW3):
                r = p.op("V", lambda e, cc=cwc: e.scalar_tensor_tensor(thp[:, :w], kfb[:, :w], cc, thp[:, :w], ALU.mult, ALU.add), after=[r])
            r = p.op("V", lambda e: e.tensor_scalar(thp[:, :w], thp[:, :w], -PI, None, ALU.add), after=[r])
            g = p.op("V", lambda e: e.tensor_scalar(kfb[:, :w], thp[:, :w], PI, None, ALU.is_gt), after=[r])
            r = p.op("V", lambda e: e.scalar_tensor_tensor(thp[:, :w], kfb[:, :w], -2.0 * PI, thp[:, :w], ALU.mult, ALU.add), after=[g])
            l = p.op("V", lambda e: e.tensor_scalar(kfb[:, :w], thp[:, :w], -PI, None, ALU.is_lt), after=[r])
            r = p.op("V", lambda e: e.scalar_tensor_tensor(thp[:, :w], kfb[:, :w], 2.0 * PI, thp[:, :w], ALU.mult, ALU.add), after=[l])
            r = p.op("V", lambda e: e.tensor_scalar(thp[:, :w], thp[:, :w], PI, -PI, ALU.min, ALU.max), after=[r])
            return r

        p = Prog(counts)
        p.op("Y", lambda e: e.dma_start(
            out=r3s(x_res[:, :], NT), in_=r3d(x_d[:, :])), dma=True)
        for dst, src in [(posp_s, posp_d), (prem_s, prem_d),
                         (jcol, jcol_d), (j64, j64_d), (sgn, sgn_d),
                         (ccol, ccol_d), (bbc_s, bbc_d)]:
            p.op("Y", lambda e, d=dst, s=src: e.dma_start(out=d[:, :], in_=s[:, :]), dma=True)
        p.op("G", lambda e: e.dma_start(out=ident_b[:, :], in_=ident_d[:, :]), dma=True)
        p.op("G", lambda e: e.dma_start(out=cmask_b[:, :], in_=cmask_d[:, :]), dma=True)
        p.op("V", lambda e: e.memset(ones_b[:, :], 1.0))
        p.op("V", lambda e: e.memset(st_f[:, :], 0.0))
        run_block(p)

        p = Prog(counts)
        t0 = p.op("V", lambda e: e.tensor_scalar(invf_c[:, :], jcol[:, :], S_HI, None, ALU.mult))
        t1 = p.op("V", lambda e: e.scalar_tensor_tensor(invf_c[:, :], jcol[:, :], S_LO, invf_c[:, :], ALU.mult, ALU.add), after=[t0])
        p.op("S", lambda e: e.activation(invf_c[:, :], invf_c[:, :], AF.Exp), after=[t1])
        t0 = p.op("V", lambda e: e.tensor_scalar(invf64[:, :], j64[:, :], S_HI, None, ALU.mult))
        t1 = p.op("V", lambda e: e.scalar_tensor_tensor(invf64[:, :], j64[:, :], S_LO, invf64[:, :], ALU.mult, ALU.add), after=[t0])
        p.op("S", lambda e: e.activation(invf64[:, :], invf64[:, :], AF.Exp), after=[t1])
        run_block(p)

        cosmM = sb("cosmM", [128, n_own], BF16)
        sinmM = sb("sinmM", [128, n_own], BF16)
        tabstk = ExitStack()
        coswA = tabstk.enter_context(nc.sbuf_tensor("coswA", [128, NPT * 64], BF16))
        sinwA = tabstk.enter_context(nc.sbuf_tensor("sinwA", [128, NPT * 64], BF16))
        with ExitStack() as ph:
            psb = lambda n, s, d=F32: ph.enter_context(nc.sbuf_tensor(n, s, d))
            WT = NPT * 32 + n_own
            tha = psb("tha", [128, WT])
            thb = psb("thb", [128, WT])
            thk = psb("thk", [128, WT])
            tki = ph.enter_context(nc.sbuf_tensor("tkiT", [128, WT], I32))
            posbf = psb("posbf", [128, n_own])
            p = Prog(counts)
            pb = p.op("Y", lambda e: e.dma_start(out=posbf[:, :], in_=posb_d[:, :]), dma=True)
            ppre = bass.AP(posp_s, 0, [[NPT, 128], [1, NPT], [0, 32]])
            ifre = bass.AP(invf64, 0, [[32, 128], [0, NPT], [1, 32]])
            tA = p.op("V", lambda e: e.tensor_tensor(
                bass.AP(tha, 0, [[WT, 128], [32, NPT], [1, 32]]), ppre, ifre, ALU.mult))
            tB = p.op("V", lambda e: e.tensor_scalar(
                tha[:, NPT * 32: WT], posbf[:, :], invf_c[:, 0:1], None, ALU.mult), after=[pb])
            red = emit_reduce(p, tha, thb, thk, tki, WT, tB)
            sn1 = p.op("S", lambda e: e.activation(tha[:, :], thb[:, :], AF.Sin), after=[red])
            c0 = p.op("V", lambda e: e.tensor_scalar(thk[:, :], thb[:, :], PI / 2.0, None, ALU.add), after=[red])
            c1 = p.op("V", lambda e: e.tensor_scalar(thb[:, :], thk[:, :], PI, None, ALU.is_gt), after=[c0, sn1])
            c2 = p.op("V", lambda e: e.scalar_tensor_tensor(thk[:, :], thb[:, :], -2.0 * PI, thk[:, :], ALU.mult, ALU.add), after=[c1])
            c3 = p.op("V", lambda e: e.tensor_scalar(thk[:, :], thk[:, :], PI, -PI, ALU.min, ALU.max), after=[c2])
            sn2 = p.op("S", lambda e: e.activation(thk[:, :], thk[:, :], AF.Sin), after=[c3])
            f0t = p.op("V", lambda e: e.tensor_scalar(
                bass.AP(sinwA, 0, [[NPT * 64, 128], [64, NPT], [1, 32]]),
                bass.AP(tha, 0, [[WT, 128], [32, NPT], [1, 32]]), -SCALE, None, ALU.mult), after=[sn1])
            f1t = p.op("V", lambda e: e.tensor_scalar(
                bass.AP(sinwA, 32, [[NPT * 64, 128], [64, NPT], [1, 32]]),
                bass.AP(tha, 0, [[WT, 128], [32, NPT], [1, 32]]), SCALE, None, ALU.mult), after=[sn1])
            f2t = p.op("V", lambda e: e.tensor_scalar(
                bass.AP(coswA, 0, [[NPT * 64, 128], [64, NPT], [1, 32]]),
                bass.AP(thk, 0, [[WT, 128], [32, NPT], [1, 32]]), SCALE, None, ALU.mult), after=[sn2])
            f3t = p.op("V", lambda e: e.tensor_scalar(
                bass.AP(coswA, 32, [[NPT * 64, 128], [64, NPT], [1, 32]]),
                bass.AP(thk, 0, [[WT, 128], [32, NPT], [1, 32]]), SCALE, None, ALU.mult), after=[sn2])
            fm0 = p.op("V", lambda e: e.tensor_scalar(sinmM[:, :], tha[:, NPT * 32: WT], sgn[:, 0:1], None, ALU.mult), after=[sn1])
            fm1 = p.op("V", lambda e: e.tensor_scalar(cosmM[:, :], thk[:, NPT * 32: WT], SCALE, None, ALU.mult), after=[sn2])
            run_block(p)

        with ExitStack() as ph:
            psb = lambda n, s, d=F32: ph.enter_context(nc.sbuf_tensor(n, s, d))
            xps = [psb("xps0", [128, 4 * D], BF16)] * 2
            xnp_t = [psb(f"xnp_t{i}", [128, 4 * D], BF16) for i in range(2)]
            xnp_f = [psb(f"xnp_f{j}", [128, KT * 512], BF16) for j in range(2)]
            kp_t = psb("kp_t", [128, 4 * D], BF16)
            vp_t = psb("vp_t", [128, 4 * D], BF16)
            wk_b = [psb(f"wk_b{i}", [128, KT * 512], BF16) for i in range(2)]
            wv_b = [psb(f"wv_b{i}", [128, KT * 512], BF16) for i in range(2)]
            lnc = [psb("lnc", [128, D])] * 2
            lcol = [psb(f"lcol{j}", [128, 4]) for j in range(2)]
            rsc1 = [psb(f"rsc1{j}", [128, D], BF16) for j in range(2)]
            rsc2 = [psb(f"rsc2{j}", [128, D], BF16) for j in range(2)]
            psK = [ph.enter_context(nc.psum_tensor(f"psK{i}", [128, 512], F32)) for i in range(2)]
            psV = [ph.enter_context(nc.psum_tensor(f"psV{i}", [128, 512], F32)) for i in range(2)]
            psT = [ph.enter_context(nc.psum_tensor(f"psT{i}", [128, 1024], BF16)) for i in range(2)]
            psS = [ph.enter_context(nc.psum_tensor(f"psS{i}", [128, 512], F32)) for i in range(2)]

            p = Prog(counts)
            wk_tok, wv_tok = [], []
            for hf in range(2):
                wk_tok.append(p.op("G", lambda e, d=wk_b[hf], hf=hf: e.dma_start(
                    out=d[:, :], in_=wkvp_d[hf, :, :]), dma=True))
                wv_tok.append(p.op("G", lambda e, d=wv_b[hf], hf=hf: e.dma_start(
                    out=d[:, :], in_=wkvp_d[2 + hf, :, :]), dma=True))

            ld_tok = [None, None]
            lnc_last = None
            ln_prev = [(None, None, None), (None, None, None)]
            stage_free = [None, None]
            xnpt_free = [None, None]
            xnpf_free = [None, None]
            psK_free = [None, None]
            psV_free = [None, None]
            psT_free = [None, None]
            kp_free = None
            vp_free = None
            rprev = [[None, None, None], [None, None, None]]
            s_chain = None
            first_s = [True, True]

            for ptb in range(NPTB):
                par = ptb % 2
                ld_tok[par] = p.op("G", lambda e, b=xps[par], i0=ptb * 512:
                                   e.dma_start(out=r3s(b[:, :], 4), in_=r3d(xpre_d[i0:i0 + 512, :])),
                                   after=[stage_free[0], stage_free[1]], dma=True)
                ln_done = []
                for i in range(4):
                    (tok, lnc_last), ln_prev[i % 2] = emit_ln(
                        p,
                        lambda b=xps[par], i=i: b[:, i * D:(i + 1) * D],
                        lambda i=i, par=par: xnp_t[par][:, i * D:(i + 1) * D],
                        lnc[i % 2], lcol[i % 2],
                        src_after=[ld_tok[par]], dst_after=[xnpt_free[par]],
                        prev=ln_prev[i % 2], lnc_prev=lnc_last)
                    ln_done.append(tok)
                stage_free[par] = ln_done[-1]

                tr_last = None
                for i in range(4):
                    for k in range(KT):
                        tp_ = (i * KT + k) % 2
                        tm = p.op("T", lambda e, i=i, k=k, tp_=tp_, par=par: e.transpose(
                            psT[tp_][:, 0:128], xnp_t[par][:, i * D + k * 128: i * D + (k + 1) * 128], ident_b[:, :]),
                            after=[ln_done[i], psT_free[tp_], xnpf_free[par]])
                        psT_free[tp_] = p.op("S" if tp_ else "V",
                            lambda e, i=i, k=k, tp_=tp_, par=par: (e.copy if tp_ else e.tensor_copy)(
                            xnp_f[par][:, k * 512 + i * 128: k * 512 + (i + 1) * 128], psT[tp_][:, 0:128]), after=[tm])
                        tr_last = psT_free[tp_]
                xnpt_free[par] = tr_last
                xnpf_ready_l = [psT_free[0], psT_free[1]]
                xnpf_ready = tr_last

                kp_ready = {}
                vp_ready = {}
                for i in range(4):
                    for kind in range(2):
                        for hf in range(2):
                            bank = (psK if kind == 0 else psV)[hf]
                            wb = (wk_b if kind == 0 else wv_b)[hf]
                            wt = (wk_tok if kind == 0 else wv_tok)[hf]
                            free = (psK_free if kind == 0 else psV_free)[hf]

                            def mmkv(e, i=i, b=bank, w=wb, par=par):
                                last = None
                                for k in range(KT):
                                    last = e.matmul(b[:, :],
                                                    xnp_f[par][:, k * 512 + i * 128: k * 512 + (i + 1) * 128],
                                                    w[:, k * 512:(k + 1) * 512],
                                                    start=(k == 0), stop=(k == KT - 1))
                                return last
                            mt = p.op("T", mmkv, after=xnpf_ready_l + [wt, free])
                            dst = kp_t if kind == 0 else vp_t
                            eng_ev = "V" if kind == 0 else "S"
                            ev = p.op(eng_ev, lambda e, d=dst, i=i, hf=hf, b=bank, k0=(kind == 0):
                                      (e.tensor_copy if k0 else e.copy)(
                                      d[:, i * D + hf * 512: i * D + hf * 512 + 512], b[:, :]),
                                      after=[mt, kp_free if kind == 0 else vp_free])
                            if kind == 0:
                                psK_free[hf] = ev
                            else:
                                psV_free[hf] = ev
                            if hf == 1:
                                (kp_ready if kind == 0 else vp_ready)[i] = ev
                xnpf_free[par] = mt

                for i in range(4):
                    ti = ptb * 4 + i
                    pj = i % 2
                    s1b, s2b = rsc1[pj], rsc2[pj]
                    cosb = bass.AP(coswA, ti * 64, [[NPT * 64, 128], [0, 16], [1, 64]])
                    sinb = bass.AP(sinwA, ti * 64, [[NPT * 64, 128], [0, 16], [1, 64]])
                    r0 = p.op("V", lambda e, i=i, cb=cosb, s1b=s1b: e.tensor_tensor(
                        s1b[:, :], kp_t[:, i * D:(i + 1) * D], cb, ALU.mult),
                        after=[kp_ready[i], rprev[pj][0]])
                    r1 = p.op("V", lambda e, i=i, sbp=sinb, s2b=s2b: e.tensor_tensor(
                        s2b[:, :], bass.AP(kp_t, i * D + 32, [[4 * D, 128], [64, 16], [-32, 2], [1, 32]]),
                        sbp, ALU.mult), after=[kp_ready[i], rprev[pj][1], rprev[pj][2]])
                    r2 = p.op("V", lambda e, s1b=s1b, s2b=s2b: e.tensor_tensor(s1b[:, :], s1b[:, :], s2b[:, :], ALU.add), after=[r0, r1])
                    r3 = p.op("V", lambda e, s1b=s1b, s2b=s2b: e.tensor_scalar(s2b[:, :], s1b[:, :], 0.0, None, ALU.min), after=[r2])
                    r4 = p.op("S", lambda e, s2b=s2b: e.activation(s2b[:, :], s2b[:, :], AF.Exp), after=[r3])
                    r5 = p.op("V", lambda e, s1b=s1b: e.tensor_scalar(s1b[:, :], s1b[:, :], 0.0, None, ALU.max), after=[r2, r3])
                    r6 = p.op("V", lambda e, s1b=s1b, s2b=s2b: e.tensor_tensor(s1b[:, :], s1b[:, :], s2b[:, :], ALU.add), after=[r5, r4])
                    r7 = p.op("V", lambda e, i=i, ptb=ptb, s1b=s1b: e.tensor_scalar(
                        kp_t[:, i * D:(i + 1) * D], s1b[:, :],
                        prem_s[:, ptb * 4 + i: ptb * 4 + i + 1], None, ALU.mult), after=[r6, kp_free])
                    kp_ready[i] = r7
                    rprev[pj] = [r7, r6, r4]

                for i in range(4):
                    def smm(e, i=i, first=first_s[0], last_acc=(ptb == NPTB - 1 and i == 3)):
                        last = None
                        for h in range(HEADS):
                            bank = psS[h // 8]
                            rb = 64 * (h % 2)
                            cb = ((h % 8) // 2) * 64
                            st = first and (h % 8 in (0, 1))
                            sp = last_acc and (h % 8 in (6, 7))
                            last = e.matmul(bank[rb:rb + 64, cb:cb + 64],
                                            kp_t[:, i * D + h * HD: i * D + (h + 1) * HD],
                                            vp_t[:, i * D + h * HD: i * D + (h + 1) * HD],
                                            start=st, stop=sp, tile_position=(0, rb),
                                            skip_group_check=True)
                            last = e.matmul(psS[h // 8][rb:rb + 64, 256 + (h % 8): 257 + (h % 8)],
                                            kp_t[:, i * D + h * HD: i * D + (h + 1) * HD],
                                            ones_b[:, :],
                                            start=False, stop=False,
                                            tile_position=(0, rb), skip_group_check=True)
                        return last
                    s_chain = p.op("T", smm, after=[kp_ready[i], vp_ready[i], s_chain])
                    first_s = [False, False]
                kp_free = s_chain
                vp_free = s_chain

            evs = []
            for h in range(HEADS):
                rb, scol = stpos(h)
                cb = ((h % 8) // 2) * 64
                evs.append(p.op("V", lambda e, h=h, rb=rb, scol=scol, cb=cb: e.tensor_copy(
                    st_f[rb:rb + 64, scol:scol + 64], psS[h // 8][rb:rb + 64, cb:cb + 64]), after=[s_chain]))
                evs.append(p.op("V", lambda e, h=h, rb=rb, scol=scol: e.tensor_copy(
                    st_f[rb:rb + 64, scol + 64:scol + 65],
                    psS[h // 8][rb:rb + 64, 256 + (h % 8): 257 + (h % 8)]), after=[s_chain]))
            ev2 = p.op("V", lambda e: e.tensor_copy(st_b[:, :], st_f[:, :]), after=evs)
            if dbg:
                p.op("Y", lambda e: e.dma_start(out=dbg_d["d_st"][:, :], in_=st_f[:, :]), after=[ev2], dma=True)
            run_block(p)
        tabstk.close()

        mid = ExitStack()
        qf = mid.enter_context(nc.sbuf_tensor("qf", [128, KT * n_own], BF16))
        kf = mid.enter_context(nc.sbuf_tensor("kf", [128, KT * n_own], BF16))
        vt = mid.enter_context(nc.sbuf_tensor("vt", [128, NT * D], BF16))
        with ExitStack() as ph:
            psb = lambda n, s, d=F32: ph.enter_context(nc.sbuf_tensor(n, s, d))
            xn_t = psb("xn_t", [128, D], BF16)
            xn_f = psb("xn_f", [128, KT * 512], BF16)
            lnc = [psb("lnc2", [128, D])] * 2
            lcol = [psb(f"lcol2{j}", [128, 4]) for j in range(2)]
            ra = [psb(f"ra{j}", [128, 512]) for j in range(2)]
            rb_ = [psb(f"rb{j}", [128, 512]) for j in range(2)]
            wqk = [psb(f"wqk{i}", [128, KT * 128], BF16) for i in range(2)]
            wvb = psb("wvb", [128, KT * 512], BF16)
            psQ = [ph.enter_context(nc.psum_tensor(f"psQ{i}", [128, 512], F32)) for i in range(4)]
            psV2 = [ph.enter_context(nc.psum_tensor(f"psV2{i}", [128, 512], F32)) for i in range(2)]
            psT2 = [ph.enter_context(nc.psum_tensor(f"psT2{i}", [128, 1024], BF16)) for i in range(2)]

            p = Prog(counts)
            xnt_free = []
            dbg_free = []
            lnc_last = None
            ln_prev = [(None, None, None), (None, None, None)]
            xnf_free = None
            psT2_free = [None, None]
            qprev = [[None, None], [None, None]]
            wqk_free = [None] * 2
            wvb_free = None
            psQ_free = [None] * 4
            psV2_free = [None] * 2
            last_write = []

            for tb in range(NTB):
                tr_last = None
                for i in range(4):
                    ti = tb * 4 + i
                    (tok, lnc_last), ln_prev[i % 2] = emit_ln(
                        p,
                        lambda ti=ti: x_res[:, ti * D:(ti + 1) * D],
                        lambda: xn_t[:, :],
                        lnc[i % 2], lcol[i % 2], dst_after=xnt_free,
                        prev=ln_prev[i % 2], lnc_prev=lnc_last)
                    if dbg:
                        dbg_free.append(p.op("G", lambda e, ti=ti: e.dma_start(
                            out=dbg_d["d_xn1"][ti * 128:(ti + 1) * 128, :],
                            in_=xn_t[:, :]), after=[tok], dma=True))
                    for k in range(KT):
                        tp_ = k % 2
                        tm = p.op("T", lambda e, i=i, k=k, tp_=tp_: e.transpose(
                            psT2[tp_][:, 0:128], xn_t[:, k * 128:(k + 1) * 128], ident_b[:, :]),
                            after=[tok, psT2_free[tp_], xnf_free])
                        psT2_free[tp_] = p.op("V", lambda e, i=i, k=k, tp_=tp_: e.tensor_copy(
                            xn_f[:, k * 512 + i * 128: k * 512 + (i + 1) * 128], psT2[tp_][:, 0:128]), after=[tm])
                        tr_last = psT2_free[tp_]
                    xnt_free = [tr_last, psT2_free[0], psT2_free[1]] + dbg_free
                    dbg_free = []
                xnf_ready_l = [psT2_free[0], psT2_free[1]]
                xnf_ready = tr_last

                for pair, dst in ((0, qf), (1, kf)):
                    for ft in range(KT):
                        bi = 0
                        bq, bp = psQ[(ft % 2) * 2], psQ[(ft % 2) * 2 + 1]
                        i_main = pair * 2 * KT + ft
                        i_perm = i_main + KT
                        wbi = 0
                        t_w0 = p.op("G", lambda e, w=wqk[wbi], ix=i_main: e.dma_start(
                            out=w[:, :], in_=wqkt_d[ix, :, :]),
                            after=[wqk_free[wbi]], dma=True)
                        t_w1 = p.op("G", lambda e, w=wqk[wbi + 1], ix=i_perm: e.dma_start(
                            out=w[:, :], in_=wqkt_d[ix, :, :]),
                            after=[wqk_free[wbi + 1]], dma=True)

                        bqi = (ft % 2) * 2

                        def mmq(e, w=wqk[wbi], b=bq):
                            last = None
                            for k in range(KT):
                                last = e.matmul(b[:, :], w[:, k * 128:(k + 1) * 128],
                                                xn_f[:, k * 512:(k + 1) * 512],
                                                start=(k == 0), stop=(k == KT - 1))
                            return last
                        m0 = p.op("T", mmq, after=xnf_ready_l + [t_w0, psQ_free[bqi]])

                        def mmp(e, w=wqk[wbi + 1], b=bp):
                            last = None
                            for k in range(KT):
                                last = e.matmul(b[:, :], w[:, k * 128:(k + 1) * 128],
                                                xn_f[:, k * 512:(k + 1) * 512],
                                                start=(k == 0), stop=(k == KT - 1))
                            return last
                        m1 = p.op("T", mmp, after=[t_w1, psQ_free[bqi + 1], m0])
                        wqk_free[wbi] = m0
                        wqk_free[wbi + 1] = m1
                        pj = ft % 2
                        rab, rbb = ra[pj], rb_[pj]
                        r0 = p.op("V", lambda e, b=bq, tb=tb, rab=rab: e.tensor_tensor(
                            rab[:, :], b[:, :], cosmM[:, tb * 512:(tb + 1) * 512], ALU.mult),
                            after=[m0, qprev[pj][0]])
                        r1 = p.op("V", lambda e, b=bp, tb=tb, rbb=rbb: e.tensor_tensor(
                            rbb[:, :], b[:, :], sinmM[:, tb * 512:(tb + 1) * 512], ALU.mult),
                            after=[m1, qprev[pj][1]])
                        r2 = p.op("V", lambda e, rab=rab, rbb=rbb: e.tensor_tensor(rab[:, :], rab[:, :], rbb[:, :], ALU.add), after=[r0, r1])
                        r3 = p.op("V", lambda e, rab=rab, rbb=rbb: e.tensor_scalar(rbb[:, :], rab[:, :], 0.0, None, ALU.min), after=[r2])
                        r4 = p.op("S", lambda e, rbb=rbb: e.activation(rbb[:, :], rbb[:, :], AF.Exp), after=[r3])
                        r5 = p.op("V", lambda e, rab=rab: e.tensor_scalar(rab[:, :], rab[:, :], 0.0, None, ALU.max), after=[r2, r3])
                        r6 = p.op("V", lambda e, d=dst, ft=ft, tb=tb, rab=rab, rbb=rbb: e.tensor_tensor(
                            d[:, ft * n_own + tb * 512: ft * n_own + (tb + 1) * 512],
                            rab[:, :], rbb[:, :], ALU.add), after=[r5, r4])
                        qprev[pj] = [r6, r4]
                        psQ_free[bqi] = r6
                        psQ_free[bqi + 1] = r6
                        last_write.append(r6)

                for hf in range(2):
                    t_wv = p.op("G", lambda e, hf=hf: e.dma_start(
                        out=wvb[:, :], in_=wkvp_d[2 + hf, :, :]), after=[wvb_free], dma=True)
                    for i in range(4):
                        ti = tb * 4 + i
                        bank = psV2[i % 2]

                        def mmv(e, i=i, b=bank):
                            last = None
                            for k in range(KT):
                                last = e.matmul(b[:, :], xn_f[:, k * 512 + i * 128: k * 512 + (i + 1) * 128],
                                                wvb[:, k * 512:(k + 1) * 512],
                                                start=(k == 0), stop=(k == KT - 1))
                            return last
                        mv = p.op("T", mmv, after=xnf_ready_l + [t_wv, psV2_free[i % 2]])
                        wvb_free = mv
                        ev = p.op("S", lambda e, ti=ti, hf=hf, b=bank: e.copy(
                            vt[:, ti * D + hf * 512: ti * D + hf * 512 + 512], b[:, :]), after=[mv])
                        psV2_free[i % 2] = ev
                        last_write.append(ev)
                xnf_free = wvb_free

            if dbg:
                lw = last_write[-20:]
                p.op("G", lambda e: e.dma_start(out=dbg_d["d_qf"][:, :], in_=qf[:, :]), after=lw, dma=True)
                p.op("G", lambda e: e.dma_start(out=dbg_d["d_kf"][:, :], in_=kf[:, :]), after=lw, dma=True)
                p.op("G", lambda e: e.dma_start(
                    out=r3d(dbg_d["d_vt"][:, :]), in_=r3s(vt[:, :], NT)), after=lw, dma=True)
            run_block(p)

        with ExitStack() as ph:
            psb = lambda n, s, d=F32: ph.enter_context(nc.sbuf_tensor(n, s, d))
            am = [psb(f"am{i}", [128, 128], BF16) for i in range(2)]
            kcs = [psb(f"kcs{i}", [128, 64], BF16) for i in range(2)]
            dcol = [psb(f"dcol{j}", [128, 2]) for j in range(2)]
            otc = psb("otc", [128, D], BF16)
            otf = psb("otf", [128, KT * 128], BF16)
            woutb = psb("woutb", [128, KT * D], BF16)
            pA = [ph.enter_context(nc.psum_tensor(f"pA{i}", [128, 512], F32)) for i in range(2)]
            pO = [ph.enter_context(nc.psum_tensor(f"pO{i}", [128, 512], F32)) for i in range(2)]
            pS2 = [ph.enter_context(nc.psum_tensor("pS2", [128, 512], F32))] * 2
            pB16 = ph.enter_context(nc.psum_tensor("pB16", [128, 1024], BF16))
            pP = [ph.enter_context(nc.psum_tensor(f"pP{i}", [128, 512], F32)) for i in range(2)]

            p = Prog(counts)
            t_wo = p.op("G", lambda e: e.dma_start(
                out=woutb[:, :], in_=wout_d[:, :]), dma=True)

            am_free = [None, None]
            dcol_free = [None, None]
            kcs_free = [None, None]
            pA_free = [None, None]
            pO_free = [None, None]
            pS2_free = [None, None]
            pB16_free = None
            pP_free = [None, None]
            st_rd = {h: None for h in range(HEADS)}
            st_wr = {h: None for h in range(HEADS)}
            otc_wr = [None] * HEADS
            otc_free = None
            otf_free = [None] * KT
            x1_toks = []

            def qsl(buf, h, c):
                return buf[64 * (h % 2):64 * (h % 2) + 64,
                           (h // 2) * n_own + c * 128:(h // 2) * n_own + (c + 1) * 128]

            for c in range(NCH):
                for h in range(HEADS):
                    par = h % 2
                    base, scol = stpos(h)
                    a0 = p.op("T", lambda e, h=h, c=c, par=par: e.matmul(
                        pA[par][:, 0:128], qsl(kf, h, c), qsl(qf, h, c), start=True, stop=True),
                        after=[pA_free[par]])
                    a1 = p.op("V", lambda e, par=par: e.tensor_tensor(
                        am[par][:, :], pA[par][:, 0:128], cmask_b[:, :], ALU.mult), after=[a0, am_free[par]])
                    pA_free[par] = a1
                    k0 = p.op("T", lambda e, h=h, c=c, base=base: e.transpose(
                        pB16[:, 0:64], qsl(kf, h, c), ident_b[base:base + 64, base:base + 64]),
                        after=[pB16_free])
                    k1 = p.op("S", lambda e, par=par: e.copy(kcs[par][:, :], pB16[:, 0:64]),
                              after=[k0, kcs_free[par]])
                    pB16_free = k1

                    def mmo(e, h=h, c=c, par=par, base=base, scol=scol):
                        ob = pO[par]
                        e.matmul(ob[:, 0:64], am[par][:, :], vt[:, c * D + h * HD: c * D + (h + 1) * HD],
                                 start=True, stop=False)
                        e.matmul(ob[:, 0:64], qsl(qf, h, c), st_b[base:base + 64, scol:scol + 64],
                                 start=False, stop=False)
                        e.matmul(ob[:, 64:65], am[par][:, :], ones_b[:, :], start=False, stop=False)
                        return e.matmul(ob[:, 64:65], qsl(qf, h, c),
                                        st_b[base:base + 64, scol + 64:scol + 65], start=False, stop=True)
                    o0 = p.op("T", mmo, after=[a1, pO_free[par], st_wr[h]])
                    st_rd[h] = o0
                    am_free[par] = o0
                    d0 = p.op("V", lambda e, par=par: e.tensor_scalar(
                        dcol[par][:, 0:1], pO[par][:, 64:65], 1e-6, None, ALU.max), after=[o0])
                    d1 = p.op("V", lambda e, par=par: e.reciprocal(dcol[par][:, 1:2], dcol[par][:, 0:1]),
                              after=[d0, dcol_free[par]])
                    d2 = p.op("S", lambda e, h=h, par=par: e.activation(
                        otc[:, h * HD:(h + 1) * HD], pO[par][:, 0:64], AF.Copy, scale=dcol[par][:, 1:2]),
                        after=[d1, otc_free])
                    dcol_free[par] = d2
                    pO_free[par] = d2
                    otc_wr[h] = d2

                    def mms(e, h=h, c=c, base=base, par=par):
                        e.matmul(pS2[par][base:base + 64, 0:64], kcs[par][:, :],
                                 vt[:, c * D + h * HD: c * D + (h + 1) * HD],
                                 start=True, stop=False, tile_position=(0, base))
                        return e.matmul(pS2[par][base:base + 64, 64:65], kcs[par][:, :], ones_b[:, :],
                                        start=False, stop=True, tile_position=(0, base))
                    s0 = p.op("T", mms, after=[k1, pS2_free[0], pS2_free[1]])
                    kcs_free[par] = s0
                    s1 = p.op("V", lambda e, base=base, scol=scol, par=par: e.tensor_tensor(
                        st_f[base:base + 64, scol:scol + 65], st_f[base:base + 64, scol:scol + 65],
                        pS2[par][base:base + 64, 0:65], ALU.add), after=[s0, st_rd[h]])
                    s2 = p.op("S", lambda e, base=base, scol=scol: e.copy(
                        st_b[base:base + 64, scol:scol + 65], st_f[base:base + 64, scol:scol + 65]), after=[s1])
                    pS2_free[par] = s2
                    st_wr[h] = s2

                trs = []
                for k in range(KT):
                    tm = p.op("T", lambda e, k=k: e.transpose(
                        pB16[:, 0:128], otc[:, k * 128:(k + 1) * 128], ident_b[:, :]),
                        after=[pB16_free, otc_wr[2 * k], otc_wr[2 * k + 1], otf_free[k]])
                    pB16_free = p.op("V", lambda e, k=k: e.tensor_copy(
                        otf[:, k * 128:(k + 1) * 128], pB16[:, 0:128]), after=[tm])
                    trs.append(pB16_free)
                otc_free = trs[-1]

                def mmp1(e):
                    last = None
                    for k in range(KT):
                        last = e.matmul(pP[0][:, :], otf[:, k * 128:(k + 1) * 128],
                                        woutb[:, k * D: k * D + 512],
                                        start=(k == 0), stop=(k == KT - 1))
                    return last
                pm0 = p.op("T", mmp1, after=trs + [t_wo, pP_free[0]])

                def mmp2(e):
                    last = None
                    for k in range(KT):
                        last = e.matmul(pP[1][:, :], otf[:, k * 128:(k + 1) * 128],
                                        woutb[:, k * D + 512: (k + 1) * D],
                                        start=(k == 0), stop=(k == KT - 1))
                    return last
                pm1 = p.op("T", mmp2, after=[pm0, pP_free[1]])
                for k in range(KT):
                    otf_free[k] = pm1
                e0 = p.op("V", lambda e, c=c: e.tensor_tensor(
                    x_res[:, c * D: c * D + 512], x_res[:, c * D: c * D + 512], pP[0][:, :], ALU.add), after=[pm0])
                e1 = p.op("V", lambda e, c=c: e.tensor_tensor(
                    x_res[:, c * D + 512: (c + 1) * D], x_res[:, c * D + 512: (c + 1) * D], pP[1][:, :], ALU.add), after=[pm1])
                pP_free[0] = e0
                pP_free[1] = e1
                x1_toks = [e0, e1]

            if dbg:
                p.op("Y", lambda e: e.dma_start(
                    out=r3d(dbg_d["d_x1"][:, :]), in_=r3s(x_res[:, :], NT)), after=x1_toks, dma=True)
            run_block(p)
        mid.close()

        xn2f = sb("xn2f", [128, KT * n_own], BF16)
        with ExitStack() as ph:
            psb = lambda n, s, d=F32: ph.enter_context(nc.sbuf_tensor(n, s, d))
            xn_t = [psb(f"xn_t2{j}", [128, D], BF16) for j in range(2)]
            lnc = [psb(f"lnc3{j}", [128, D]) for j in range(2)]
            lcol = [psb(f"lcol3{j}", [128, 4]) for j in range(2)]
            rwtb = psb("rwtb", [128, KT * 8], BF16)
            lgall = psb("lgall", [128, NT * 8])
            l2all = psb("l2all", [128, NT * 8])
            m1t = psb("m1t", [128, NT])
            m2t = psb("m2t", [128, NT])
            g1t = psb("g1t", [128, NT])
            g2t = psb("g2t", [128, NT])
            pR = ph.enter_context(nc.psum_tensor("pR", [128, 512], F32))
            pT4 = [ph.enter_context(nc.psum_tensor(f"pT4{i}", [128, 1024], BF16)) for i in range(2)]

            p = Prog(counts)
            t_rw = p.op("G", lambda e: e.dma_start(
                out=rwtb[:, :], in_=rwt_d[:, :]), dma=True)
            pT4_free = [None, None]
            pR_free = None
            lg_toks = []
            xnt_free2 = [[], []]
            dbg_free = []
            w8_last = None
            ln_prev = [(None, None, None), (None, None, None)]
            lnc_lastL = [None, None]
            for ti in range(NT):
                pj = ti % 2
                (a7, lnc_lastL[pj]), ln_prev[pj] = emit_ln(
                    p,
                    lambda ti=ti: x_res[:, ti * D:(ti + 1) * D],
                    lambda pj=pj: xn_t[pj][:, :],
                    lnc[pj], lcol[pj], dst_after=xnt_free2[pj],
                    prev=ln_prev[pj], lnc_prev=lnc_lastL[pj])
                if dbg:
                    dbg_free.append(p.op("G", lambda e, ti=ti, pj=pj: e.dma_start(
                        out=dbg_d["d_xn2"][ti * 128:(ti + 1) * 128, :],
                        in_=xn_t[pj][:, :]), after=[a7], dma=True))
                trs = []
                for k in range(KT):
                    tp_ = k % 2
                    tm = p.op("T", lambda e, k=k, tp_=tp_, pj=pj: e.transpose(
                        pT4[tp_][:, 0:128], xn_t[pj][:, k * 128:(k + 1) * 128], ident_b[:, :]),
                        after=[a7, pT4_free[tp_]])
                    pT4_free[tp_] = p.op("S" if tp_ else "V",
                        lambda e, k=k, ti=ti, tp_=tp_: (e.copy if tp_ else e.tensor_copy)(
                        xn2f[:, k * n_own + ti * 128: k * n_own + (ti + 1) * 128], pT4[tp_][:, 0:128]), after=[tm])
                    trs.append(pT4_free[tp_])
                xnt_free2[pj] = [trs[-1], trs[-2]] + dbg_free
                dbg_free = []

                def mmr(e, ti=ti):
                    last = None
                    for k in range(KT):
                        last = e.matmul(pR[:, 0:8], xn2f[:, k * n_own + ti * 128: k * n_own + (ti + 1) * 128],
                                        rwtb[:, k * 8:(k + 1) * 8], start=(k == 0), stop=(k == KT - 1))
                    return last
                r0 = p.op("T", mmr, after=trs + [t_rw, pR_free])
                pR_free = p.op("V", lambda e, ti=ti: e.tensor_copy(
                    lgall[:, ti * 8:(ti + 1) * 8], pR[:, 0:8]), after=[r0])
                lg_toks.append(pR_free)

            bbcb = bass.AP(bbc_s, 0, [[8, 128], [0, NT], [1, 8]])
            b0 = p.op("V", lambda e: e.tensor_tensor(lgall[:, :], lgall[:, :], bbcb, ALU.add), after=lg_toks)
            b1 = p.op("V", lambda e: e.tensor_reduce(
                m1t[:, :], r3s(lgall[:, :], NT), axis=AX.X, op=ALU.max), after=[b0])
            m1b = bass.AP(m1t, 0, [[NT, 128], [1, NT], [0, 8]])
            b2 = p.op("V", lambda e: e.tensor_tensor(l2all[:, :], lgall[:, :], m1b, ALU.is_equal), after=[b1])
            b3 = p.op("V", lambda e: e.scalar_tensor_tensor(l2all[:, :], l2all[:, :], -1e30, lgall[:, :], ALU.mult, ALU.add), after=[b2])
            b4 = p.op("V", lambda e: e.tensor_reduce(
                m2t[:, :], r3s(l2all[:, :], NT), axis=AX.X, op=ALU.max), after=[b3])
            b5 = p.op("V", lambda e: e.tensor_tensor(g1t[:, :], m1t[:, :], m2t[:, :], ALU.subtract), after=[b4])
            b6 = p.op("S", lambda e: e.activation(g2t[:, :], g1t[:, :], AF.Sigmoid, scale=-1.0), after=[b5])
            b7 = p.op("S", lambda e: e.activation(g1t[:, :], g1t[:, :], AF.Sigmoid), after=[b6])
            m2b = bass.AP(m2t, 0, [[NT, 128], [1, NT], [0, 8]])
            b8 = p.op("V", lambda e: e.tensor_tensor(lgall[:, :], lgall[:, :], m1b, ALU.is_equal), after=[b3, b1])
            b9 = p.op("V", lambda e: e.tensor_tensor(l2all[:, :], l2all[:, :], m2b, ALU.is_equal), after=[b4])
            g1b = bass.AP(g1t, 0, [[NT, 128], [1, NT], [0, 8]])
            g2b = bass.AP(g2t, 0, [[NT, 128], [1, NT], [0, 8]])
            ba = p.op("V", lambda e: e.tensor_tensor(lgall[:, :], lgall[:, :], g1b, ALU.mult), after=[b8, b7])
            bb = p.op("V", lambda e: e.tensor_tensor(l2all[:, :], l2all[:, :], g2b, ALU.mult), after=[b9, b6])
            w8_last = p.op("V", lambda e: e.tensor_tensor(w8_s[:, :], lgall[:, :], l2all[:, :], ALU.add), after=[ba, bb])
            if dbg:
                p.op("Y", lambda e: e.dma_start(out=dbg_d["d_w8"][:, :], in_=w8_s[:, :]), after=[w8_last], dma=True)
            run_block(p)

        with ExitStack() as ph:
            psb = lambda n, s, d=F32: ph.enter_context(nc.sbuf_tensor(n, s, d))
            gb = [psb(f"gb{i}", [128, KT * 512], BF16) for i in range(2)]
            ub = [psb(f"ub{i}", [128, KT * 512], BF16) for i in range(2)]
            db_ = [psb(f"db{i}", [128, 4 * D], BF16) for i in range(2)]
            hs = psb("hs", [128, 4 * n_own], BF16)
            sil = [psb(f"sil{i}", [128, 512]) for i in range(3)]
            pG = [ph.enter_context(nc.psum_tensor(f"pG{i}", [128, 512], F32)) for i in range(3)]
            pU = [ph.enter_context(nc.psum_tensor(f"pU{i}", [128, 512], F32)) for i in range(3)]
            pD = [ph.enter_context(nc.psum_tensor(f"pD{i}", [128, 512], F32)) for i in range(2)]

            p = Prog(counts)
            wg_free = [None, None]
            wu_free = [None, None]
            wd_free = [None, None]
            pG_free = [None, None, None]
            pU_free = [None, None, None]
            sil3 = None
            pD_free = [None, None]
            sil_free = [None, None, None]
            hs_free = None
            out_toks = []

            for ei in range(E):
                for fb in range(NFB):
                    par = (ei * NFB + fb) % 2
                    tg = p.op("G", lambda e, ei=ei, fb=fb, par=par: e.dma_start(
                        out=gb[par][:, :], in_=gw_d[ei * NFB + fb, :, :]),
                        after=[wg_free[par]], dma=True)
                    tu = p.op("G", lambda e, ei=ei, fb=fb, par=par: e.dma_start(
                        out=ub[par][:, :], in_=uw_d[ei * NFB + fb, :, :]),
                        after=[wu_free[par]], dma=True)
                    td = p.op("G", lambda e, ei=ei, fb=fb, par=par: e.dma_start(
                        out=db_[par][:, :], in_=dw_d[ei * NFB + fb, :, :]),
                        after=[wd_free[par]], dma=True)
                    for tb in range(NTB):
                        p1_done = []
                        for f0 in range(4):
                            bpar = f0 % 3

                            def mmg(e, par=par, f0=f0, tb=tb, bpar=bpar):
                                last = None
                                for k in range(KT):
                                    last = e.matmul(pG[bpar][:, :],
                                                    gb[par][:, k * 512 + f0 * 128: k * 512 + (f0 + 1) * 128],
                                                    xn2f[:, k * n_own + tb * 512: k * n_own + (tb + 1) * 512],
                                                    start=(k == 0), stop=(k == KT - 1))
                                return last
                            mg = p.op("T", mmg, after=[tg, pG_free[bpar]])

                            def mmu(e, par=par, f0=f0, tb=tb, bpar=bpar):
                                last = None
                                for k in range(KT):
                                    last = e.matmul(pU[bpar][:, :],
                                                    ub[par][:, k * 512 + f0 * 128: k * 512 + (f0 + 1) * 128],
                                                    xn2f[:, k * n_own + tb * 512: k * n_own + (tb + 1) * 512],
                                                    start=(k == 0), stop=(k == KT - 1))
                                return last
                            mu = p.op("T", mmu, after=[tu, pU_free[bpar], mg])
                            wg_free[par] = mg
                            wu_free[par] = mu
                            sv = p.op("S", lambda e, bpar=bpar: e.activation(
                                sil[bpar][:, :], pG[bpar][:, :], AF.Silu), after=[mg, sil_free[bpar]])
                            pG_free[bpar] = sv
                            hw = p.op("V", lambda e, f0=f0, tb=tb, bpar=bpar: e.tensor_tensor(
                                hs[:, f0 * n_own + tb * 512: f0 * n_own + (tb + 1) * 512],
                                sil[bpar][:, :], pU[bpar][:, :], ALU.mult),
                                after=[sv, mu, hs_free])
                            pU_free[bpar] = hw
                            sil_free[bpar] = hw
                            p1_done.append(hw)
                    last_rd = None
                    for tb in range(NTB):
                        for i in range(4):
                            ti = tb * 4 + i
                            for hf in range(2):
                                bpar = hf

                                def mmd(e, par=par, ti=ti, hf=hf, bpar=bpar):
                                    last = None
                                    for f0 in range(4):
                                        last = e.matmul(pD[bpar][:, :],
                                                        hs[:, f0 * n_own + ti * 128: f0 * n_own + (ti + 1) * 128],
                                                        db_[par][:, f0 * D + hf * 512: f0 * D + (hf + 1) * 512],
                                                        start=(f0 == 0), stop=(f0 == 3))
                                    return last
                                md = p.op("T", mmd, after=p1_done + [td, pD_free[bpar]])
                                wd_free[par] = md
                                last_rd = md
                                ev = p.op("V", lambda e, ti=ti, hf=hf, ei=ei, bpar=bpar: e.scalar_tensor_tensor(
                                    x_res[:, ti * D + hf * 512: ti * D + (hf + 1) * 512],
                                    pD[bpar][:, :], w8_s[:, ti * 8 + ei: ti * 8 + ei + 1],
                                    x_res[:, ti * D + hf * 512: ti * D + (hf + 1) * 512],
                                    ALU.mult, ALU.add), after=[md])
                                pD_free[bpar] = ev
                                out_toks.append(ev)
                    hs_free = last_rd
            run_block(p)

        p = Prog(counts)
        st_tok = p.op("Y", lambda e: e.dma_start(
            out=r3d(out_d[:, :]), in_=r3s(x_res[:, :], NT)),
            after=out_toks[-2 * NT:], dma=True)
        p.op("Y", lambda e: e.nop(), after=[st_tok])
        run_block(p)

    return nc



def _prep_host(L0_attn_x, L0_attn_w_qkv, L0_attn_w_out, L0_ffn_router_w,
               L0_ffn_router_b, L0_ffn_gate_w, L0_ffn_up_w, L0_ffn_down_w,
               n_own, n_pre, f_ffn):
    x = np.ascontiguousarray(np.asarray(L0_attn_x, dtype=np.float32).reshape(-1, D))
    wqkv = np.asarray(L0_attn_w_qkv, dtype=np.float32)
    rw = np.asarray(L0_ffn_router_w, dtype=np.float32)
    rb = np.asarray(L0_ffn_router_b, dtype=np.float32)
    gw = np.asarray(L0_ffn_gate_w, dtype=np.float32)
    uw = np.asarray(L0_ffn_up_w, dtype=np.float32)
    dw = np.asarray(L0_ffn_down_w, dtype=np.float32)

    ntok = x.shape[0]
    assert ntok == NC * n_own

    KTn = D // 128
    NFB = f_ffn // 512

    def pshuf(mat, wtile):
        Wc = mat.shape[1]
        a = mat.reshape(KTn, 128, Wc // wtile, wtile)
        return np.ascontiguousarray(a.transpose(2, 1, 0, 3).reshape(Wc // wtile, 128, KTn * wtile))

    perm = np.arange(D)
    perm = (perm // HD) * HD + ((perm % HD) + HD // 2) % HD
    wq = wqkv[:, 0:D]
    wk = wqkv[:, D:2 * D]
    wv = wqkv[:, 2 * D:3 * D]
    wqkt = np.ascontiguousarray(np.concatenate(
        [pshuf(wq, 128), pshuf(wq[:, perm], 128),
         pshuf(wk, 128), pshuf(wk[:, perm], 128)], axis=0))
    wkvp = np.ascontiguousarray(np.concatenate([pshuf(wk, 512), pshuf(wv, 512)], axis=0))
    wouts = pshuf(np.ascontiguousarray(np.asarray(L0_attn_w_out, dtype=np.float32)), D)[0]
    rwt = pshuf(np.ascontiguousarray(rw.T), 8)[0]
    bbc = np.ascontiguousarray(np.broadcast_to(rb[None, :], (128, 8)).astype(np.float32))
    gwt = np.concatenate([pshuf(np.ascontiguousarray(gw[e].T), 512) for e in range(8)], axis=0)
    uwt = np.concatenate([pshuf(np.ascontiguousarray(uw[e].T), 512) for e in range(8)], axis=0)
    def pshuf_d(mat):
        a = mat.reshape(NFB, 4, 128, D)
        return np.ascontiguousarray(a.transpose(0, 2, 1, 3).reshape(NFB, 128, 4 * D))
    dwt = np.concatenate([pshuf_d(np.ascontiguousarray(dw[e].T)) for e in range(8)], axis=0)

    NPT = n_pre // 128
    r = np.arange(128)
    jcol = (r % 32).astype(np.float32).reshape(128, 1)
    j64 = np.ascontiguousarray(
        np.broadcast_to(np.arange(32, dtype=np.float32)[None, :], (128, 32)))
    sgn = np.where((r % 64) < 32, -np.float32(SCALE), np.float32(SCALE)).astype(np.float32).reshape(128, 1)
    posp = (np.arange(NPT, dtype=np.float32)[None, :] * 128 + r[:, None]).astype(np.float32)
    ident = np.eye(128, dtype=np.float32)
    cmask = np.triu(np.ones((128, 128), dtype=np.float32))
    ccol = np.zeros((128, 2), np.float32)
    ccol[:, 0] = EPS
    ccol[:, 1] = 1.0

    xpre = np.zeros((n_pre, D), dtype=np.float32)
    npre_eff = min(n_pre, (NC - 1) * n_own)
    xpre[:npre_eff] = x[:npre_eff]

    in_maps = []
    for c in range(NC):
        pos0 = c * n_own
        posb = np.ascontiguousarray(np.broadcast_to(
            (pos0 + np.arange(n_own, dtype=np.float32))[None, :], (128, n_own)))
        prem = np.zeros((128, NPT), np.float32)
        npre_c = c * n_own
        full_tiles = npre_c // 128
        prem[:, :full_tiles] = 1.0
        rem = npre_c - full_tiles * 128
        if rem > 0:
            prem[:rem, full_tiles] = 1.0
        in_maps.append(dict(
            x=np.ascontiguousarray(x[c * n_own:(c + 1) * n_own]),
            xpre=xpre, wqkt=wqkt, wkvp=wkvp, wout=wouts, rwt=rwt, bbc=bbc,
            gw=gwt, uw=uwt, dw=dwt,
            posb=posb, posp=posp, jcol=jcol, j64=j64, sgn=sgn,
            prem=prem, ident=ident, cmask=cmask, ccol=ccol,
        ))
    return in_maps


def kernel(L0_attn_x, L0_attn_w_qkv, L0_attn_w_out, L0_ffn_router_w,
           L0_ffn_router_b, L0_ffn_gate_w, L0_ffn_up_w, L0_ffn_down_w):
    n_own = 2048
    n_pre = 14336
    f_ffn = 4096
    in_maps = _prep_host(L0_attn_x, L0_attn_w_qkv, L0_attn_w_out,
                         L0_ffn_router_w, L0_ffn_router_b, L0_ffn_gate_w,
                         L0_ffn_up_w, L0_ffn_down_w, n_own, n_pre, f_ffn)
    nc = build(n_own, n_pre, f_ffn, dbg=False)
    res = run_bass_kernel_spmd(nc, in_maps, core_ids=list(range(NC)))
    out = np.concatenate([np.asarray(r["out"]) for r in res.results], axis=0)
    return out.reshape(1, NC * n_own, D)


# version 23
# speedup vs baseline: 1.2107x; 1.1314x over previous
import math
import numpy as np
import concourse.bass as bass
import concourse.mybir as mybir
from concourse.bass_utils import run_bass_kernel_spmd

F32 = mybir.dt.float32
BF16 = mybir.dt.bfloat16
I32 = mybir.dt.int32
AF = mybir.ActivationFunctionType
ALU = mybir.AluOpType
AX = mybir.AxisListType

NC = 8
D = 1024
HEADS = 16
HD = 64
KT = D // 128
SCALE = float(HD ** (-0.25))
PI = float(np.float32(math.pi))
EPS = 1e-5


def _trunc12(x):
    import struct
    b = struct.unpack('>I', struct.pack('>f', np.float32(x)))[0]
    b &= ~((1 << 11) - 1)
    return float(struct.unpack('>f', struct.pack('>I', b))[0])


TWO_PI = 2.0 * math.pi
CW1 = _trunc12(TWO_PI)
CW2 = _trunc12(TWO_PI - CW1)
CW3 = float(np.float32(TWO_PI - CW1 - CW2))
INV_2PI = float(np.float32(1.0 / TWO_PI))
S_HI = float(np.float32(-math.log(10000.0) / 32.0))
S_LO = float(np.float32(-math.log(10000.0) / 32.0 - S_HI))

NLANES = 4
NLANES_G = 8
DMA_LANES = tuple(f"Y{i}" for i in range(NLANES)) + tuple(f"G{i}" for i in range(NLANES_G))
ENGS = ("T", "V", "S", "G", "Y")
SEM_NAMES = ENGS + DMA_LANES
_AUTO_EDGE = ("V", "S", "G")


class Prog:

    def __init__(self, counts):
        self.q = {e: [] for e in ENGS}
        self.count = counts
        self.entry_waits = [(l, counts[l]) for l in DMA_LANES if counts[l] > 0]

    def op(self, eng, fn, after=(), dma=False):
        waits = {}
        if eng in _AUTO_EDGE and not dma and self.count[eng] > 3:
            waits[eng] = self.count[eng] - 3
        for t in after:
            if t is None:
                continue
            e2, v = t
            if e2 == eng == "T":
                continue
            if v > 0:
                waits[e2] = max(waits.get(e2, 0), v)
        if dma:
            lane_i = self.count[eng + "_rr"] if (eng + "_rr") in self.count else 0
            self.count[eng + "_rr"] = (lane_i + 1) % (NLANES_G if eng == "G" else NLANES)
            lane = f"{eng}{lane_i}"
            if self.count[lane] > 0:
                waits[lane] = max(waits.get(lane, 0), self.count[lane])
            self.count[lane] += 16
            tok = (lane, self.count[lane])
            self.q[eng].append((fn, sorted(waits.items()), 16, lane))
            return tok
        self.count[eng] += 1
        tok = (eng, self.count[eng])
        self.q[eng].append((fn, sorted(waits.items()), 1, eng))
        return tok

    def replay(self, eng_name, eng_obj, sems):
        if self.q[eng_name]:
            for e2, v in self.entry_waits:
                eng_obj.wait_ge(sems[e2], v)
        for fn, waits, inc, sem in self.q[eng_name]:
            for e2, v in waits:
                eng_obj.wait_ge(sems[e2], v)
            inst = fn(eng_obj)
            inst.then_inc(sems[sem], inc)


def r3d(ap, p=128):
    return ap.rearrange("(a p) b -> p a b", p=p)


def r3s(ap, a):
    return ap.rearrange("p (a b) -> p a b", a=a)


def build(n_own, n_pre, f_ffn, dbg=False):
    NT = n_own // 128
    NTB = n_own // 512
    NCH = n_own // 128
    NPTB = n_pre // 512
    NPT = n_pre // 128
    NFB = f_ffn // 512
    E = 8
    NSLOT_A = 2 * n_own
    NBA = NSLOT_A // 512
    NBLK = NBA + E
    C = NT * 8
    YD0 = 128

    def stpos(h):
        return 64 * (h % 2), (h // 8) * 260 + ((h % 8) // 2) * 65

    nc = bass.Bass("TRN2", target_bir_lowering=False)

    x_d = nc.dram_tensor("x", [n_own, D], F32, kind="ExternalInput")
    xpre_d = nc.dram_tensor("xpre", [n_pre, D], F32, kind="ExternalInput")
    wqkt_d = nc.dram_tensor("wqkt", [4 * KT, 128, KT * 128], F32, kind="ExternalInput")
    wkvp_d = nc.dram_tensor("wkvp", [4, 128, KT * 512], F32, kind="ExternalInput")
    wout_d = nc.dram_tensor("wout", [128, KT * D], F32, kind="ExternalInput")
    rwt_d = nc.dram_tensor("rwt", [128, KT * 8], F32, kind="ExternalInput")
    bbc_d = nc.dram_tensor("bbc", [128, 8], F32, kind="ExternalInput")
    gw_d = nc.dram_tensor("gw", [E * NFB, 128, KT * 512], F32, kind="ExternalInput")
    uw_d = nc.dram_tensor("uw", [E * NFB, 128, KT * 512], F32, kind="ExternalInput")
    dw_d = nc.dram_tensor("dw", [E * NFB, 128, 4 * D], F32, kind="ExternalInput")
    posb_d = nc.dram_tensor("posb", [128, n_own], F32, kind="ExternalInput")
    posp_d = nc.dram_tensor("posp", [128, NPT], F32, kind="ExternalInput")
    jcol_d = nc.dram_tensor("jcol", [128, 1], F32, kind="ExternalInput")
    j64_d = nc.dram_tensor("j64", [128, 32], F32, kind="ExternalInput")
    sgn_d = nc.dram_tensor("sgn", [128, 1], F32, kind="ExternalInput")
    prem_d = nc.dram_tensor("prem", [128, NPT], F32, kind="ExternalInput")
    ident_d = nc.dram_tensor("ident", [128, 128], F32, kind="ExternalInput")
    cmask_d = nc.dram_tensor("cmask", [128, 128], F32, kind="ExternalInput")
    ccol_d = nc.dram_tensor("ccol", [128, 2], F32, kind="ExternalInput")
    cm1_d = nc.dram_tensor("cm1", [128, 128], F32, kind="ExternalInput")
    pf16_d = nc.dram_tensor("pf16", [128, 128], F32, kind="ExternalInput")
    e8_d = nc.dram_tensor("e8", [128, 128], F32, kind="ExternalInput")
    tk1_d = nc.dram_tensor("tk1", [128, C], F32, kind="ExternalInput")
    econ_d = nc.dram_tensor("econ", [128, C], F32, kind="ExternalInput")
    kbl_d = nc.dram_tensor("kbl", [1, 8 * NBA], F32, kind="ExternalInput")
    jp_d = nc.dram_tensor("jp", [128, 1], F32, kind="ExternalInput")
    cix_d = nc.dram_tensor("cix", [128, C], F32, kind="ExternalInput")

    scratch_kind = "ExternalOutput" if dbg else "Internal"
    xmid_d = nc.dram_tensor("xmid", [n_own, D], F32, kind="Internal")
    y_d = nc.dram_tensor("yp", [YD0 + 2 * n_own, D], F32, kind=scratch_kind)
    sl_d = nc.dram_tensor("slt", [NSLOT_A + C, 4], F32, kind=scratch_kind)
    cl_d = nc.dram_tensor("clt", [E * 512 + C, 4], F32, kind=scratch_kind)

    out_d = nc.dram_tensor("out", [n_own, D], F32, kind="ExternalOutput")
    dbg_d = {}
    if dbg:
        for nm, sh in [("d_xn1", [n_own, D]), ("d_qf", [128, KT * n_own]),
                       ("d_kf", [128, KT * n_own]), ("d_vt", [n_own, D]),
                       ("d_st", [128, 8 * 65]), ("d_x1", [n_own, D]),
                       ("d_xn2", [n_own, D]), ("d_w8", [128, NT * 8])]:
            dbg_d[nm] = nc.dram_tensor(nm, sh, F32, kind="ExternalOutput")

    counts = {e: 0 for e in SEM_NAMES}
    counts["Y_rr"] = 0
    counts["G_rr"] = 0
    from contextlib import ExitStack

    with ExitStack() as top:
        sems = {e: top.enter_context(nc.semaphore("sem_" + e)) for e in SEM_NAMES}
        sb = lambda n, s, d=F32: top.enter_context(nc.sbuf_tensor(n, s, d))

        x_res = sb("x_res", [128, NT * D])
        st_f = sb("st_f", [128, 8 * 65])
        st_b = sb("st_b", [128, 8 * 65], BF16)
        invf_c = sb("invf_c", [128, 1])
        invf64 = sb("invf64", [128, 32])
        jcol = sb("jcol_s", [128, 1])
        j64 = sb("j64_s", [128, 32])
        sgn = sb("sgn_s", [128, 1])
        ccol = sb("ccol_s", [128, 2])
        ident_b = sb("ident_b", [128, 128], BF16)
        cmask_b = sb("cmask_b", [128, 128], BF16)
        ones_b = sb("ones_b", [128, 1], BF16)
        posp_s = sb("posp_s", [128, NPT])
        prem_s = sb("prem_s", [128, NPT])
        bbc_s = sb("bbc_s", [128, 8])
        w8_s = sb("w8_s", [128, NT * 8])
        wi_i32 = sb("wi_i32", [128, NBA], I32)
        efp = sb("efp", [128, NBA])

        def run_block(p):
            with nc.Block() as blk:
                @blk.tensor
                def _(e):
                    p.replay("T", e, sems)

                @blk.vector
                def _(e):
                    p.replay("V", e, sems)

                @blk.scalar
                def _(e):
                    p.replay("S", e, sems)

                @blk.gpsimd
                def _(e):
                    p.replay("G", e, sems)

                @blk.sync
                def _(e):
                    p.replay("Y", e, sems)

        def emit_ln(p, src_ap_fn, dst_ap_fn, lnc, lcol, src_after=(), dst_after=(),
                    prev=(None, None, None), lnc_prev=None):
            a0 = p.op("V", lambda e: e.reduce_sum(lcol[:, 0:1], src_ap_fn(), axis=AX.X),
                      after=list(src_after) + [prev[0]])
            a2 = p.op("V", lambda e: e.tensor_tensor(lnc[:, :], src_ap_fn(), src_ap_fn(), ALU.mult),
                      after=list(src_after) + [prev[1], lnc_prev])
            a3 = p.op("V", lambda e: e.reduce_sum(lcol[:, 2:3], lnc[:, :], axis=AX.X),
                      after=[a2, prev[2]])
            a1 = p.op("V", lambda e: e.tensor_scalar(lcol[:, 1:2], lcol[:, 0:1], 1.0 / D, None, ALU.mult), after=[a0])
            a4 = p.op("V", lambda e: e.tensor_tensor(lcol[:, 3:4], lcol[:, 1:2], lcol[:, 1:2], ALU.mult), after=[a1])
            a5 = p.op("V", lambda e: e.tensor_scalar(lcol[:, 2:3], lcol[:, 2:3], 1.0 / D, None, ALU.mult), after=[a3])
            a6 = p.op("V", lambda e: e.tensor_tensor(lcol[:, 2:3], lcol[:, 2:3], lcol[:, 3:4], ALU.subtract), after=[a5, a4])
            a7 = p.op("S", lambda e: e.activation(lcol[:, 2:3], lcol[:, 2:3], AF.Ln, bias=ccol[:, 0:1]), after=[a6])
            a8 = p.op("S", lambda e: e.activation(lcol[:, 3:4], lcol[:, 2:3], AF.Exp, scale=-0.5), after=[a7])
            a9 = p.op("V", lambda e: e.tensor_scalar(dst_ap_fn(), src_ap_fn(), lcol[:, 1:2], lcol[:, 3:4], ALU.subtract, ALU.mult),
                      after=[a8] + list(dst_after))
            return (a9, a3), (a1, a3, a7)

        def emit_reduce(p, th, thp, kfb, kib, w, tok_in):
            a = p.op("V", lambda e: e.tensor_scalar(thp[:, :w], th[:, :w], PI, None, ALU.add), after=[tok_in])
            b = p.op("V", lambda e: e.tensor_scalar(kfb[:, :w], thp[:, :w], INV_2PI, None, ALU.mult), after=[a])
            c = p.op("V", lambda e: e.tensor_copy(kib[:, :w], kfb[:, :w]), after=[b])
            d = p.op("V", lambda e: e.tensor_copy(kfb[:, :w], kib[:, :w]), after=[c])
            ng = p.op("V", lambda e: e.tensor_scalar(kfb[:, :w], kfb[:, :w], -1.0, None, ALU.mult), after=[d])
            r = ng
            for cwc in (CW1, CW2, CW3):
                r = p.op("V", lambda e, cc=cwc: e.scalar_tensor_tensor(thp[:, :w], kfb[:, :w], cc, thp[:, :w], ALU.mult, ALU.add), after=[r])
            r = p.op("V", lambda e: e.tensor_scalar(thp[:, :w], thp[:, :w], -PI, None, ALU.add), after=[r])
            g = p.op("V", lambda e: e.tensor_scalar(kfb[:, :w], thp[:, :w], PI, None, ALU.is_gt), after=[r])
            r = p.op("V", lambda e: e.scalar_tensor_tensor(thp[:, :w], kfb[:, :w], -2.0 * PI, thp[:, :w], ALU.mult, ALU.add), after=[g])
            l = p.op("V", lambda e: e.tensor_scalar(kfb[:, :w], thp[:, :w], -PI, None, ALU.is_lt), after=[r])
            r = p.op("V", lambda e: e.scalar_tensor_tensor(thp[:, :w], kfb[:, :w], 2.0 * PI, thp[:, :w], ALU.mult, ALU.add), after=[l])
            r = p.op("V", lambda e: e.tensor_scalar(thp[:, :w], thp[:, :w], PI, -PI, ALU.min, ALU.max), after=[r])
            return r

        p = Prog(counts)
        p.op("Y", lambda e: e.dma_start(
            out=r3s(x_res[:, :], NT), in_=r3d(x_d[:, :])), dma=True)
        for dst, src in [(posp_s, posp_d), (prem_s, prem_d),
                         (jcol, jcol_d), (j64, j64_d), (sgn, sgn_d),
                         (ccol, ccol_d), (bbc_s, bbc_d)]:
            p.op("Y", lambda e, d=dst, s=src: e.dma_start(out=d[:, :], in_=s[:, :]), dma=True)
        p.op("G", lambda e: e.dma_start(out=ident_b[:, :], in_=ident_d[:, :]), dma=True)
        p.op("G", lambda e: e.dma_start(out=cmask_b[:, :], in_=cmask_d[:, :]), dma=True)
        p.op("V", lambda e: e.memset(ones_b[:, :], 1.0))
        p.op("V", lambda e: e.memset(st_f[:, :], 0.0))
        run_block(p)

        p = Prog(counts)
        t0 = p.op("V", lambda e: e.tensor_scalar(invf_c[:, :], jcol[:, :], S_HI, None, ALU.mult))
        t1 = p.op("V", lambda e: e.scalar_tensor_tensor(invf_c[:, :], jcol[:, :], S_LO, invf_c[:, :], ALU.mult, ALU.add), after=[t0])
        p.op("S", lambda e: e.activation(invf_c[:, :], invf_c[:, :], AF.Exp), after=[t1])
        t0 = p.op("V", lambda e: e.tensor_scalar(invf64[:, :], j64[:, :], S_HI, None, ALU.mult))
        t1 = p.op("V", lambda e: e.scalar_tensor_tensor(invf64[:, :], j64[:, :], S_LO, invf64[:, :], ALU.mult, ALU.add), after=[t0])
        p.op("S", lambda e: e.activation(invf64[:, :], invf64[:, :], AF.Exp), after=[t1])
        run_block(p)

        cosmM = sb("cosmM", [128, n_own], BF16)
        sinmM = sb("sinmM", [128, n_own], BF16)
        tabstk = ExitStack()
        coswA = tabstk.enter_context(nc.sbuf_tensor("coswA", [128, NPT * 64], BF16))
        sinwA = tabstk.enter_context(nc.sbuf_tensor("sinwA", [128, NPT * 64], BF16))
        with ExitStack() as ph:
            psb = lambda n, s, d=F32: ph.enter_context(nc.sbuf_tensor(n, s, d))
            WT = NPT * 32 + n_own
            tha = psb("tha", [128, WT])
            thb = psb("thb", [128, WT])
            thk = psb("thk", [128, WT])
            tki = ph.enter_context(nc.sbuf_tensor("tkiT", [128, WT], I32))
            posbf = psb("posbf", [128, n_own])
            p = Prog(counts)
            pb = p.op("Y", lambda e: e.dma_start(out=posbf[:, :], in_=posb_d[:, :]), dma=True)
            ppre = bass.AP(posp_s, 0, [[NPT, 128], [1, NPT], [0, 32]])
            ifre = bass.AP(invf64, 0, [[32, 128], [0, NPT], [1, 32]])
            tA = p.op("V", lambda e: e.tensor_tensor(
                bass.AP(tha, 0, [[WT, 128], [32, NPT], [1, 32]]), ppre, ifre, ALU.mult))
            tB = p.op("V", lambda e: e.tensor_scalar(
                tha[:, NPT * 32: WT], posbf[:, :], invf_c[:, 0:1], None, ALU.mult), after=[pb])
            red = emit_reduce(p, tha, thb, thk, tki, WT, tB)
            sn1 = p.op("S", lambda e: e.activation(tha[:, :], thb[:, :], AF.Sin), after=[red])
            c0 = p.op("V", lambda e: e.tensor_scalar(thk[:, :], thb[:, :], PI / 2.0, None, ALU.add), after=[red])
            c1 = p.op("V", lambda e: e.tensor_scalar(thb[:, :], thk[:, :], PI, None, ALU.is_gt), after=[c0, sn1])
            c2 = p.op("V", lambda e: e.scalar_tensor_tensor(thk[:, :], thb[:, :], -2.0 * PI, thk[:, :], ALU.mult, ALU.add), after=[c1])
            c3 = p.op("V", lambda e: e.tensor_scalar(thk[:, :], thk[:, :], PI, -PI, ALU.min, ALU.max), after=[c2])
            sn2 = p.op("S", lambda e: e.activation(thk[:, :], thk[:, :], AF.Sin), after=[c3])
            f0t = p.op("V", lambda e: e.tensor_scalar(
                bass.AP(sinwA, 0, [[NPT * 64, 128], [64, NPT], [1, 32]]),
                bass.AP(tha, 0, [[WT, 128], [32, NPT], [1, 32]]), -SCALE, None, ALU.mult), after=[sn1])
            f1t = p.op("V", lambda e: e.tensor_scalar(
                bass.AP(sinwA, 32, [[NPT * 64, 128], [64, NPT], [1, 32]]),
                bass.AP(tha, 0, [[WT, 128], [32, NPT], [1, 32]]), SCALE, None, ALU.mult), after=[sn1])
            f2t = p.op("V", lambda e: e.tensor_scalar(
                bass.AP(coswA, 0, [[NPT * 64, 128], [64, NPT], [1, 32]]),
                bass.AP(thk, 0, [[WT, 128], [32, NPT], [1, 32]]), SCALE, None, ALU.mult), after=[sn2])
            f3t = p.op("V", lambda e: e.tensor_scalar(
                bass.AP(coswA, 32, [[NPT * 64, 128], [64, NPT], [1, 32]]),
                bass.AP(thk, 0, [[WT, 128], [32, NPT], [1, 32]]), SCALE, None, ALU.mult), after=[sn2])
            fm0 = p.op("V", lambda e: e.tensor_scalar(sinmM[:, :], tha[:, NPT * 32: WT], sgn[:, 0:1], None, ALU.mult), after=[sn1])
            fm1 = p.op("V", lambda e: e.tensor_scalar(cosmM[:, :], thk[:, NPT * 32: WT], SCALE, None, ALU.mult), after=[sn2])
            run_block(p)

        with ExitStack() as ph:
            psb = lambda n, s, d=F32: ph.enter_context(nc.sbuf_tensor(n, s, d))
            xps = [psb("xps0", [128, 4 * D], BF16)] * 2
            xnp_t = [psb(f"xnp_t{i}", [128, 4 * D], BF16) for i in range(2)]
            xnp_f = [psb(f"xnp_f{j}", [128, KT * 512], BF16) for j in range(2)]
            kp_t = psb("kp_t", [128, 4 * D], BF16)
            vp_t = psb("vp_t", [128, 4 * D], BF16)
            wk_b = [psb(f"wk_b{i}", [128, KT * 512], BF16) for i in range(2)]
            wv_b = [psb(f"wv_b{i}", [128, KT * 512], BF16) for i in range(2)]
            lnc = [psb("lnc", [128, D])] * 2
            lcol = [psb(f"lcol{j}", [128, 4]) for j in range(2)]
            rsc1 = [psb(f"rsc1{j}", [128, D], BF16) for j in range(2)]
            rsc2 = [psb(f"rsc2{j}", [128, D], BF16) for j in range(2)]
            ztab = psb("ztab", [128, 128])
            psK = [ph.enter_context(nc.psum_tensor(f"psK{i}", [128, 512], F32)) for i in range(2)]
            psV = [ph.enter_context(nc.psum_tensor(f"psV{i}", [128, 512], F32)) for i in range(2)]
            psT = [ph.enter_context(nc.psum_tensor(f"psT{i}", [128, 1024], BF16)) for i in range(2)]
            psS = [ph.enter_context(nc.psum_tensor(f"psS{i}", [128, 512], F32)) for i in range(2)]

            p = Prog(counts)
            wk_tok, wv_tok = [], []
            for hf in range(2):
                wk_tok.append(p.op("G", lambda e, d=wk_b[hf], hf=hf: e.dma_start(
                    out=d[:, :], in_=wkvp_d[hf, :, :]), dma=True))
                wv_tok.append(p.op("G", lambda e, d=wv_b[hf], hf=hf: e.dma_start(
                    out=d[:, :], in_=wkvp_d[2 + hf, :, :]), dma=True))

            zt0 = p.op("V", lambda e: e.memset(ztab[:, :], 0.0))
            for i in range(2 * n_own // 128):
                for cc in range(8):
                    p.op("Y", lambda e, i=i, cc=cc: e.dma_start(
                        out=y_d[YD0 + i * 128:YD0 + (i + 1) * 128, cc * 128:(cc + 1) * 128],
                        in_=ztab[:, :]), after=[zt0], dma=True)
            zsl_tok = p.op("Y", lambda e: e.dma_start(
                out=r3d(sl_d[0:NSLOT_A, :]), in_=r3s(ztab[:, 0:(NSLOT_A // 128) * 4], NSLOT_A // 128)),
                after=[zt0], dma=True)
            zcl_tok = p.op("Y", lambda e: e.dma_start(
                out=r3d(cl_d[0:E * 512, :]), in_=r3s(ztab[:, 0:(E * 512 // 128) * 4], E * 512 // 128)),
                after=[zt0], dma=True)

            ld_tok = [None, None]
            lnc_last = None
            ln_prev = [(None, None, None), (None, None, None)]
            stage_free = [None, None]
            xnpt_free = [None, None]
            xnpf_free = [None, None]
            psK_free = [None, None]
            psV_free = [None, None]
            psT_free = [None, None]
            kp_free = None
            vp_free = None
            rprev = [[None, None, None], [None, None, None]]
            s_chain = None
            first_s = [True, True]

            for ptb in range(NPTB):
                par = ptb % 2
                ld_tok[par] = p.op("G", lambda e, b=xps[par], i0=ptb * 512:
                                   e.dma_start(out=r3s(b[:, :], 4), in_=r3d(xpre_d[i0:i0 + 512, :])),
                                   after=[stage_free[0], stage_free[1]], dma=True)
                ln_done = []
                for i in range(4):
                    (tok, lnc_last), ln_prev[i % 2] = emit_ln(
                        p,
                        lambda b=xps[par], i=i: b[:, i * D:(i + 1) * D],
                        lambda i=i, par=par: xnp_t[par][:, i * D:(i + 1) * D],
                        lnc[i % 2], lcol[i % 2],
                        src_after=[ld_tok[par]], dst_after=[xnpt_free[par]],
                        prev=ln_prev[i % 2], lnc_prev=lnc_last)
                    ln_done.append(tok)
                stage_free[par] = ln_done[-1]

                tr_last = None
                for i in range(4):
                    for k in range(KT):
                        tp_ = (i * KT + k) % 2
                        tm = p.op("T", lambda e, i=i, k=k, tp_=tp_, par=par: e.transpose(
                            psT[tp_][:, 0:128], xnp_t[par][:, i * D + k * 128: i * D + (k + 1) * 128], ident_b[:, :]),
                            after=[ln_done[i], psT_free[tp_], xnpf_free[par]])
                        psT_free[tp_] = p.op("S" if tp_ else "V",
                            lambda e, i=i, k=k, tp_=tp_, par=par: (e.copy if tp_ else e.tensor_copy)(
                            xnp_f[par][:, k * 512 + i * 128: k * 512 + (i + 1) * 128], psT[tp_][:, 0:128]), after=[tm])
                        tr_last = psT_free[tp_]
                xnpt_free[par] = tr_last
                xnpf_ready_l = [psT_free[0], psT_free[1]]
                xnpf_ready = tr_last

                kp_ready = {}
                vp_ready = {}
                for i in range(4):
                    for kind in range(2):
                        for hf in range(2):
                            bank = (psK if kind == 0 else psV)[hf]
                            wb = (wk_b if kind == 0 else wv_b)[hf]
                            wt = (wk_tok if kind == 0 else wv_tok)[hf]
                            free = (psK_free if kind == 0 else psV_free)[hf]

                            def mmkv(e, i=i, b=bank, w=wb, par=par):
                                last = None
                                for k in range(KT):
                                    last = e.matmul(b[:, :],
                                                    xnp_f[par][:, k * 512 + i * 128: k * 512 + (i + 1) * 128],
                                                    w[:, k * 512:(k + 1) * 512],
                                                    start=(k == 0), stop=(k == KT - 1))
                                return last
                            mt = p.op("T", mmkv, after=xnpf_ready_l + [wt, free])
                            dst = kp_t if kind == 0 else vp_t
                            eng_ev = "V" if kind == 0 else "S"
                            ev = p.op(eng_ev, lambda e, d=dst, i=i, hf=hf, b=bank, k0=(kind == 0):
                                      (e.tensor_copy if k0 else e.copy)(
                                      d[:, i * D + hf * 512: i * D + hf * 512 + 512], b[:, :]),
                                      after=[mt, kp_free if kind == 0 else vp_free])
                            if kind == 0:
                                psK_free[hf] = ev
                            else:
                                psV_free[hf] = ev
                            if hf == 1:
                                (kp_ready if kind == 0 else vp_ready)[i] = ev
                xnpf_free[par] = mt

                for i in range(4):
                    ti = ptb * 4 + i
                    pj = i % 2
                    s1b, s2b = rsc1[pj], rsc2[pj]
                    cosb = bass.AP(coswA, ti * 64, [[NPT * 64, 128], [0, 16], [1, 64]])
                    sinb = bass.AP(sinwA, ti * 64, [[NPT * 64, 128], [0, 16], [1, 64]])
                    r0 = p.op("V", lambda e, i=i, cb=cosb, s1b=s1b: e.tensor_tensor(
                        s1b[:, :], kp_t[:, i * D:(i + 1) * D], cb, ALU.mult),
                        after=[kp_ready[i], rprev[pj][0]])
                    r1 = p.op("V", lambda e, i=i, sbp=sinb, s2b=s2b: e.tensor_tensor(
                        s2b[:, :], bass.AP(kp_t, i * D + 32, [[4 * D, 128], [64, 16], [-32, 2], [1, 32]]),
                        sbp, ALU.mult), after=[kp_ready[i], rprev[pj][1], rprev[pj][2]])
                    r2 = p.op("V", lambda e, s1b=s1b, s2b=s2b: e.tensor_tensor(s1b[:, :], s1b[:, :], s2b[:, :], ALU.add), after=[r0, r1])
                    r3 = p.op("V", lambda e, s1b=s1b, s2b=s2b: e.tensor_scalar(s2b[:, :], s1b[:, :], 0.0, None, ALU.min), after=[r2])
                    r4 = p.op("S", lambda e, s2b=s2b: e.activation(s2b[:, :], s2b[:, :], AF.Exp), after=[r3])
                    r5 = p.op("V", lambda e, s1b=s1b: e.tensor_scalar(s1b[:, :], s1b[:, :], 0.0, None, ALU.max), after=[r2, r3])
                    r6 = p.op("V", lambda e, s1b=s1b, s2b=s2b: e.tensor_tensor(s1b[:, :], s1b[:, :], s2b[:, :], ALU.add), after=[r5, r4])
                    r7 = p.op("V", lambda e, i=i, ptb=ptb, s1b=s1b: e.tensor_scalar(
                        kp_t[:, i * D:(i + 1) * D], s1b[:, :],
                        prem_s[:, ptb * 4 + i: ptb * 4 + i + 1], None, ALU.mult), after=[r6, kp_free])
                    kp_ready[i] = r7
                    rprev[pj] = [r7, r6, r4]

                for i in range(4):
                    def smm(e, i=i, first=first_s[0], last_acc=(ptb == NPTB - 1 and i == 3)):
                        last = None
                        for h in range(HEADS):
                            bank = psS[h // 8]
                            rb = 64 * (h % 2)
                            cb = ((h % 8) // 2) * 64
                            st = first and (h % 8 in (0, 1))
                            sp = last_acc and (h % 8 in (6, 7))
                            last = e.matmul(bank[rb:rb + 64, cb:cb + 64],
                                            kp_t[:, i * D + h * HD: i * D + (h + 1) * HD],
                                            vp_t[:, i * D + h * HD: i * D + (h + 1) * HD],
                                            start=st, stop=sp, tile_position=(0, rb),
                                            skip_group_check=True)
                            last = e.matmul(psS[h // 8][rb:rb + 64, 256 + (h % 8): 257 + (h % 8)],
                                            kp_t[:, i * D + h * HD: i * D + (h + 1) * HD],
                                            ones_b[:, :],
                                            start=False, stop=False,
                                            tile_position=(0, rb), skip_group_check=True)
                        return last
                    s_chain = p.op("T", smm, after=[kp_ready[i], vp_ready[i], s_chain])
                    first_s = [False, False]
                kp_free = s_chain
                vp_free = s_chain

            evs = []
            for h in range(HEADS):
                rb, scol = stpos(h)
                cb = ((h % 8) // 2) * 64
                evs.append(p.op("V", lambda e, h=h, rb=rb, scol=scol, cb=cb: e.tensor_copy(
                    st_f[rb:rb + 64, scol:scol + 64], psS[h // 8][rb:rb + 64, cb:cb + 64]), after=[s_chain]))
                evs.append(p.op("V", lambda e, h=h, rb=rb, scol=scol: e.tensor_copy(
                    st_f[rb:rb + 64, scol + 64:scol + 65],
                    psS[h // 8][rb:rb + 64, 256 + (h % 8): 257 + (h % 8)]), after=[s_chain]))
            ev2 = p.op("V", lambda e: e.tensor_copy(st_b[:, :], st_f[:, :]), after=evs)
            if dbg:
                p.op("Y", lambda e: e.dma_start(out=dbg_d["d_st"][:, :], in_=st_f[:, :]), after=[ev2], dma=True)
            run_block(p)
        tabstk.close()

        mid = ExitStack()
        qf = mid.enter_context(nc.sbuf_tensor("qf", [128, KT * n_own], BF16))
        kf = mid.enter_context(nc.sbuf_tensor("kf", [128, KT * n_own], BF16))
        vt = mid.enter_context(nc.sbuf_tensor("vt", [128, NT * D], BF16))
        with ExitStack() as ph:
            psb = lambda n, s, d=F32: ph.enter_context(nc.sbuf_tensor(n, s, d))
            xn_t = psb("xn_t", [128, D], BF16)
            xn_f = psb("xn_f", [128, KT * 512], BF16)
            lnc = [psb("lnc2", [128, D])] * 2
            lcol = [psb(f"lcol2{j}", [128, 4]) for j in range(2)]
            ra = [psb(f"ra{j}", [128, 512]) for j in range(2)]
            rb_ = [psb(f"rb{j}", [128, 512]) for j in range(2)]
            wqk = [psb(f"wqk{i}", [128, KT * 128], BF16) for i in range(2)]
            wvb = psb("wvb", [128, KT * 512], BF16)
            psQ = [ph.enter_context(nc.psum_tensor(f"psQ{i}", [128, 512], F32)) for i in range(4)]
            psV2 = [ph.enter_context(nc.psum_tensor(f"psV2{i}", [128, 512], F32)) for i in range(2)]
            psT2 = [ph.enter_context(nc.psum_tensor(f"psT2{i}", [128, 1024], BF16)) for i in range(2)]

            p = Prog(counts)
            xnt_free = []
            dbg_free = []
            lnc_last = None
            ln_prev = [(None, None, None), (None, None, None)]
            xnf_free = None
            psT2_free = [None, None]
            qprev = [[None, None], [None, None]]
            wqk_free = [None] * 2
            wvb_free = None
            psQ_free = [None] * 4
            psV2_free = [None] * 2
            last_write = []

            for tb in range(NTB):
                tr_last = None
                for i in range(4):
                    ti = tb * 4 + i
                    (tok, lnc_last), ln_prev[i % 2] = emit_ln(
                        p,
                        lambda ti=ti: x_res[:, ti * D:(ti + 1) * D],
                        lambda: xn_t[:, :],
                        lnc[i % 2], lcol[i % 2], dst_after=xnt_free,
                        prev=ln_prev[i % 2], lnc_prev=lnc_last)
                    if dbg:
                        dbg_free.append(p.op("G", lambda e, ti=ti: e.dma_start(
                            out=dbg_d["d_xn1"][ti * 128:(ti + 1) * 128, :],
                            in_=xn_t[:, :]), after=[tok], dma=True))
                    for k in range(KT):
                        tp_ = k % 2
                        tm = p.op("T", lambda e, i=i, k=k, tp_=tp_: e.transpose(
                            psT2[tp_][:, 0:128], xn_t[:, k * 128:(k + 1) * 128], ident_b[:, :]),
                            after=[tok, psT2_free[tp_], xnf_free])
                        psT2_free[tp_] = p.op("V", lambda e, i=i, k=k, tp_=tp_: e.tensor_copy(
                            xn_f[:, k * 512 + i * 128: k * 512 + (i + 1) * 128], psT2[tp_][:, 0:128]), after=[tm])
                        tr_last = psT2_free[tp_]
                    xnt_free = [tr_last, psT2_free[0], psT2_free[1]] + dbg_free
                    dbg_free = []
                xnf_ready_l = [psT2_free[0], psT2_free[1]]
                xnf_ready = tr_last

                for pair, dst in ((0, qf), (1, kf)):
                    for ft in range(KT):
                        bi = 0
                        bq, bp = psQ[(ft % 2) * 2], psQ[(ft % 2) * 2 + 1]
                        i_main = pair * 2 * KT + ft
                        i_perm = i_main + KT
                        wbi = 0
                        t_w0 = p.op("G", lambda e, w=wqk[wbi], ix=i_main: e.dma_start(
                            out=w[:, :], in_=wqkt_d[ix, :, :]),
                            after=[wqk_free[wbi]], dma=True)
                        t_w1 = p.op("G", lambda e, w=wqk[wbi + 1], ix=i_perm: e.dma_start(
                            out=w[:, :], in_=wqkt_d[ix, :, :]),
                            after=[wqk_free[wbi + 1]], dma=True)

                        bqi = (ft % 2) * 2

                        def mmq(e, w=wqk[wbi], b=bq):
                            last = None
                            for k in range(KT):
                                last = e.matmul(b[:, :], w[:, k * 128:(k + 1) * 128],
                                                xn_f[:, k * 512:(k + 1) * 512],
                                                start=(k == 0), stop=(k == KT - 1))
                            return last
                        m0 = p.op("T", mmq, after=xnf_ready_l + [t_w0, psQ_free[bqi]])

                        def mmp(e, w=wqk[wbi + 1], b=bp):
                            last = None
                            for k in range(KT):
                                last = e.matmul(b[:, :], w[:, k * 128:(k + 1) * 128],
                                                xn_f[:, k * 512:(k + 1) * 512],
                                                start=(k == 0), stop=(k == KT - 1))
                            return last
                        m1 = p.op("T", mmp, after=[t_w1, psQ_free[bqi + 1], m0])
                        wqk_free[wbi] = m0
                        wqk_free[wbi + 1] = m1
                        pj = ft % 2
                        rab, rbb = ra[pj], rb_[pj]
                        r0 = p.op("V", lambda e, b=bq, tb=tb, rab=rab: e.tensor_tensor(
                            rab[:, :], b[:, :], cosmM[:, tb * 512:(tb + 1) * 512], ALU.mult),
                            after=[m0, qprev[pj][0]])
                        r1 = p.op("V", lambda e, b=bp, tb=tb, rbb=rbb: e.tensor_tensor(
                            rbb[:, :], b[:, :], sinmM[:, tb * 512:(tb + 1) * 512], ALU.mult),
                            after=[m1, qprev[pj][1]])
                        r2 = p.op("V", lambda e, rab=rab, rbb=rbb: e.tensor_tensor(rab[:, :], rab[:, :], rbb[:, :], ALU.add), after=[r0, r1])
                        r3 = p.op("V", lambda e, rab=rab, rbb=rbb: e.tensor_scalar(rbb[:, :], rab[:, :], 0.0, None, ALU.min), after=[r2])
                        r4 = p.op("S", lambda e, rbb=rbb: e.activation(rbb[:, :], rbb[:, :], AF.Exp), after=[r3])
                        r5 = p.op("V", lambda e, rab=rab: e.tensor_scalar(rab[:, :], rab[:, :], 0.0, None, ALU.max), after=[r2, r3])
                        r6 = p.op("V", lambda e, d=dst, ft=ft, tb=tb, rab=rab, rbb=rbb: e.tensor_tensor(
                            d[:, ft * n_own + tb * 512: ft * n_own + (tb + 1) * 512],
                            rab[:, :], rbb[:, :], ALU.add), after=[r5, r4])
                        qprev[pj] = [r6, r4]
                        psQ_free[bqi] = r6
                        psQ_free[bqi + 1] = r6
                        last_write.append(r6)

                for hf in range(2):
                    t_wv = p.op("G", lambda e, hf=hf: e.dma_start(
                        out=wvb[:, :], in_=wkvp_d[2 + hf, :, :]), after=[wvb_free], dma=True)
                    for i in range(4):
                        ti = tb * 4 + i
                        bank = psV2[i % 2]

                        def mmv(e, i=i, b=bank):
                            last = None
                            for k in range(KT):
                                last = e.matmul(b[:, :], xn_f[:, k * 512 + i * 128: k * 512 + (i + 1) * 128],
                                                wvb[:, k * 512:(k + 1) * 512],
                                                start=(k == 0), stop=(k == KT - 1))
                            return last
                        mv = p.op("T", mmv, after=xnf_ready_l + [t_wv, psV2_free[i % 2]])
                        wvb_free = mv
                        ev = p.op("S", lambda e, ti=ti, hf=hf, b=bank: e.copy(
                            vt[:, ti * D + hf * 512: ti * D + hf * 512 + 512], b[:, :]), after=[mv])
                        psV2_free[i % 2] = ev
                        last_write.append(ev)
                xnf_free = wvb_free

            if dbg:
                lw = last_write[-20:]
                p.op("G", lambda e: e.dma_start(out=dbg_d["d_qf"][:, :], in_=qf[:, :]), after=lw, dma=True)
                p.op("G", lambda e: e.dma_start(out=dbg_d["d_kf"][:, :], in_=kf[:, :]), after=lw, dma=True)
                p.op("G", lambda e: e.dma_start(
                    out=r3d(dbg_d["d_vt"][:, :]), in_=r3s(vt[:, :], NT)), after=lw, dma=True)
            run_block(p)

        with ExitStack() as ph:
            psb = lambda n, s, d=F32: ph.enter_context(nc.sbuf_tensor(n, s, d))
            am = [psb(f"am{i}", [128, 128], BF16) for i in range(2)]
            kcs = [psb(f"kcs{i}", [128, 64], BF16) for i in range(2)]
            dcol = [psb(f"dcol{j}", [128, 2]) for j in range(2)]
            otc = psb("otc", [128, D], BF16)
            otf = psb("otf", [128, KT * 128], BF16)
            woutb = psb("woutb", [128, KT * D], BF16)
            pA = [ph.enter_context(nc.psum_tensor(f"pA{i}", [128, 512], F32)) for i in range(2)]
            pO = [ph.enter_context(nc.psum_tensor(f"pO{i}", [128, 512], F32)) for i in range(2)]
            pS2 = [ph.enter_context(nc.psum_tensor("pS2", [128, 512], F32))] * 2
            pB16 = ph.enter_context(nc.psum_tensor("pB16", [128, 1024], BF16))
            pP = [ph.enter_context(nc.psum_tensor(f"pP{i}", [128, 512], F32)) for i in range(2)]

            p = Prog(counts)
            t_wo = p.op("G", lambda e: e.dma_start(
                out=woutb[:, :], in_=wout_d[:, :]), dma=True)

            am_free = [None, None]
            dcol_free = [None, None]
            kcs_free = [None, None]
            pA_free = [None, None]
            pO_free = [None, None]
            pS2_free = [None, None]
            pB16_free = None
            pP_free = [None, None]
            st_rd = {h: None for h in range(HEADS)}
            st_wr = {h: None for h in range(HEADS)}
            otc_wr = [None] * HEADS
            otc_free = None
            otf_free = [None] * KT
            x1_toks = []

            def qsl(buf, h, c):
                return buf[64 * (h % 2):64 * (h % 2) + 64,
                           (h // 2) * n_own + c * 128:(h // 2) * n_own + (c + 1) * 128]

            for c in range(NCH):
                for h in range(HEADS):
                    par = h % 2
                    base, scol = stpos(h)
                    a0 = p.op("T", lambda e, h=h, c=c, par=par: e.matmul(
                        pA[par][:, 0:128], qsl(kf, h, c), qsl(qf, h, c), start=True, stop=True),
                        after=[pA_free[par]])
                    a1 = p.op("V", lambda e, par=par: e.tensor_tensor(
                        am[par][:, :], pA[par][:, 0:128], cmask_b[:, :], ALU.mult), after=[a0, am_free[par]])
                    pA_free[par] = a1
                    k0 = p.op("T", lambda e, h=h, c=c, base=base: e.transpose(
                        pB16[:, 0:64], qsl(kf, h, c), ident_b[base:base + 64, base:base + 64]),
                        after=[pB16_free])
                    k1 = p.op("S", lambda e, par=par: e.copy(kcs[par][:, :], pB16[:, 0:64]),
                              after=[k0, kcs_free[par]])
                    pB16_free = k1

                    def mmo(e, h=h, c=c, par=par, base=base, scol=scol):
                        ob = pO[par]
                        e.matmul(ob[:, 0:64], am[par][:, :], vt[:, c * D + h * HD: c * D + (h + 1) * HD],
                                 start=True, stop=False)
                        e.matmul(ob[:, 0:64], qsl(qf, h, c), st_b[base:base + 64, scol:scol + 64],
                                 start=False, stop=False)
                        e.matmul(ob[:, 64:65], am[par][:, :], ones_b[:, :], start=False, stop=False)
                        return e.matmul(ob[:, 64:65], qsl(qf, h, c),
                                        st_b[base:base + 64, scol + 64:scol + 65], start=False, stop=True)
                    o0 = p.op("T", mmo, after=[a1, pO_free[par], st_wr[h]])
                    st_rd[h] = o0
                    am_free[par] = o0
                    d0 = p.op("V", lambda e, par=par: e.tensor_scalar(
                        dcol[par][:, 0:1], pO[par][:, 64:65], 1e-6, None, ALU.max), after=[o0])
                    d1 = p.op("V", lambda e, par=par: e.reciprocal(dcol[par][:, 1:2], dcol[par][:, 0:1]),
                              after=[d0, dcol_free[par]])
                    d2 = p.op("S", lambda e, h=h, par=par: e.activation(
                        otc[:, h * HD:(h + 1) * HD], pO[par][:, 0:64], AF.Copy, scale=dcol[par][:, 1:2]),
                        after=[d1, otc_free])
                    dcol_free[par] = d2
                    pO_free[par] = d2
                    otc_wr[h] = d2

                    def mms(e, h=h, c=c, base=base, par=par):
                        e.matmul(pS2[par][base:base + 64, 0:64], kcs[par][:, :],
                                 vt[:, c * D + h * HD: c * D + (h + 1) * HD],
                                 start=True, stop=False, tile_position=(0, base))
                        return e.matmul(pS2[par][base:base + 64, 64:65], kcs[par][:, :], ones_b[:, :],
                                        start=False, stop=True, tile_position=(0, base))
                    s0 = p.op("T", mms, after=[k1, pS2_free[0], pS2_free[1]])
                    kcs_free[par] = s0
                    s1 = p.op("V", lambda e, base=base, scol=scol, par=par: e.tensor_tensor(
                        st_f[base:base + 64, scol:scol + 65], st_f[base:base + 64, scol:scol + 65],
                        pS2[par][base:base + 64, 0:65], ALU.add), after=[s0, st_rd[h]])
                    s2 = p.op("S", lambda e, base=base, scol=scol: e.copy(
                        st_b[base:base + 64, scol:scol + 65], st_f[base:base + 64, scol:scol + 65]), after=[s1])
                    pS2_free[par] = s2
                    st_wr[h] = s2

                trs = []
                for k in range(KT):
                    tm = p.op("T", lambda e, k=k: e.transpose(
                        pB16[:, 0:128], otc[:, k * 128:(k + 1) * 128], ident_b[:, :]),
                        after=[pB16_free, otc_wr[2 * k], otc_wr[2 * k + 1], otf_free[k]])
                    pB16_free = p.op("V", lambda e, k=k: e.tensor_copy(
                        otf[:, k * 128:(k + 1) * 128], pB16[:, 0:128]), after=[tm])
                    trs.append(pB16_free)
                otc_free = trs[-1]

                def mmp1(e):
                    last = None
                    for k in range(KT):
                        last = e.matmul(pP[0][:, :], otf[:, k * 128:(k + 1) * 128],
                                        woutb[:, k * D: k * D + 512],
                                        start=(k == 0), stop=(k == KT - 1))
                    return last
                pm0 = p.op("T", mmp1, after=trs + [t_wo, pP_free[0]])

                def mmp2(e):
                    last = None
                    for k in range(KT):
                        last = e.matmul(pP[1][:, :], otf[:, k * 128:(k + 1) * 128],
                                        woutb[:, k * D + 512: (k + 1) * D],
                                        start=(k == 0), stop=(k == KT - 1))
                    return last
                pm1 = p.op("T", mmp2, after=[pm0, pP_free[1]])
                for k in range(KT):
                    otf_free[k] = pm1
                e0 = p.op("V", lambda e, c=c: e.tensor_tensor(
                    x_res[:, c * D: c * D + 512], x_res[:, c * D: c * D + 512], pP[0][:, :], ALU.add), after=[pm0])
                e1 = p.op("V", lambda e, c=c: e.tensor_tensor(
                    x_res[:, c * D + 512: (c + 1) * D], x_res[:, c * D + 512: (c + 1) * D], pP[1][:, :], ALU.add), after=[pm1])
                pP_free[0] = e0
                pP_free[1] = e1
                x1_toks = [e0, e1]

            if dbg:
                p.op("Y", lambda e: e.dma_start(
                    out=r3d(dbg_d["d_x1"][:, :]), in_=r3s(x_res[:, :], NT)), after=x1_toks, dma=True)
            run_block(p)
        mid.close()

        with ExitStack() as ph:
            psb = lambda n, s, d=F32: ph.enter_context(nc.sbuf_tensor(n, s, d))
            xn_t = [psb(f"xn_t2{j}", [128, D], BF16) for j in range(2)]
            xn2t = [psb(f"xn2t{j}", [128, KT * 128], BF16) for j in range(2)]
            lnc = [psb(f"lnc3{j}", [128, D]) for j in range(2)]
            lcol = [psb(f"lcol3{j}", [128, 4]) for j in range(2)]
            rwtb = psb("rwtb", [128, KT * 8], BF16)
            lgall = psb("lgall", [128, NT * 8])
            l2all = psb("l2all", [128, NT * 8])
            m1t = psb("m1t", [128, NT])
            m2t = psb("m2t", [128, NT])
            g1t = psb("g1t", [128, NT])
            g2t = psb("g2t", [128, NT])
            cm1s = psb("cm1s", [128, 128])
            pf16s = psb("pf16s", [128, 128])
            e8s = psb("e8s", [128, 128])
            identf = psb("identf", [128, 128])
            onesr = psb("onesr", [1, 128])
            tk1s = psb("tk1s", [128, C])
            econs = psb("econs", [128, C])
            kbls = psb("kbls", [1, 8 * NBA])
            jps = psb("jps", [128, 1])
            cixs = psb("cixs", [128, C])
            dmpA = psb("dmpA", [128, C])
            dmpC = psb("dmpC", [128, C])
            msk = psb("msk", [128, C])
            pgt = psb("pgt", [128, C])
            destf = psb("destf", [128, C])
            dstA = psb("dstA", [128, C])
            cmt = psb("cmt", [128, C])
            dst2 = psb("dst2", [128, C])
            covr = psb("covr", [128, C])
            tmp1 = psb("tmp1", [128, C])
            tmp2 = psb("tmp2", [128, C])
            csS = psb("csS", [128, 1])
            cbS = psb("cbS", [128, 1])
            cbT = psb("cbT", [1, 128])
            cbT05 = psb("cbT05", [1, 128])
            cbrep = psb("cbrep", [128, C])
            ef1 = psb("ef1", [1, 8 * NBA])
            efr = psb("efr", [1, NBA])
            stgA = psb("stgA", [128, 4 * C])
            stgC = psb("stgC", [128, 4 * C])
            ydstf = psb("ydstf", [128, C])
            diA = psb("diA", [128, C], I32)
            diC = psb("diC", [128, C], I32)
            pR = ph.enter_context(nc.psum_tensor("pR", [128, 512], F32))
            pT4 = [ph.enter_context(nc.psum_tensor(f"pT4{i}", [128, 1024], BF16)) for i in range(2)]
            pDs = [ph.enter_context(nc.psum_tensor(f"pDs{i}", [128, 512], F32)) for i in range(2)]

            p = Prog(counts)
            xm_tok = p.op("Y", lambda e: e.dma_start(
                out=r3d(xmid_d[:, :]), in_=r3s(x_res[:, :], NT)), dma=True)
            t_rw = p.op("G", lambda e: e.dma_start(
                out=rwtb[:, :], in_=rwt_d[:, :]), dma=True)
            t_cm1 = p.op("Y", lambda e: e.dma_start(out=cm1s[:, :], in_=cm1_d[:, :]), dma=True)
            t_pf = p.op("Y", lambda e: e.dma_start(out=pf16s[:, :], in_=pf16_d[:, :]), dma=True)
            t_e8 = p.op("Y", lambda e: e.dma_start(out=e8s[:, :], in_=e8_d[:, :]), dma=True)
            t_idf = p.op("Y", lambda e: e.dma_start(out=identf[:, :], in_=ident_d[:, :]), dma=True)
            t_on = p.op("Y", lambda e: e.dma_start(out=onesr[:, :], in_=cmask_d[0:1, :]), dma=True)
            t_tk1 = p.op("Y", lambda e: e.dma_start(out=tk1s[:, :], in_=tk1_d[:, :]), dma=True)
            t_ec = p.op("Y", lambda e: e.dma_start(out=econs[:, :], in_=econ_d[:, :]), dma=True)
            t_kb = p.op("Y", lambda e: e.dma_start(out=kbls[:, :], in_=kbl_d[:, :]), dma=True)
            t_jp = p.op("Y", lambda e: e.dma_start(out=jps[:, :], in_=jp_d[:, :]), dma=True)
            t_cx = p.op("Y", lambda e: e.dma_start(out=cixs[:, :], in_=cix_d[:, :]), dma=True)
            dm0 = p.op("V", lambda e: e.tensor_scalar(dmpA[:, :], cixs[:, :], float(NSLOT_A), None, ALU.add), after=[t_cx])
            dm1 = p.op("V", lambda e: e.tensor_scalar(dmpC[:, :], cixs[:, :], float(E * 512), None, ALU.add), after=[t_cx])
            pT4_free = [None, None]
            pR_free = None
            r0_prev = [None, None]
            lg_toks = []
            xnt_free2 = [[], []]
            dbg_free = []
            w8_last = None
            ln_prev = [(None, None, None), (None, None, None)]
            lnc_lastL = [None, None]
            for ti in range(NT):
                pj = ti % 2
                (a7, lnc_lastL[pj]), ln_prev[pj] = emit_ln(
                    p,
                    lambda ti=ti: x_res[:, ti * D:(ti + 1) * D],
                    lambda pj=pj: xn_t[pj][:, :],
                    lnc[pj], lcol[pj], dst_after=xnt_free2[pj],
                    prev=ln_prev[pj], lnc_prev=lnc_lastL[pj])
                if dbg:
                    dbg_free.append(p.op("G", lambda e, ti=ti, pj=pj: e.dma_start(
                        out=dbg_d["d_xn2"][ti * 128:(ti + 1) * 128, :],
                        in_=xn_t[pj][:, :]), after=[a7], dma=True))
                trs = []
                for k in range(KT):
                    tp_ = k % 2
                    tm = p.op("T", lambda e, k=k, tp_=tp_, pj=pj: e.transpose(
                        pT4[tp_][:, 0:128], xn_t[pj][:, k * 128:(k + 1) * 128], ident_b[:, :]),
                        after=[a7, pT4_free[tp_]])
                    pT4_free[tp_] = p.op("S" if tp_ else "V",
                        lambda e, k=k, pj=pj, tp_=tp_: (e.copy if tp_ else e.tensor_copy)(
                        xn2t[pj][:, k * 128:(k + 1) * 128], pT4[tp_][:, 0:128]),
                        after=[tm, r0_prev[pj]])
                    trs.append(pT4_free[tp_])

                def mmr(e, pj=pj):
                    last = None
                    for k in range(KT):
                        last = e.matmul(pR[:, 0:8], xn2t[pj][:, k * 128:(k + 1) * 128],
                                        rwtb[:, k * 8:(k + 1) * 8], start=(k == 0), stop=(k == KT - 1))
                    return last
                r0 = p.op("T", mmr, after=trs + [t_rw, pR_free])
                pR_free = p.op("V", lambda e, ti=ti: e.tensor_copy(
                    lgall[:, ti * 8:(ti + 1) * 8], pR[:, 0:8]), after=[r0])
                lg_toks.append(pR_free)
                r0_prev[pj] = r0
                xnt_free2[pj] = [trs[-1], trs[-2]] + dbg_free
                dbg_free = []

            bbcb = bass.AP(bbc_s, 0, [[8, 128], [0, NT], [1, 8]])
            b0 = p.op("V", lambda e: e.tensor_tensor(lgall[:, :], lgall[:, :], bbcb, ALU.add), after=lg_toks)
            b1 = p.op("V", lambda e: e.tensor_reduce(
                m1t[:, :], r3s(lgall[:, :], NT), axis=AX.X, op=ALU.max), after=[b0])
            m1b = bass.AP(m1t, 0, [[NT, 128], [1, NT], [0, 8]])
            b2 = p.op("V", lambda e: e.tensor_tensor(l2all[:, :], lgall[:, :], m1b, ALU.is_equal), after=[b1])
            b3 = p.op("V", lambda e: e.scalar_tensor_tensor(l2all[:, :], l2all[:, :], -1e30, lgall[:, :], ALU.mult, ALU.add), after=[b2])
            b4 = p.op("V", lambda e: e.tensor_reduce(
                m2t[:, :], r3s(l2all[:, :], NT), axis=AX.X, op=ALU.max), after=[b3])
            b5 = p.op("V", lambda e: e.tensor_tensor(g1t[:, :], m1t[:, :], m2t[:, :], ALU.subtract), after=[b4])
            b6 = p.op("S", lambda e: e.activation(g2t[:, :], g1t[:, :], AF.Sigmoid, scale=-1.0), after=[b5])
            b7 = p.op("S", lambda e: e.activation(g1t[:, :], g1t[:, :], AF.Sigmoid), after=[b6])
            m2b = bass.AP(m2t, 0, [[NT, 128], [1, NT], [0, 8]])
            b8 = p.op("V", lambda e: e.tensor_tensor(lgall[:, :], lgall[:, :], m1b, ALU.is_equal), after=[b3, b1])
            b9 = p.op("V", lambda e: e.tensor_tensor(l2all[:, :], l2all[:, :], m2b, ALU.is_equal), after=[b4])
            g1b = bass.AP(g1t, 0, [[NT, 128], [1, NT], [0, 8]])
            g2b = bass.AP(g2t, 0, [[NT, 128], [1, NT], [0, 8]])
            ba = p.op("V", lambda e: e.tensor_tensor(lgall[:, :], lgall[:, :], g1b, ALU.mult), after=[b8, b7])
            bb = p.op("V", lambda e: e.tensor_tensor(l2all[:, :], l2all[:, :], g2b, ALU.mult), after=[b9, b6])
            w8_last = p.op("V", lambda e: e.tensor_tensor(w8_s[:, :], lgall[:, :], l2all[:, :], ALU.add), after=[ba, bb])
            if dbg:
                p.op("Y", lambda e: e.dma_start(out=dbg_d["d_w8"][:, :], in_=w8_s[:, :]), after=[w8_last], dma=True)

            zcs = p.op("V", lambda e: e.memset(csS[:, :], 0.0))
            c0 = p.op("V", lambda e: e.tensor_scalar(msk[:, :], w8_s[:, :], 0.0, None, ALU.is_gt), after=[w8_last])
            mm1 = p.op("T", lambda e: e.matmul(pDs[0][:, 0:C], cm1s[:, :], msk[:, :]), after=[c0, t_cm1])
            mm2 = p.op("T", lambda e: e.matmul(pDs[1][0:C, 0:1], msk[:, :], ccol[:, 1:2]), after=[c0, mm1])
            ev2 = p.op("V", lambda e: e.tensor_copy(csS[0:C, :], pDs[1][0:C, 0:1]), after=[mm2, zcs])
            mm3a = p.op("T", lambda e: e.matmul(pDs[1][:, 0:1], pf16s[:, :], csS[:, :], start=True, stop=False), after=[ev2, t_pf])
            mm3b = p.op("T", lambda e: e.matmul(pDs[1][:, 0:1], e8s[:, :], csS[:, :], start=False, stop=True), after=[t_e8])
            ev3 = p.op("V", lambda e: e.tensor_copy(cbS[:, :], pDs[1][:, 0:1]), after=[mm3b])
            mm4 = p.op("T", lambda e: e.matmul(pR[0:1, 0:128], cbS[:, 0:1], identf[:, :]), after=[ev3, t_idf, pR_free])
            ev4 = p.op("V", lambda e: e.tensor_copy(cbT[:, :], pR[0:1, 0:128]), after=[mm4])
            mm5 = p.op("T", lambda e: e.matmul(pDs[1][:, 0:C], onesr[:, :], cbT[0:1, 0:C]), after=[ev4, t_on, ev3])
            ev5 = p.op("V", lambda e: e.tensor_copy(cbrep[:, :], pDs[1][:, 0:C]), after=[mm5])
            d0 = p.op("V", lambda e: e.tensor_tensor(destf[:, :], pDs[0][:, 0:C], cbrep[:, :], ALU.add), after=[mm1, ev5])
            g0a = p.op("V", lambda e: e.tensor_scalar(cbT05[:, :], cbT[:, :], -0.5, None, ALU.add), after=[ev4])
            cbTt = bass.AP(cbT05, 0, [[128, 1], [0, NBA], [1, 8]])
            g0 = p.op("V", lambda e: e.tensor_tensor(ef1[:, :], kbls[:, :], cbTt, ALU.is_gt), after=[g0a, t_kb])
            g1 = p.op("V", lambda e: e.reduce_sum(efr[:, :], r3s(ef1[:, :], NBA), axis=AX.X), after=[g0])
            g2 = p.op("V", lambda e: e.tensor_scalar(efr[:, :], efr[:, :], -1.0, None, ALU.add), after=[g1])
            mm6 = p.op("T", lambda e: e.matmul(pDs[0][:, 0:NBA], onesr[:, :], efr[0:1, 0:NBA]), after=[g2, d0])
            ev6 = p.op("V", lambda e: e.tensor_copy(efp[:, :], pDs[0][:, 0:NBA]), after=[mm6])
            jpb = bass.AP(jps, 0, [[1, 128], [0, NBA]])
            w0 = p.op("V", lambda e: e.scalar_tensor_tensor(tmp1[:, 0:NBA], efp[:, :], float(NFB * 128), jpb, ALU.mult, ALU.add), after=[ev6, t_jp])
            w1_tok = p.op("V", lambda e: e.tensor_copy(wi_i32[:, :], tmp1[:, 0:NBA]), after=[w0])
            z0 = p.op("V", lambda e: e.tensor_scalar(covr[:, :], msk[:, :], 0.0, None, ALU.mult), after=[c0])
            kprev = z0
            for k in range(NBA):
                k1 = p.op("V", lambda e, k=k: e.tensor_scalar(tmp1[:, :], destf[:, :], 512.0 * k - 0.5, None, ALU.is_gt), after=[d0, kprev, w1_tok])
                k2 = p.op("V", lambda e, k=k: e.tensor_scalar(tmp2[:, :], destf[:, :], 512.0 * (k + 1) - 0.5, None, ALU.is_lt), after=[d0, kprev])
                k3 = p.op("V", lambda e: e.tensor_tensor(tmp1[:, :], tmp1[:, :], tmp2[:, :], ALU.mult), after=[k1, k2])
                efb = bass.AP(efp, k, [[NBA, 128], [0, C]])
                k4 = p.op("V", lambda e, efb=efb: e.tensor_tensor(tmp2[:, :], econs[:, :], efb, ALU.is_equal), after=[ev6, t_ec, k3])
                k5 = p.op("V", lambda e: e.tensor_tensor(tmp1[:, :], tmp1[:, :], tmp2[:, :], ALU.mult), after=[k3, k4])
                kprev = p.op("V", lambda e: e.tensor_tensor(covr[:, :], covr[:, :], tmp1[:, :], ALU.add), after=[k5])
            stA1 = bass.AP(stgA, 1, [[4 * C, 128], [4, C]])
            sA1 = p.op("V", lambda e: e.tensor_tensor(stA1, w8_s[:, :], covr[:, :], ALU.mult), after=[kprev])
            c1 = p.op("V", lambda e: e.tensor_scalar(tmp2[:, :], covr[:, :], -1.0, 1.0, ALU.mult, ALU.add), after=[kprev])
            c2 = p.op("V", lambda e: e.tensor_tensor(cmt[:, :], msk[:, :], tmp2[:, :], ALU.mult), after=[c1, c0])
            pgv = lambda e_, t=None: bass.AP(pgt, e_, [[C, 128], [8, NT]])
            mkv = lambda e_: bass.AP(msk, e_, [[C, 128], [8, NT]])
            pg_prev = p.op("V", lambda e: e.tensor_scalar(pgv(0), mkv(0), 0.0, None, ALU.mult), after=[c0])
            for e_ in range(1, 8):
                pg_prev = p.op("V", lambda e, e_=e_: e.tensor_tensor(pgv(e_), pgv(e_ - 1), mkv(e_ - 1), ALU.add), after=[pg_prev])
            y0 = p.op("V", lambda e: e.scalar_tensor_tensor(ydstf[:, :], pgt[:, :], float(n_own), tk1s[:, :], ALU.mult, ALU.add), after=[pg_prev, t_tk1])
            dA1 = p.op("V", lambda e: e.tensor_scalar(tmp1[:, :], msk[:, :], -1e6, 1e6, ALU.mult, ALU.add), after=[sA1, kprev])
            dA2 = p.op("V", lambda e: e.tensor_tensor(dstA[:, :], destf[:, :], tmp1[:, :], ALU.add), after=[dA1, d0])
            dA3 = p.op("V", lambda e: e.tensor_tensor(dstA[:, :], dstA[:, :], dmpA[:, :], ALU.min), after=[dA2, dm0])
            cvA = p.op("V", lambda e: e.tensor_copy(diA[:, :], dstA[:, :]), after=[dA3])
            mm7 = p.op("T", lambda e: e.matmul(pDs[1][:, 0:C], cm1s[:, :], cmt[:, :]), after=[c2, ev5])
            mm8 = p.op("T", lambda e: e.matmul(pR[0:C, 0:1], cmt[:, :], ccol[:, 1:2]), after=[c2, ev4, mm7])
            ev7 = p.op("V", lambda e: e.tensor_copy(csS[0:C, :], pR[0:C, 0:1]), after=[mm8, mm3b])
            mm9 = p.op("T", lambda e: e.matmul(pDs[0][:, 0:1], pf16s[:, :], csS[:, :]), after=[ev7, ev6])
            ev8 = p.op("V", lambda e: e.tensor_copy(cbS[:, :], pDs[0][:, 0:1]), after=[mm9, mm4])
            mm10 = p.op("T", lambda e: e.matmul(pR[0:1, 0:128], cbS[:, 0:1], identf[:, :]), after=[ev8, ev7])
            ev9 = p.op("V", lambda e: e.tensor_copy(cbT[:, :], pR[0:1, 0:128]), after=[mm10, mm5, g0])
            s1 = p.op("V", lambda e: e.scalar_tensor_tensor(dst2[:, :], econs[:, :], 512.0, pDs[1][:, 0:C], ALU.mult, ALU.add), after=[mm7, t_ec])
            mm11 = p.op("T", lambda e: e.matmul(pDs[1][:, 0:C], onesr[:, :], cbT[0:1, 0:C]), after=[ev9, s1])
            s2 = p.op("V", lambda e: e.tensor_tensor(dst2[:, :], dst2[:, :], pDs[1][:, 0:C], ALU.add), after=[mm11, s1])
            s3 = p.op("V", lambda e: e.tensor_scalar(tmp1[:, :], cmt[:, :], -1e6, 1e6, ALU.mult, ALU.add), after=[dA2, c2])
            s4 = p.op("V", lambda e: e.tensor_tensor(dst2[:, :], dst2[:, :], tmp1[:, :], ALU.add), after=[s2, s3])
            s5 = p.op("V", lambda e: e.tensor_tensor(dst2[:, :], dst2[:, :], dmpC[:, :], ALU.min), after=[s4, dm1])
            cvC = p.op("V", lambda e: e.tensor_copy(diC[:, :], dst2[:, :]), after=[s5])
            stv = lambda t, j: bass.AP(t, j, [[4 * C, 128], [4, C]])
            sA0 = p.op("V", lambda e: e.tensor_scalar(stv(stgA, 0), tk1s[:, :], float(-YD0), None, ALU.add), after=[t_tk1])
            sA2 = p.op("V", lambda e: e.tensor_copy(stv(stgA, 2), ydstf[:, :]), after=[y0])
            sA3 = p.op("V", lambda e: e.tensor_copy(stv(stgA, 3), ydstf[:, :]), after=[y0])
            sC0 = p.op("V", lambda e: e.tensor_scalar(stv(stgC, 0), tk1s[:, :], float(-YD0), None, ALU.add), after=[t_tk1])
            sC1 = p.op("V", lambda e: e.tensor_copy(stv(stgC, 1), w8_s[:, :]), after=[w8_last])
            sC2 = p.op("V", lambda e: e.tensor_copy(stv(stgC, 2), ydstf[:, :]), after=[y0])
            sC3 = p.op("V", lambda e: e.tensor_copy(stv(stgC, 3), ydstf[:, :]), after=[y0])
            scA_toks = []
            scC_toks = []
            from concourse.bass import IndirectOffsetOnAxis as IOA
            rbA = nc.gpsimd.alloc_register("rbA")
            rbC = nc.gpsimd.alloc_register("rbC")
            rg0 = p.op("G", lambda e: e.reg_mov(rbA, NSLOT_A - 1))
            rg1 = p.op("G", lambda e: e.reg_mov(rbC, E * 512 - 1))
            for cc in range(C):
                scA_toks.append(p.op("G", lambda e, cc=cc: e.indirect_dma_start(
                    sl_d[:, :], IOA(diA[:, cc:cc + 1], 0),
                    stgA[:, 4 * cc:4 * cc + 4], None,
                    bounds_check=rbA, oob_is_err=False),
                    after=[cvA, sA0, sA1, sA2, sA3, zsl_tok, rg0], dma=True))
            for cc in range(C):
                scC_toks.append(p.op("G", lambda e, cc=cc: e.indirect_dma_start(
                    cl_d[:, :], IOA(diC[:, cc:cc + 1], 0),
                    stgC[:, 4 * cc:4 * cc + 4], None,
                    bounds_check=rbC, oob_is_err=False),
                    after=[cvC, sC0, sC1, sC2, sC3, zcl_tok, rg1], dma=True))
            run_block(p)

        gwf = gw_d.rearrange("a p b -> (a p) b")
        uwf = uw_d.rearrange("a p b -> (a p) b")
        dwf = dw_d.rearrange("a p b -> (a p) b")
        from concourse.bass import IndirectOffsetOnAxis as IOA
        sc_done = scA_toks[-NLANES:] + scC_toks[-NLANES:]
        with ExitStack() as ph:
            psb = lambda n, s, d=F32: ph.enter_context(nc.sbuf_tensor(n, s, d))
            gb = [psb(f"gb{i}", [128, KT * 512], BF16) for i in range(2)]
            ub = [psb(f"ub{i}", [128, KT * 512], BF16) for i in range(2)]
            db_ = [psb(f"db{i}", [128, 4 * D], BF16) for i in range(2)]
            hs2 = [psb(f"hs2{i}", [128, 4 * 512], BF16) for i in range(2)]
            sil = [psb(f"sil{i}", [128, 512]) for i in range(2)]
            xg = [psb(f"xg{i}", [128, D], BF16) for i in range(2)]
            xnb_t = [psb(f"xnbt{i}", [128, D], BF16) for i in range(2)]
            xnbf = [psb(f"xnbf{i}", [128, KT * 512], BF16) for i in range(2)]
            ybuf = [psb(f"ybuf{i}", [128, 4 * D]) for i in range(2)]
            tbs = [psb(f"tbs{i}", [128, 4]) for i in range(2)]
            toki2 = psb("toki2", [128, 8], I32)
            ydi2 = psb("ydi2", [128, 8], I32)
            gcolb = psb("gcolb", [128, 8])
            lncF = psb("lncF", [128, D])
            lcolF = [psb(f"lcolF{j}", [128, 4]) for j in range(2)]
            yld = [psb(f"yld{i}", [128, D]) for i in range(2)]
            pG = [ph.enter_context(nc.psum_tensor(f"pG{i}", [128, 512], F32)) for i in range(2)]
            pU = [ph.enter_context(nc.psum_tensor(f"pU{i}", [128, 512], F32)) for i in range(2)]
            pD = [ph.enter_context(nc.psum_tensor(f"pD{i}", [128, 512], F32)) for i in range(2)]
            pT6 = [ph.enter_context(nc.psum_tensor(f"pT6{i}", [128, 1024], BF16)) for i in range(2)]

            p = Prog(counts)
            wg_free = [None, None]
            wu_free = [None, None]
            wd_free = [None, None]
            pG_free = [None, None]
            pU_free = [None, None]
            pD_free = [None, None]
            pT6_free = [None, None]
            sil_free = [None, None]
            hs_free = [None, None]
            tbs_free = [[], []]
            tokc_free = [None] * 8
            ydic_free = [None] * 8
            gcol_free = [[] for _ in range(8)]
            xg_free = [None, None]
            xnbt_free = [[], []]
            xnbf_free = [[], []]
            ysc_free = [[], []]
            lnprev6 = [(None, None, None), (None, None, None)]
            lnc_last6 = None
            ysc_all = []

            for b in range(NBLK):
                isA = b < NBA
                eS = b - NBA
                tab = sl_d if isA else cl_d
                row0 = (b if isA else eS) * 512
                bp = b % 2
                fr_evs = []
                gc_toks = [None] * 4
                yi_toks = [None] * 4
                for sg in range(4):
                    j = bp * 4 + sg
                    sp = sg % 2
                    rb = p.op("Y", lambda e, tab=tab, row0=row0, sg=sg, sp=sp: e.dma_start(
                        out=tbs[sp][:, :], in_=tab[row0 + sg * 128:row0 + (sg + 1) * 128, :]),
                        after=tbs_free[sp] + sc_done, dma=True)
                    ci = p.op("V", lambda e, j=j, sp=sp: e.tensor_copy(
                        toki2[:, j:j + 1], tbs[sp][:, 0:1]), after=[rb, tokc_free[j]])
                    dv = float(b * 4 + sg)
                    yi0 = p.op("V", lambda e, sp=sp, dv=dv: e.tensor_scalar(
                        tbs[sp][:, 3:4], tbs[sp][:, 2:3], dv, None, ALU.max), after=[rb])
                    yi = p.op("V", lambda e, j=j, sp=sp: e.tensor_copy(
                        ydi2[:, j:j + 1], tbs[sp][:, 3:4]), after=[yi0, ydic_free[j]])
                    gc = p.op("V", lambda e, j=j, sp=sp: e.tensor_copy(
                        gcolb[:, j:j + 1], tbs[sp][:, 1:2]), after=[rb] + gcol_free[j])
                    gcol_free[j] = []
                    gc_toks[sg] = gc
                    yi_toks[sg] = yi
                    tbs_free[sp] = [ci, yi, gc, yi0]
                    xga = p.op("G", lambda e, j=j, sp=sp: e.indirect_dma_start(
                        xg[sp][:, :], None,
                        xmid_d[:, :], IOA(toki2[:, j:j + 1], 0)),
                        after=[ci, xg_free[sp], xm_tok], dma=True)
                    tokc_free[j] = xga
                    (lntok, lnc_last6), lnprev6[sp] = emit_ln(
                        p,
                        lambda sp=sp: xg[sp][:, :],
                        lambda sp=sp: xnb_t[sp][:, :],
                        lncF, lcolF[sp], src_after=[xga], dst_after=xnbt_free[sp],
                        prev=lnprev6[sp], lnc_prev=lnc_last6)
                    xg_free[sp] = lntok
                    trs = []
                    for k in range(KT):
                        tp_ = k % 2
                        tm = p.op("T", lambda e, k=k, sp=sp, tp_=tp_: e.transpose(
                            pT6[tp_][:, 0:128], xnb_t[sp][:, k * 128:(k + 1) * 128], ident_b[:, :]),
                            after=[lntok, pT6_free[tp_]])
                        pT6_free[tp_] = p.op("S" if tp_ else "V",
                            lambda e, k=k, sg=sg, bp=bp, tp_=tp_: (e.copy if tp_ else e.tensor_copy)(
                            xnbf[bp][:, k * 512 + sg * 128:k * 512 + sg * 128 + 128], pT6[tp_][:, 0:128]),
                            after=[tm] + (xnbf_free[bp] if (sg == 0 and k < 2) else []))
                        trs.append(pT6_free[tp_])
                    if sg == 0:
                        xnbf_free[bp] = []
                    xnbt_free[sp] = [trs[-1], trs[-2]]
                    fr_evs += trs

                for fb in range(NFB):
                    par = (b * NFB + fb) % 2
                    if isA:
                        tg = p.op("G", lambda e, b=b, fb=fb, par=par: e.indirect_dma_start(
                            gb[par][:, :], None, gwf, IOA(wi_i32[:, b:b + 1], 0),
                            element_offset=fb * 128 * (KT * 512)),
                            after=[wg_free[par], w1_tok], dma=True)
                        tu = p.op("G", lambda e, b=b, fb=fb, par=par: e.indirect_dma_start(
                            ub[par][:, :], None, uwf, IOA(wi_i32[:, b:b + 1], 0),
                            element_offset=fb * 128 * (KT * 512)),
                            after=[wu_free[par], w1_tok], dma=True)
                        td = p.op("G", lambda e, b=b, fb=fb, par=par: e.indirect_dma_start(
                            db_[par][:, :], None, dwf, IOA(wi_i32[:, b:b + 1], 0),
                            element_offset=fb * 128 * (4 * D)),
                            after=[wd_free[par], w1_tok], dma=True)
                    else:
                        tg = p.op("G", lambda e, eS=eS, fb=fb, par=par: e.dma_start(
                            out=gb[par][:, :], in_=gw_d[eS * NFB + fb, :, :]),
                            after=[wg_free[par]], dma=True)
                        tu = p.op("G", lambda e, eS=eS, fb=fb, par=par: e.dma_start(
                            out=ub[par][:, :], in_=uw_d[eS * NFB + fb, :, :]),
                            after=[wu_free[par]], dma=True)
                        td = p.op("G", lambda e, eS=eS, fb=fb, par=par: e.dma_start(
                            out=db_[par][:, :], in_=dw_d[eS * NFB + fb, :, :]),
                            after=[wd_free[par]], dma=True)
                    hb = fb % 2
                    p1_done = []
                    mg = mu = None
                    for f0 in range(4):
                        bq = f0 % 2

                        def mmg(e, par=par, f0=f0, bp=bp, bq=bq):
                            last = None
                            for k in range(KT):
                                last = e.matmul(pG[bq][:, :],
                                                gb[par][:, k * 512 + f0 * 128:k * 512 + (f0 + 1) * 128],
                                                xnbf[bp][:, k * 512:(k + 1) * 512],
                                                start=(k == 0), stop=(k == KT - 1))
                            return last
                        mg = p.op("T", mmg, after=[tg, pG_free[bq]] + (fr_evs if (fb == 0 and f0 == 0) else []))

                        def mmu(e, par=par, f0=f0, bp=bp, bq=bq):
                            last = None
                            for k in range(KT):
                                last = e.matmul(pU[bq][:, :],
                                                ub[par][:, k * 512 + f0 * 128:k * 512 + (f0 + 1) * 128],
                                                xnbf[bp][:, k * 512:(k + 1) * 512],
                                                start=(k == 0), stop=(k == KT - 1))
                            return last
                        mu = p.op("T", mmu, after=[tu, pU_free[bq], mg])
                        sv = p.op("S", lambda e, bq=bq: e.activation(
                            sil[bq][:, :], pG[bq][:, :], AF.Silu), after=[mg, sil_free[bq]])
                        pG_free[bq] = sv
                        hw = p.op("V", lambda e, f0=f0, hb=hb, bq=bq: e.tensor_tensor(
                            hs2[hb][:, f0 * 512:(f0 + 1) * 512],
                            sil[bq][:, :], pU[bq][:, :], ALU.mult),
                            after=[sv, mu, hs_free[hb]])
                        pU_free[bq] = hw
                        sil_free[bq] = hw
                        p1_done.append(hw)
                    wg_free[par] = mg
                    wu_free[par] = mu
                    md = None
                    for sg in range(4):
                        j = bp * 4 + sg
                        for hf in range(2):
                            dq = hf

                            def mmd(e, par=par, sg=sg, hf=hf, hb=hb, dq=dq):
                                last = None
                                for f0 in range(4):
                                    last = e.matmul(pD[dq][:, :],
                                                    hs2[hb][:, f0 * 512 + sg * 128:f0 * 512 + (sg + 1) * 128],
                                                    db_[par][:, f0 * D + hf * 512:f0 * D + (hf + 1) * 512],
                                                    start=(f0 == 0), stop=(f0 == 3))
                                return last
                            md = p.op("T", mmd, after=p1_done + [td, pD_free[dq]])
                            if fb == 0:
                                ev = p.op("V", lambda e, sg=sg, hf=hf, j=j, bp=bp, dq=dq: e.tensor_scalar(
                                    ybuf[bp][:, sg * D + hf * 512:sg * D + (hf + 1) * 512],
                                    pD[dq][:, :], gcolb[:, j:j + 1], None, ALU.mult),
                                    after=[md, gc_toks[sg]] + ysc_free[bp])
                            else:
                                ev = p.op("V", lambda e, sg=sg, hf=hf, j=j, bp=bp, dq=dq: e.scalar_tensor_tensor(
                                    ybuf[bp][:, sg * D + hf * 512:sg * D + (hf + 1) * 512],
                                    pD[dq][:, :], gcolb[:, j:j + 1],
                                    ybuf[bp][:, sg * D + hf * 512:sg * D + (hf + 1) * 512],
                                    ALU.mult, ALU.add), after=[md, gc_toks[sg]])
                            pD_free[dq] = ev
                            gcol_free[j].append(ev)
                            if fb == NFB - 1:
                                if hf == 1:
                                    ysc = p.op("G", lambda e, sg=sg, j=j, bp=bp: e.indirect_dma_start(
                                        y_d[:, :], IOA(ydi2[:, j:j + 1], 0),
                                        ybuf[bp][:, sg * D:(sg + 1) * D], None),
                                        after=[ev, pD_free[0], yi_toks[sg]], dma=True)
                                    ysc_all.append(ysc)
                                    ydic_free[j] = ysc
                                    ysc_free[bp] = ysc_free[bp] + [ysc] if sg else [ysc]
                    hs_free[hb] = md
                    wd_free[par] = md
                xnbf_free[bp] = [wg_free[(b * NFB + NFB - 1) % 2], wu_free[(b * NFB + NFB - 1) % 2]]

            st_toks = []
            yld_free = [None] * 2
            for ti in range(NT):
                l0 = p.op("Y", lambda e, ti=ti: e.dma_start(
                    out=yld[0][:, :], in_=y_d[YD0 + ti * 128:YD0 + (ti + 1) * 128, :]),
                    after=[yld_free[0]] + (ysc_all if ti < 2 else []), dma=True)
                l1 = p.op("Y", lambda e, ti=ti: e.dma_start(
                    out=yld[1][:, :],
                    in_=y_d[YD0 + n_own + ti * 128:YD0 + n_own + (ti + 1) * 128, :]),
                    after=[yld_free[1]] + (ysc_all if ti < 2 else []), dma=True)
                v0 = p.op("V", lambda e, ti=ti: e.tensor_tensor(
                    x_res[:, ti * D:(ti + 1) * D], x_res[:, ti * D:(ti + 1) * D],
                    yld[0][:, :], ALU.add), after=[l0])
                v1 = p.op("V", lambda e, ti=ti: e.tensor_tensor(
                    x_res[:, ti * D:(ti + 1) * D], x_res[:, ti * D:(ti + 1) * D],
                    yld[1][:, :], ALU.add), after=[l1, v0])
                yld_free[0] = v0
                yld_free[1] = v1
                st = p.op("Y", lambda e, ti=ti: e.dma_start(
                    out=out_d[ti * 128:(ti + 1) * 128, :], in_=x_res[:, ti * D:(ti + 1) * D]),
                    after=[v1], dma=True)
                st_toks.append(st)
            p.op("Y", lambda e: e.nop(), after=st_toks[-NLANES:])
            run_block(p)

    return nc



def _prep_host(L0_attn_x, L0_attn_w_qkv, L0_attn_w_out, L0_ffn_router_w,
               L0_ffn_router_b, L0_ffn_gate_w, L0_ffn_up_w, L0_ffn_down_w,
               n_own, n_pre, f_ffn):
    x = np.ascontiguousarray(np.asarray(L0_attn_x, dtype=np.float32).reshape(-1, D))
    wqkv = np.asarray(L0_attn_w_qkv, dtype=np.float32)
    rw = np.asarray(L0_ffn_router_w, dtype=np.float32)
    rb = np.asarray(L0_ffn_router_b, dtype=np.float32)
    gw = np.asarray(L0_ffn_gate_w, dtype=np.float32)
    uw = np.asarray(L0_ffn_up_w, dtype=np.float32)
    dw = np.asarray(L0_ffn_down_w, dtype=np.float32)

    ntok = x.shape[0]
    assert ntok == NC * n_own

    KTn = D // 128
    NFB = f_ffn // 512

    def pshuf(mat, wtile):
        Wc = mat.shape[1]
        a = mat.reshape(KTn, 128, Wc // wtile, wtile)
        return np.ascontiguousarray(a.transpose(2, 1, 0, 3).reshape(Wc // wtile, 128, KTn * wtile))

    perm = np.arange(D)
    perm = (perm // HD) * HD + ((perm % HD) + HD // 2) % HD
    wq = wqkv[:, 0:D]
    wk = wqkv[:, D:2 * D]
    wv = wqkv[:, 2 * D:3 * D]
    wqkt = np.ascontiguousarray(np.concatenate(
        [pshuf(wq, 128), pshuf(wq[:, perm], 128),
         pshuf(wk, 128), pshuf(wk[:, perm], 128)], axis=0))
    wkvp = np.ascontiguousarray(np.concatenate([pshuf(wk, 512), pshuf(wv, 512)], axis=0))
    wouts = pshuf(np.ascontiguousarray(np.asarray(L0_attn_w_out, dtype=np.float32)), D)[0]
    rwt = pshuf(np.ascontiguousarray(rw.T), 8)[0]
    bbc = np.ascontiguousarray(np.broadcast_to(rb[None, :], (128, 8)).astype(np.float32))
    gwt = np.concatenate([pshuf(np.ascontiguousarray(gw[e].T), 512) for e in range(8)], axis=0)
    uwt = np.concatenate([pshuf(np.ascontiguousarray(uw[e].T), 512) for e in range(8)], axis=0)
    def pshuf_d(mat):
        a = mat.reshape(NFB, 4, 128, D)
        return np.ascontiguousarray(a.transpose(0, 2, 1, 3).reshape(NFB, 128, 4 * D))
    dwt = np.concatenate([pshuf_d(np.ascontiguousarray(dw[e].T)) for e in range(8)], axis=0)

    NPT = n_pre // 128
    r = np.arange(128)
    jcol = (r % 32).astype(np.float32).reshape(128, 1)
    j64 = np.ascontiguousarray(
        np.broadcast_to(np.arange(32, dtype=np.float32)[None, :], (128, 32)))
    sgn = np.where((r % 64) < 32, -np.float32(SCALE), np.float32(SCALE)).astype(np.float32).reshape(128, 1)
    posp = (np.arange(NPT, dtype=np.float32)[None, :] * 128 + r[:, None]).astype(np.float32)
    ident = np.eye(128, dtype=np.float32)
    cmask = np.triu(np.ones((128, 128), dtype=np.float32))
    ccol = np.zeros((128, 2), np.float32)
    ccol[:, 0] = EPS
    ccol[:, 1] = 1.0

    NT = n_own // 128
    C = NT * 8
    NBA = 2 * n_own // 512
    YD0 = 128
    cm1 = (np.arange(128)[:, None] < np.arange(128)[None, :]).astype(np.float32)
    cidx = np.arange(C)
    tic, eic = cidx // 8, cidx % 8
    pf16 = np.zeros((128, 128), np.float32)
    pf16[:C, :C] = ((eic[:, None] == eic[None, :]) & (tic[:, None] < tic[None, :]))
    e8m = np.zeros((128, 128), np.float32)
    e8m[:C, :C] = (eic[:, None] < eic[None, :])
    tk1 = np.ascontiguousarray(
        (tic[None, :] * 128 + np.arange(128)[:, None] + YD0).astype(np.float32))
    econ = np.ascontiguousarray(
        np.broadcast_to(eic[None, :], (128, C)).astype(np.float32))
    kbl = np.ascontiguousarray(
        (512.0 * np.repeat(np.arange(NBA), 8))[None, :].astype(np.float32))
    jp = np.arange(128, dtype=np.float32).reshape(128, 1)
    cix = np.ascontiguousarray(np.broadcast_to(
        cidx[None, :], (128, C)).astype(np.float32))

    xpre = np.zeros((n_pre, D), dtype=np.float32)
    npre_eff = min(n_pre, (NC - 1) * n_own)
    xpre[:npre_eff] = x[:npre_eff]

    in_maps = []
    for c in range(NC):
        pos0 = c * n_own
        posb = np.ascontiguousarray(np.broadcast_to(
            (pos0 + np.arange(n_own, dtype=np.float32))[None, :], (128, n_own)))
        prem = np.zeros((128, NPT), np.float32)
        npre_c = c * n_own
        full_tiles = npre_c // 128
        prem[:, :full_tiles] = 1.0
        rem = npre_c - full_tiles * 128
        if rem > 0:
            prem[:rem, full_tiles] = 1.0
        in_maps.append(dict(
            x=np.ascontiguousarray(x[c * n_own:(c + 1) * n_own]),
            xpre=xpre, wqkt=wqkt, wkvp=wkvp, wout=wouts, rwt=rwt, bbc=bbc,
            gw=gwt, uw=uwt, dw=dwt,
            posb=posb, posp=posp, jcol=jcol, j64=j64, sgn=sgn,
            prem=prem, ident=ident, cmask=cmask, ccol=ccol,
            cm1=cm1, pf16=pf16, e8=e8m, tk1=tk1, econ=econ, kbl=kbl, jp=jp,
            cix=cix,
        ))
    return in_maps


def kernel(L0_attn_x, L0_attn_w_qkv, L0_attn_w_out, L0_ffn_router_w,
           L0_ffn_router_b, L0_ffn_gate_w, L0_ffn_up_w, L0_ffn_down_w):
    n_own = 2048
    n_pre = 14336
    f_ffn = 4096
    in_maps = _prep_host(L0_attn_x, L0_attn_w_qkv, L0_attn_w_out,
                         L0_ffn_router_w, L0_ffn_router_b, L0_ffn_gate_w,
                         L0_ffn_up_w, L0_ffn_down_w, n_own, n_pre, f_ffn)
    nc = build(n_own, n_pre, f_ffn, dbg=False)
    res = run_bass_kernel_spmd(nc, in_maps, core_ids=list(range(NC)))
    out = np.concatenate([np.asarray(r["out"]) for r in res.results], axis=0)
    return out.reshape(1, NC * n_own, D)
